# version 25
# baseline (speedup 1.0000x reference)
import os
import numpy as np
import ml_dtypes
from contextlib import ExitStack

import concourse.bass as bass
import concourse.mybir as mybir
from concourse.bass_utils import run_bass_kernel_spmd

F32 = mybir.dt.float32
BF16 = mybir.dt.bfloat16
AF = mybir.ActivationFunctionType
ALU = mybir.AluOpType
AX = mybir.AxisListType

SHIFT = 44.0
MAGIC = 12582912.0
EPS = 1e-5
NCORE = 8
T = 2048
NPAIR = 4
NQG = 4
NKC = 32
D = 1024
INNER = 4096
NB_ELEM = 4096 * 1024

BD = ml_dtypes.bfloat16
_PH0 = 0
_PH = int(os.environ.get("BASSK_PH", "15"))
_P1L = int(os.environ.get("BASSK_P1L", "9"))


class Ctr:
    def __init__(self, stack, nc, name):
        self.h = stack.enter_context(nc.semaphore(name))
        self.n = 0

    def inc(self, instr, k=1):
        instr.then_inc(self.h, k)
        self.n += k
        return self.n


def build_nc():
    nc = bass.Bass("TRN2", target_bir_lowering=False)

    xT = nc.dram_tensor("xT", [D, T], F32, kind="ExternalInput").ap()
    xN = nc.dram_tensor("xN", [T, D], F32, kind="ExternalInput").ap()
    wqT = nc.dram_tensor("wqT", [D, D], F32, kind="ExternalInput").ap()
    wkT = nc.dram_tensor("wkT", [D, D], F32, kind="ExternalInput").ap()
    wvT = nc.dram_tensor("wvT", [D, D], F32, kind="ExternalInput").ap()
    wgT = nc.dram_tensor("wgT", [D, INNER], F32, kind="ExternalInput").ap()
    wuT = nc.dram_tensor("wuT", [D, INNER], F32, kind="ExternalInput").ap()
    wdT = nc.dram_tensor("wdT", [INNER, D], F32, kind="ExternalInput").ap()
    idb = nc.dram_tensor("idb", [128, 128], BF16, kind="ExternalInput").ap()
    idf = nc.dram_tensor("idf", [128, 128], F32, kind="ExternalInput").ap()
    onesr = nc.dram_tensor("onesr", [1, 128], F32, kind="ExternalInput").ap()
    onesc = nc.dram_tensor("onesc", [128, 1], F32, kind="ExternalInput").ap()
    out = nc.dram_tensor("out", [T, D], F32, kind="ExternalOutput").ap()

    y_d = nc.dram_tensor("y_d", [T, D], F32, kind="Internal").ap()
    wguq_d = nc.dram_tensor("wguq_d", [16, 128, INNER], BF16, kind="Internal").ap()
    wdq_d = nc.dram_tensor("wdq_d", [32, 128, D], BF16, kind="Internal").ap()
    cc_in = nc.dram_tensor("cc_in", [1, 2], F32, kind="Internal").ap()
    cc_out = nc.dram_tensor("cc_out", [1, 2], F32, kind="Internal").ap()

    st = ExitStack()
    with st:
        C = lambda name: Ctr(st, nc, name)
        dC = C("dC"); dWA = [C("dWA0"), C("dWA1"), C("dWA2")]; dXF = C("dXF"); dXT = C("dXT")
        dBQ = C("dBQ"); dBK = C("dBK"); dSPL = [C("dSPL0"), C("dSPL1")]
        dWP = [C("dWP0"), C("dWP1"), C("dWP2"), C("dWP3")]; dWQS = [C("dWQS0"), C("dWQS1")]; dCC = C("dCC")
        dY = C("dY"); dYb = C("dYb"); dXN = C("dXN"); dXNb = C("dXNb"); dGU = [C("dGU0"), C("dGU1")]; dWD = [C("dWD0"), C("dWD1")]; dOUT = C("dOUT")
        tM1 = C("tM1"); tM2 = C("tM2"); sEXP = C("sEXP")
        tYTR = C("tYTR"); tPRJ = C("tPRJ"); tVTR = C("tVTR")
        gCC = C("gCC"); sMSC = C("sMSC"); tMSC = C("tMSC")
        tGU = C("tGU"); sSIL = C("sSIL"); sOCP = C("sOCP")
        tTRX = C("tTRX"); tDWN = C("tDWN")
        vORD = C("vORD")
        gORD = C("gORD")

        class VMark:
            def __init__(self):
                self.h = vORD.h
                self.n = 0
            def inc(self, ins):
                self.n = vORD.n
                return self.n

        vOCP = VMark(); vYST = VMark(); vPEV = VMark(); vVEV = VMark()
        vSTA = VMark(); vMSC = VMark(); vHST = VMark(); vQNT = VMark()
        vTEV = VMark(); vZ2 = VMark()

        def gx(_m, *a, **k):
            if gORD.n:
                nc.gpsimd.wait_ge(gORD.h, gORD.n)
            ins = getattr(nc.gpsimd, _m)(*a, **k)
            gORD.inc(ins)
            return ins

        def vx(_m, *a, **k):
            if vORD.n:
                nc.vector.wait_ge(vORD.h, vORD.n)
            ins = getattr(nc.vector, _m)(*a, **k)
            vORD.inc(ins)
            return ins

        sb = lambda *a: st.enter_context(nc.sbuf_tensor(*a))
        identb = sb("identb", [128, 128], BF16)
        identf = sb("identf", [128, 128], F32)
        ones_r = sb("ones_r", [1, 128], F32)
        ones_c = sb("ones_c", [128, 1], F32)
        expb = sb("expb", [128, 1], F32)
        SCAL = sb("SCAL", [128, 8], F32)
        sacc = sb("sacc", [128, 1], F32)
        ccs = sb("ccs", [1, 4], F32)
        sqacc = sb("sqacc", [128, 1], F32)

        for (dst_t, src_t) in ((identb, idb), (identf, idf),
                               (ones_r, onesr), (ones_c, onesc)):
            nc.sync.dma_start(dst_t[:, :], src_t).then_inc(dC.h, 16); dC.n += 16
        vx("memset", expb[:, :], -SHIFT)
        vx("memset", sacc[:, :], 0.0)
        ins = vx("memset", sqacc[:, :], 0.0)
        vMSC.inc(ins)

        wprep_vqnt = 0
        wprep_dwqs = 0
        if _PH & 2:
            with ExitStack() as ph1:
                wq_sb = ph1.enter_context(nc.sbuf_tensor("wq_sb", [128, 8, D], BF16))
                wk_sb = ph1.enter_context(nc.sbuf_tensor("wk_sb", [128, 8, D], BF16))
                wv_sb = ph1.enter_context(nc.sbuf_tensor("wv_sb", [128, 8, D], BF16))
                xfm0 = ph1.enter_context(nc.sbuf_tensor("xfm0", [128, 8, 512], BF16))
                xtok = ph1.enter_context(nc.sbuf_tensor("xtok", [128, 4, D], F32))
                QzA0 = ph1.enter_context(nc.sbuf_tensor("QzA0", [128, 4096], BF16))
                QzA1 = ph1.enter_context(nc.sbuf_tensor("QzA1", [128, 4096], BF16))
                QzB0 = ph1.enter_context(nc.sbuf_tensor("QzB0", [128, 4096], BF16))
                QzB1 = ph1.enter_context(nc.sbuf_tensor("QzB1", [128, 4096], BF16))
                Kt0 = ph1.enter_context(nc.sbuf_tensor("Kt0", [128, 4096], BF16))
                Kt1 = ph1.enter_context(nc.sbuf_tensor("Kt1", [128, 4096], BF16))
                Vt0 = ph1.enter_context(nc.sbuf_tensor("Vt0", [128, 8, 512], BF16))
                bQK = ph1.enter_context(nc.sbuf_tensor("bQK", [128, 8, 512], BF16))
                Vp0 = ph1.enter_context(nc.sbuf_tensor("Vp0", [128, 32, 68], BF16))
                Vp1 = ph1.enter_context(nc.sbuf_tensor("Vp1", [128, 32, 68], BF16))
                P0 = ph1.enter_context(nc.sbuf_tensor("P0", [128, 1024], BF16))
                P1 = ph1.enter_context(nc.sbuf_tensor("P1", [128, 1024], BF16))
                P2 = ph1.enter_context(nc.sbuf_tensor("P2", [128, 1024], BF16))
                Osb = ph1.enter_context(nc.sbuf_tensor("Osb", [128, 1024], F32))
                yT0 = ph1.enter_context(nc.sbuf_tensor("yT0", [128, 4, D], F32))
                yT1 = ph1.enter_context(nc.sbuf_tensor("yT1", [128, 4, D], F32))
                rec = ph1.enter_context(nc.sbuf_tensor("rec", [128, 8], F32))
                wpb0 = ph1.enter_context(nc.sbuf_tensor("wpb0", [128, 512], F32))
                wpb1 = ph1.enter_context(nc.sbuf_tensor("wpb1", [128, 512], F32))
                wpb2 = ph1.enter_context(nc.sbuf_tensor("wpb2", [128, 512], F32))
                wpb3 = ph1.enter_context(nc.sbuf_tensor("wpb3", [128, 512], F32))
                wpq0 = ph1.enter_context(nc.sbuf_tensor("wpq0", [128, 512], BF16))
                wpq1 = ph1.enter_context(nc.sbuf_tensor("wpq1", [128, 512], BF16))
                wcol = ph1.enter_context(nc.sbuf_tensor("wcol", [128, 4], F32))
                wsc = ph1.enter_context(nc.sbuf_tensor("wsc", [1, 4], F32))
                S0 = ph1.enter_context(nc.psum_tensor("S0", [128, 1024], F32))
                S1 = ph1.enter_context(nc.psum_tensor("S1", [128, 1024], F32))
                Ops = ph1.enter_context(nc.psum_tensor("Ops", [128, 1024], F32))
                yTp = ph1.enter_context(nc.psum_tensor("yTp", [128, 512], F32))
                Vps = ph1.enter_context(nc.psum_tensor("Vps", [128, 1024], BF16))

                xfms = [xfm0, xfm0]
                QzAs = [QzA0, QzA1]; QzBs = [QzB0, QzB1]
                Kts = [Kt0, Kt1]; Vts = [Vt0, Vt0]
                Vps_ = [Vp0, Vp1]; Ps = [P0, P1, P2]; yTs = [yT0, yT1]
                Ss = [S0, S1]

                nc.gpsimd.wait_ge(vQNT.h, wprep_vqnt)

                nc.gpsimd.dma_start(wq_sb[:, :, :],
                                    wqT.rearrange("(j p) k -> p j k", p=128)).then_inc(dWA[0].h, 16); dWA[0].n += 16
                nc.gpsimd.dma_start(xfm0[:, :, :],
                                    xT.rearrange("(j p) t -> p j t", p=128)[:, :, 0:512]).then_inc(dXF.h, 16); dXF.n += 16
                for (wi, (w_sb, w_hbm)) in enumerate(((wk_sb, wkT), (wv_sb, wvT))):
                    nc.gpsimd.dma_start(w_sb[:, :, :],
                                        w_hbm.rearrange("(j p) k -> p j k", p=128)).then_inc(dWA[1 + wi].h, 16); dWA[1 + wi].n += 16
                vx("memset", Vp0[:, :, 64:65], 1.0)
                vx("memset", Vp1[:, :, 64:65], 1.0)
                vx("memset", QzA0[64:128, :], 0.0)
                vx("memset", QzA1[64:128, :], 0.0)
                vx("memset", QzB0[0:64, :], 0.0)
                ins = vx("memset", QzB1[0:64, :], 0.0)
                vMSC.inc(ins)

                nc.sync.wait_ge(vQNT.h, wprep_vqnt)

                nc.sync.dma_start(xtok[:, :, :],
                                  xN[0:512, :].rearrange("(g p) d -> p g d", p=128)).then_inc(dXT.h, 16); dXT.n += 16

                exp_cnt = 0
                m1_cnt = 0
                m2_cnt = 0
                slot_evt = [0] * 4
                exp_at_pair = []
                pending_stats = []
                m2_at_block = {}
                stt_at_pair = []
                spill_at_pair = []
                vev_at_block = {}

                def pslot(k):
                    t_ = Ss[(k // 2) % 2]
                    return t_, 512 * (k % 2)

                wbufs = [wpb0, wpb1, wpb2, wpb3]
                wspecs = [
                    ("g", wgT, wguq_d, 8, INNER, 0, 3),
                    ("u", wuT, wguq_d, 8, INNER, 1, 4),
                    ("d", wdT, wdq_d, 32, D, 2, 5),
                ]
                wspec_by = {sp[0]: sp for sp in wspecs}
                wsub = {}
                for (nm, srcw, dstw, nch, wdt, s_slot, m_slot) in wspecs:
                    subs = []
                    for c in range(nch):
                        for c0 in range(0, wdt, 512):
                            subs.append((c, c0, 512))
                    wsub[nm] = subs
                wstream = []
                for kind, nm in (("A", "g"), ("A", "u"), ("B", "g"),
                                 ("A", "d"), ("B", "u"), ("B", "d")):
                    for sub in wsub[nm]:
                        wstream.append((kind, nm, sub))
                segA_last = {}
                segA_first = {}
                segB_first = {}
                for k, (kind, nm, sub) in enumerate(wstream):
                    if kind == "A":
                        segA_last[nm] = k
                        segA_first.setdefault(nm, k)
                    else:
                        segB_first.setdefault(nm, k)
                fin_mark = {}
                wprocmark = {}
                wdma_issued = [False] * (len(wstream) + 1)
                wpqs = [wpq0, wpq1]
                wqs_mark = [0, 0]
                bq_idx = [0]

                def w_issue_dma(k):
                    if k >= len(wstream) or wdma_issued[k]:
                        return
                    wdma_issued[k] = True
                    (kind, nm, (c, c0, w_)) = wstream[k]
                    srcw = wspec_by[nm][1]
                    par = k % 4
                    dw = dWP[par]
                    if k >= 4:
                        (eng, val) = wprocmark[k - 4]
                        nc.gpsimd.wait_ge(vORD.h if eng == "V" else gORD.h, val)
                    nc.gpsimd.dma_start(wbufs[par][:, 0:w_],
                                        srcw[c * 128:(c + 1) * 128, c0:c0 + w_]).then_inc(dw.h, 16)
                    dw.n += 16

                def w_step(k):
                    (kind, nm, (c, c0, w_)) = wstream[k]
                    (nm2, srcw, dstw, nch, wdt, s_slot, m_slot) = wspec_by[nm]
                    par = k % 4
                    dw = dWP[par]
                    b = wbufs[par]
                    need = 16 * (k // 4 + 1)
                    if kind == "A":
                        ac = {"g": 0, "u": 1, "d": 2}[nm]
                        nc.vector.wait_ge(dw.h, need)
                        if k == segA_first[nm]:
                            vx("memset", wcol[:, ac:ac + 1], 0.0)
                        vx("tensor_reduce", wcol[:, 3:4], b[:, 0:w_], AX.X, ALU.add,
                           apply_absolute_value=True)
                        ins = vx("tensor_add", wcol[:, ac:ac + 1], wcol[:, ac:ac + 1],
                                 wcol[:, 3:4])
                        wprocmark[k] = ("V", vORD.n)
                    else:
                        nc.vector.wait_ge(dw.h, need)
                        if k == segB_first[nm]:
                            nc.vector.wait_ge(vMSC.h, fin_mark[nm])
                        vx("tensor_scalar", b[:, 0:w_], b[:, 0:w_],
                           SCAL[:, s_slot:s_slot + 1], MAGIC, ALU.mult, ALU.add)
                        vx("tensor_scalar", b[:, 0:w_], b[:, 0:w_], -MAGIC, 1.0,
                           ALU.add, ALU.min)
                        bp = bq_idx[0] % 2
                        bq_idx[0] += 1
                        wpq = wpqs[bp]
                        dq = dWQS[bp]
                        if wqs_mark[bp] > 0:
                            nc.vector.wait_ge(dq.h, wqs_mark[bp])
                        ins = vx("tensor_scalar", wpq[:, 0:w_], b[:, 0:w_], -1.0, None, ALU.max)
                        nc.sync.wait_ge(vORD.h, vORD.n)
                        row_off = 8 if nm == "u" else 0
                        nc.sync.dma_start(dstw[row_off + c, :, c0:c0 + w_],
                                            wpq[:, 0:w_]).then_inc(dq.h, 16)
                        dq.n += 16
                        wqs_mark[bp] = dq.n
                        wprocmark[k] = ("V", vORD.n)
                    w_issue_dma(k + 1)
                    w_issue_dma(k + 2)

                def w_finalize(nm, ytp_guard):
                    (nm2, srcw, dstw, nch, wdt, s_slot, m_slot) = wspec_by[nm]
                    nel = float(nch * 128 * wdt)
                    (eng, val) = wprocmark[segA_last[nm]]
                    nc.tensor.wait_ge(vORD.h, val)
                    if ytp_guard > 0:
                        nc.tensor.wait_ge(vYST.h, ytp_guard)
                    ac = {"g": 0, "u": 1, "d": 2}[nm]
                    ins = nc.tensor.matmul(yTp[0:1, 0:1], wcol[:, ac:ac + 1], ones_c[:, :],
                                           start=True, stop=True)
                    tMSC.inc(ins)
                    nc.vector.wait_ge(tMSC.h, tMSC.n)
                    vx("tensor_scalar", wsc[0:1, 0:1], yTp[0:1, 0:1],
                       1.0 / nel, 1e-5, ALU.mult, ALU.max)
                    ins = vx("reciprocal", wsc[0:1, 1:2], wsc[0:1, 0:1])
                    vMSC.inc(ins)
                    nc.tensor.wait_ge(vMSC.h, vMSC.n)
                    nc.tensor.matmul(yTp[:, 1:2], ones_r[:, :], wsc[0:1, 0:1],
                                     start=True, stop=True)
                    ins = nc.tensor.matmul(yTp[:, 2:3], ones_r[:, :], wsc[0:1, 1:2],
                                           start=True, stop=True)
                    tMSC.inc(ins)
                    nc.vector.wait_ge(tMSC.h, tMSC.n)
                    vx("tensor_copy", SCAL[:, m_slot:m_slot + 1], yTp[:, 1:2])
                    ins = vx("tensor_copy", SCAL[:, s_slot:s_slot + 1], yTp[:, 2:3])
                    vMSC.inc(ins)
                    fin_mark[nm] = vMSC.n

                wsched = {0: list(range(0, 96)), 1: list(range(96, 192)),
                          2: list(range(192, 304)), 3: list(range(304, 384))}
                wfin = {1: "g", 2: "u", 3: "d"}
                if not (_PH & 1):
                    wsched = {0: [], 1: [], 2: [], 3: []}
                    wfin = {}
                else:
                    w_issue_dma(0)
                    w_issue_dma(1)
                    w_issue_dma(2)

                deferred_evac = None
                stats_osb_mark = [0]
                for beta in range(NPAIR):
                    xf = xfms[beta % 2]
                    QzA, QzB = QzAs[beta % 2], QzBs[beta % 2]
                    Kt, Vt = Kts[beta % 2], Vts[beta % 2]
                    yT = yTs[beta % 2]

                    if beta in wfin:
                        w_finalize(wfin[beta], stt_at_pair[beta - 1])
                    wq_todo = [(lambda k=k: w_step(k)) for k in wsched.get(beta, [])]
                    wq_point = [0]
                    while pending_stats:
                        wq_todo.insert(0, pending_stats.pop())
                    _npts = 52 if beta == NPAIR - 1 else 64
                    _nw = len(wq_todo)
                    wq_counts = [_nw * (i + 1) // _npts - _nw * i // _npts
                                 for i in range(_npts)] + [0] * 16
                    def wq_drain():
                        take = wq_counts[wq_point[0]]
                        wq_point[0] += 1
                        for _ in range(take):
                            if wq_todo:
                                wq_todo.pop(0)()

                    nc.tensor.wait_ge(dXF.h, 16 * (beta + 1))
                    if beta == 0:
                        nc.tensor.wait_ge(vMSC.h, vMSC.n)
                        nc.tensor.wait_ge(dC.h, dC.n)
                    if beta >= 1:
                        nc.vector.wait_ge(dBQ.h, dBQ.n)
                        nc.vector.wait_ge(dBK.h, dBK.n)
                        if beta >= 2:
                            nc.vector.wait_ge(sEXP.h, exp_at_pair[beta - 2])
                    kslot = 0
                    for (wsb, which) in ((wq_sb, "q"), (wk_sb, "k"), (wv_sb, "v")):
                        if beta == 0:
                            nc.tensor.wait_ge(dWA[{"q": 0, "k": 1, "v": 2}[which]].h, 16)
                        for i in range(8):
                            pt, off = pslot(kslot)
                            if beta > 0 and kslot < 4:
                                nc.tensor.wait_ge(sEXP.h, exp_at_pair[beta - 1])
                            if slot_evt[kslot % 4] > 0:
                                nc.tensor.wait_ge(vPEV.h, slot_evt[kslot % 4])
                            ins = None
                            for j in range(8):
                                ins = nc.tensor.matmul(pt[:, off:off + 512],
                                                       wsb[:, j, i * 128:(i + 1) * 128],
                                                       xf[:, j, :],
                                                       start=(j == 0), stop=(j == 7))
                            tPRJ.inc(ins)
                            nc.vector.wait_ge(tPRJ.h, tPRJ.n)
                            if which == "v":
                                ins = vx("tensor_copy", Vt[:, i, :], pt[:, off:off + 512])
                            elif which == "k":
                                vx("tensor_copy", Kt[0:64, (2 * i) * 256:(2 * i) * 256 + 256],
                                   pt[0:64, off:off + 256])
                                vx("tensor_copy", Kt[64:128, (2 * i + 1) * 256:(2 * i + 1) * 256 + 256],
                                   pt[64:128, off + 256:off + 512])
                                vx("tensor_copy", bQK[0:64, i, 256:512], pt[0:64, off + 256:off + 512])
                                ins = vx("tensor_copy", bQK[64:128, i, 256:512], pt[64:128, off:off + 256])
                            else:
                                vx("tensor_copy", QzA[0:64, (2 * i) * 256:(2 * i) * 256 + 256],
                                   pt[0:64, off:off + 256])
                                vx("tensor_copy", QzB[64:128, (2 * i + 1) * 256:(2 * i + 1) * 256 + 256],
                                   pt[64:128, off + 256:off + 512])
                                vx("tensor_copy", bQK[0:64, i, 0:256], pt[0:64, off + 256:off + 512])
                                ins = vx("tensor_copy", bQK[64:128, i, 0:256], pt[64:128, off:off + 256])
                            slot_evt[kslot % 4] = vPEV.inc(ins)
                            kslot += 1
                    nc.sync.wait_ge(vPEV.h, vPEV.n)
                    dst_e = Kt[64:128, :].rearrange("p (i q s) -> p i q s", q=2, s=256)[:, :, 0, :]
                    nc.sync.dma_start(dst_e, bQK[0:64, :, 256:512]).then_inc(dBK.h, 16); dBK.n += 16
                    dst_o = Kt[0:64, :].rearrange("p (i q s) -> p i q s", q=2, s=256)[:, :, 1, :]
                    nc.sync.dma_start(dst_o, bQK[64:128, :, 256:512]).then_inc(dBK.h, 16); dBK.n += 16
                    dst_e = QzB[64:128, :].rearrange("p (i q s) -> p i q s", q=2, s=256)[:, :, 0, :]
                    nc.sync.dma_start(dst_e, bQK[0:64, :, 0:256]).then_inc(dBQ.h, 16); dBQ.n += 16
                    dst_o = QzA[0:64, :].rearrange("p (i q s) -> p i q s", q=2, s=256)[:, :, 1, :]
                    nc.sync.dma_start(dst_o, bQK[64:128, :, 0:256]).then_inc(dBQ.h, 16); dBQ.n += 16

                    nc.tensor.wait_ge(vPEV.h, vPEV.n)
                    for h in (range(2) if _P1L >= 2 else []):
                        Vp = Vps_[h]
                        prev_m2 = m2_at_block.get((beta - 1, h), 0)
                        for batch in range(2):
                            nc.tensor.wait_ge(vVEV.h, vVEV.n)
                            ins = None
                            for s in range(8):
                                i = batch * 4 + s // 2
                                kh = s % 2
                                ins = nc.tensor.transpose(
                                    Vps[:, s * 128:(s + 1) * 128],
                                    Vt[:, i, h * 256 + kh * 128:h * 256 + (kh + 1) * 128],
                                    identb[:, :])
                            tVTR.inc(ins)
                            nc.vector.wait_ge(tVTR.h, tVTR.n)
                            if prev_m2 > 0 and batch == 0:
                                nc.vector.wait_ge(tM2.h, prev_m2)
                            srcv = Vps[:, :].rearrange("p (a b c) -> p a b c", b=2, c=128)
                            dstv = Vp[:, 16 * batch:16 * batch + 16, 0:64].rearrange(
                                "p (a r) c -> p a r c", r=4)
                            vx("tensor_copy", dstv[:, :, 0:2, :], srcv[:, :, :, 0:64])
                            ins = vx("tensor_copy", dstv[:, :, 2:4, :], srcv[:, :, :, 64:128])
                            vVEV.inc(ins)
                        vev_at_block[(beta, h)] = vVEV.n

                    def mk_evac(beta, h, qg, yT):
                        def run():
                            for w in range(2):
                                nc.tensor.wait_ge(sOCP.h, sOCP.n)
                                if tYTR.n >= 1:
                                    nc.tensor.wait_ge(vYST.h, vYST.n)
                                ins = None
                                for pp in range(4):
                                    p = w * 4 + pp
                                    ins = nc.tensor.transpose(yTp[:, pp * 128:pp * 128 + 65],
                                                              Osb[0:65, p * 128:(p + 1) * 128],
                                                              identf[0:65, 0:65])
                                tYTR.inc(ins)
                                nc.vector.wait_ge(tYTR.h, tYTR.n)
                                if beta == 0 and h == 0 and qg == 0 and w == 0:
                                    nc.vector.wait_ge(dXT.h, dXT.n)
                                if beta >= 1 and h == 0 and qg == 0 and w == 0:
                                    if beta >= 2:
                                        pp_, vv_ = spill_at_pair[beta - 2]
                                        nc.vector.wait_ge(dSPL[pp_].h, vv_)
                                    nc.vector.wait_ge(dXT.h, 16 * (beta + 1))
                                ins = None
                                for pp in range(4):
                                    p = w * 4 + pp
                                    qk = 4 * qg + p // 2
                                    g = 2 * h + (p % 2)
                                    c0 = pp * 128
                                    vx("reciprocal", rec[:, p:p + 1], yTp[:, c0 + 64:c0 + 65])
                                    ins = vx("scalar_tensor_tensor",
                                        yT[:, g, qk * 64:(qk + 1) * 64],
                                        yTp[:, c0:c0 + 64], rec[:, p:p + 1],
                                        xtok[:, g, qk * 64:(qk + 1) * 64],
                                        ALU.mult, ALU.add)
                                vYST.inc(ins)
                            if beta == NPAIR - 1 and qg == NQG - 1:
                                mk_stats_g(beta, yT, 2 * h)()
                                mk_stats_g(beta, yT, 2 * h + 1)()
                        return run
                    for h in (range(2) if _P1L >= 3 else []):
                        o = h * 64
                        Vp = Vps_[h]
                        nc.tensor.wait_ge(vVEV.h, vev_at_block[(beta, h)])
                        if h == 0:
                            nc.tensor.wait_ge(dBQ.h, dBQ.n)
                            nc.tensor.wait_ge(dBK.h, dBK.n)
                        Qz = QzA if h == 0 else QzB
                        for qg in range(NQG):
                            LAG = 2
                            e_base = exp_cnt
                            for kc in range(NKC + LAG):
                                if kc == 1 and deferred_evac is not None:
                                    deferred_evac()
                                    deferred_evac = None
                                if kc < NKC:
                                    e = exp_cnt + 1
                                    si = (e - 1) % 2
                                    Sb = Ss[si]
                                    if e > 2:
                                        nc.tensor.wait_ge(sEXP.h, e - 2)
                                    nc.tensor.matmul(Sb[:, 0:512],
                                                     Kt[:, kc * 128:(kc + 1) * 128],
                                                     Qz[:, qg * 1024:qg * 1024 + 512],
                                                     start=True, stop=True)
                                    ins = nc.tensor.matmul(Sb[:, 512:1024],
                                                           Kt[:, kc * 128:(kc + 1) * 128],
                                                           Qz[:, qg * 1024 + 512:(qg + 1) * 1024],
                                                           start=True, stop=True)
                                    m1_cnt = tM1.inc(ins)
                                    nc.scalar.wait_ge(tM1.h, m1_cnt)
                                    if e > 3:
                                        nc.scalar.wait_ge(tM2.h, e - 3)
                                    if e == 1:
                                        nc.scalar.wait_ge(dC.h, dC.n)
                                    ins = nc.scalar.activation(Ps[(e - 1) % 3][:, :], Sb[:, :], AF.Exp,
                                                               bias=expb[:, 0:1], scale=1.0)
                                    exp_cnt = sEXP.inc(ins)
                                if _P1L < 4:
                                    continue
                                kc2 = kc - LAG
                                if kc2 < 0:
                                    continue
                                x2 = e_base + kc2 + 1
                                if kc >= NKC:
                                    nc.tensor.wait_ge(sEXP.h, x2)
                                if kc2 == 0:
                                    nc.tensor.wait_ge(sOCP.h, sOCP.n)
                                Pb = Ps[(x2 - 1) % 3]
                                nc.tensor.matmul(Ops[0:65, 0:512], Vp[:, kc2, 0:65],
                                                 Pb[:, 0:512],
                                                 start=(kc2 == 0), stop=(kc2 == NKC - 1))
                                ins = nc.tensor.matmul(Ops[0:65, 512:1024], Vp[:, kc2, 0:65],
                                                       Pb[:, 512:1024],
                                                       start=(kc2 == 0), stop=(kc2 == NKC - 1))
                                m2_cnt = tM2.inc(ins)
                                if kc2 % 4 == 3:
                                    wq_drain()
                            if _P1L < 5:
                                continue
                            nc.scalar.wait_ge(tM2.h, m2_cnt)
                            if stats_osb_mark[0] > 0:
                                nc.scalar.wait_ge(vORD.h, stats_osb_mark[0])
                            ins = nc.scalar.activation(Osb[0:65, :], Ops[0:65, :], AF.Copy)
                            sOCP.inc(ins)
                            deferred_evac = mk_evac(beta, h, qg, yT)
                        m2_at_block[(beta, h)] = m2_cnt
                    if deferred_evac is not None:
                        deferred_evac()
                        deferred_evac = None
                    exp_at_pair.append(exp_cnt)

                    def mk_stats_g(beta, yT, g):
                        def run():
                            vx("tensor_mul", Osb[:, 0:1024], yT[:, g, :], yT[:, g, :])
                            vx("tensor_reduce", rec[:, 0:1], yT[:, g, :], AX.X, ALU.add)
                            vx("tensor_add", sacc[:, :], sacc[:, :], rec[:, 0:1])
                            vx("tensor_reduce", rec[:, 1:2], Osb[:, 0:1024], AX.X, ALU.add)
                            ins = vx("tensor_add", sqacc[:, :], sqacc[:, :], rec[:, 1:2])
                            stats_osb_mark[0] = vORD.n
                            if g == 3:
                                vSTA.inc(ins)
                                dsp = dSPL[beta % 2]
                                nc.sync.wait_ge(vSTA.h, vSTA.n)
                                nc.sync.dma_start(
                                    y_d[512 * beta:512 * (beta + 1), :].rearrange("(g p) d -> p g d", p=128),
                                    yT[:, :, :]).then_inc(dsp.h, 16); dsp.n += 16
                                spill_at_pair.append((beta % 2, dsp.n))
                        return run
                    if _P1L >= 6 and beta < NPAIR - 1:
                        for g in range(4):
                            pending_stats.append(mk_stats_g(beta, yT, g))
                    stt_at_pair.append(vYST.n)

                    if beta + 1 < NPAIR:
                        t0 = 512 * (beta + 1)
                        nc.gpsimd.wait_ge(tPRJ.h, tPRJ.n)
                        nc.gpsimd.dma_start(
                            xfms[(beta + 1) % 2][:, :, :],
                            xT.rearrange("(j p) t -> p j t", p=128)[:, :, t0:t0 + 512]).then_inc(dXF.h, 16); dXF.n += 16
                        nc.sync.wait_ge(vYST.h, stt_at_pair[beta])
                        nc.sync.dma_start(
                            xtok[:, :, :],
                            xN[t0:t0 + 512, :].rearrange("(g p) d -> p g d", p=128)).then_inc(dXT.h, 16); dXT.n += 16

                while pending_stats:
                    pending_stats.pop(0)()

        phase2_end = vMSC.n
        attn_T_prj = tPRJ.n
        attn_T_m1 = tM1.n
        attn_T_m2 = tM2.n
        attn_T_ytr = tYTR.n
        attn_T_vtr = tVTR.n
        attn_V_end = vSTA.n
        if _PH & 4:
            with (
                nc.psum_tensor("pb2", [128, 512], F32) as pb2,
            ):
                nc.tensor.wait_ge(vSTA.h, vSTA.n)
                nc.tensor.matmul(pb2[0:1, 0:1], sacc[:, :], ones_c[:, :], start=True, stop=True)
                ins = nc.tensor.matmul(pb2[0:1, 1:2], sqacc[:, :], ones_c[:, :],
                                       start=True, stop=True)
                tMSC.inc(ins)
                nc.vector.wait_ge(tMSC.h, tMSC.n)
                ins = vx("tensor_copy", ccs[0:1, 0:2], pb2[0:1, 0:2])
                vMSC.inc(ins)
                nc.sync.wait_ge(vMSC.h, vMSC.n)
                nc.sync.dma_start(cc_in, ccs[0:1, 0:2]).then_inc(dCC.h, 16); dCC.n += 16
                nc.gpsimd.wait_ge(dCC.h, dCC.n)
                ins = nc.gpsimd.collective_compute(
                    "AllReduce", ALU.add,
                    replica_groups=[[0, 1], [2, 3], [4, 5], [6, 7]],
                    ins=[cc_in], outs=[cc_out],
                )
                gCC.inc(ins)
                nc.sync.wait_ge(gCC.h, gCC.n)
                nc.sync.dma_start(ccs[0:1, 0:2], cc_out).then_inc(dCC.h, 16); dCC.n += 16
                nc.vector.wait_ge(dCC.h, dCC.n)
                vx("tensor_scalar_mul", ccs[0:1, 0:1], ccs[0:1, 0:1], 1.0 / NB_ELEM)
                vx("tensor_scalar_mul", ccs[0:1, 1:2], ccs[0:1, 1:2], 1.0 / NB_ELEM)
                vx("tensor_mul", ccs[0:1, 2:3], ccs[0:1, 0:1], ccs[0:1, 0:1])
                vx("tensor_sub", ccs[0:1, 1:2], ccs[0:1, 1:2], ccs[0:1, 2:3])
                ins = vx("tensor_scalar_add", ccs[0:1, 1:2], ccs[0:1, 1:2], EPS)
                vMSC.inc(ins)
                nc.scalar.wait_ge(vMSC.h, vMSC.n)
                ins = nc.scalar.activation(ccs[0:1, 2:3], ccs[0:1, 1:2], AF.Sqrt)
                sMSC.inc(ins)
                nc.vector.wait_ge(sMSC.h, sMSC.n)
                ins = vx("reciprocal", ccs[0:1, 3:4], ccs[0:1, 2:3])
                vMSC.inc(ins)
                nc.tensor.wait_ge(vMSC.h, vMSC.n)
                ins = nc.tensor.matmul(pb2[:, 3:4], ones_r[:, :], ccs[0:1, 3:4],
                                       start=True, stop=True)
                tMSC.inc(ins)
                nc.vector.wait_ge(tMSC.h, tMSC.n)
                ins = vx("tensor_copy", SCAL[:, 6:7], pb2[:, 3:4])
                vMSC.inc(ins)
            phase2_end = vMSC.n

        if _PH & 8:
            with ExitStack() as ph3:
                xz2 = ph3.enter_context(nc.sbuf_tensor("xz2", [128, 4, D], F32))
                zt = ph3.enter_context(nc.sbuf_tensor("zt", [128, 4, D], F32))
                tmpf = ph3.enter_context(nc.sbuf_tensor("tmpf", [128, 2048], F32))
                tmpb = ph3.enter_context(nc.sbuf_tensor("tmpb", [128, INNER], BF16))
                xqT = ph3.enter_context(nc.sbuf_tensor("xqT", [128, 8, 512], BF16))
                hsb = ph3.enter_context(nc.sbuf_tensor("hsb", [128, 4, INNER], F32))
                hqT = ph3.enter_context(nc.sbuf_tensor("hqT", [128, 32, 512], BF16))
                wgu0 = ph3.enter_context(nc.sbuf_tensor("wgu0", [128, 16, 512], BF16))
                wgu1 = ph3.enter_context(nc.sbuf_tensor("wgu1", [128, 16, 512], BF16))
                wd0 = ph3.enter_context(nc.sbuf_tensor("wd0", [128, 4, 1024], BF16))
                wd1 = ph3.enter_context(nc.sbuf_tensor("wd1", [128, 4, 1024], BF16))
                gs = ph3.enter_context(nc.sbuf_tensor("gs", [128, 512], F32))
                st8 = ph3.enter_context(nc.sbuf_tensor("st8", [128, 16], F32))
                pg8 = ph3.enter_context(nc.sbuf_tensor("pg8", [128, 12], F32))
                ah4 = ph3.enter_context(nc.sbuf_tensor("ah4", [128, 4], F32))
                ln6 = ph3.enter_context(nc.sbuf_tensor("ln6", [128, 12], F32))
                ln8 = ph3.enter_context(nc.sbuf_tensor("ln8", [128, 8], F32))
                lnS = ph3.enter_context(nc.sbuf_tensor("lnS", [128, 8], F32))
                PA = ph3.enter_context(nc.psum_tensor("PA", [128, 1024], F32))
                PB = ph3.enter_context(nc.psum_tensor("PB", [128, 1024], F32))
                PC = ph3.enter_context(nc.psum_tensor("PC", [128, 1024], F32))
                ptb = ph3.enter_context(nc.psum_tensor("ptb", [128, 1024], BF16))

                wgus = [wgu0, wgu1]
                wds = [wd0, wd1]
                gu_slab_mark = [0, 0]
                dwn_slab_mark = [0, 0]
                z2ev_mark = [0]
                PAB = [PA, PB]
                pab_sil = [0, 0]
                pab_hst = [0, 0]
                pab_z2 = 0

                xqt_mark = {}
                scg_mark = {}
                pkq = []

                def issue_zt(tau_, pairg, mark=None):
                    r0 = 512 * tau_ + 256 * pairg
                    ga = 2 * pairg
                    dyp = dY if pairg == 0 else dYb
                    nc.gpsimd.wait_ge(vZ2.h, vZ2.n if mark is None else mark)
                    nc.gpsimd.dma_start(
                        zt[:, ga:ga + 2, :],
                        y_d[r0:r0 + 256, :].rearrange("(g p) d -> p g d", p=128)
                    ).then_inc(dyp.h, 16); dyp.n += 16

                def issue_xz2(tau_, pairg):
                    r0 = 512 * tau_ + 256 * pairg
                    ga = 2 * pairg
                    dxp = dXN if pairg == 0 else dXNb
                    nc.gpsimd.wait_ge(dOUT.h, dOUT.n)
                    nc.gpsimd.dma_start(
                        xz2[:, ga:ga + 2, :],
                        xN[r0:r0 + 256, :].rearrange("(g p) d -> p g d", p=128)
                    ).then_inc(dxp.h, 16); dxp.n += 16

                def issue_loads(tau_, pairg):
                    issue_zt(tau_, pairg)
                    issue_xz2(tau_, pairg)

                def mk_ln1_closures(pairg):
                    ga, gb = 2 * pairg, 2 * pairg + 1
                    (dyp, dxp) = (dY, dXN) if pairg == 0 else (dYb, dXNb)
                    def c1():
                        nc.vector.wait_ge(dyp.h, dyp.n)
                        nc.vector.wait_ge(dxp.h, dxp.n)
                        for g in (ga, gb):
                            vx("scalar_tensor_tensor", zt[:, g, :], zt[:, g, :],
                               SCAL[:, 6:7], xz2[:, g, :], ALU.mult, ALU.add)
                        for g in (ga, gb):
                            vx("bn_stats", ln6[:, 0:6], zt[:, g, 0:512])
                            vx("bn_stats", ln6[:, 6:12], zt[:, g, 512:1024])
                            vx("bn_aggr", ln8[:, 2 * g:2 * g + 2], ln6[:, :])
                    def c2():
                        for i, g in ((0, ga), (1, gb)):
                            vx("tensor_reduce", st8[:, 0 + i:1 + i], zt[:, g, :], AX.X, ALU.max)
                            vx("tensor_reduce", st8[:, 2 + i:3 + i], zt[:, g, :], AX.X, ALU.min)
                        lpr = ln8[:, 2 * ga:2 * ga + 4].rearrange(
                            "p (g two) -> p two g", two=2)
                        vx("tensor_copy", pg8[:, ga:ga + 2], lpr[:, 0, :])
                        ins = vx("tensor_scalar_add", st8[:, 4:6], lpr[:, 1, :], EPS)
                        vMSC.inc(ins)
                        nc.scalar.wait_ge(vMSC.h, vMSC.n)
                        ins = nc.scalar.activation(pg8[:, 4 + ga:4 + ga + 2], st8[:, 4:6],
                                                   AF.Sqrt)
                        sMSC.inc(ins)
                    def c3():
                        nc.vector.wait_ge(sMSC.h, sMSC.n)
                        vx("reciprocal", pg8[:, 4 + ga:4 + ga + 2], pg8[:, 4 + ga:4 + ga + 2])
                        vx("tensor_sub", st8[:, 0:2], st8[:, 0:2], pg8[:, ga:ga + 2])
                        vx("tensor_sub", st8[:, 2:4], pg8[:, ga:ga + 2], st8[:, 2:4])
                        vx("tensor_max", st8[:, 0:2], st8[:, 0:2], st8[:, 2:4])
                        vx("tensor_mul", st8[:, 0:2], st8[:, 0:2], pg8[:, 4 + ga:4 + ga + 2])
                        vx("tensor_scalar_max", st8[:, 0:2], st8[:, 0:2], 1e-5)
                        vx("reciprocal", st8[:, 2:4], st8[:, 0:2])
                        vx("tensor_scalar_mul", st8[:, 2:4], st8[:, 2:4], 127.0)
                        vx("tensor_scalar_mul", st8[:, 4:6], st8[:, 0:2], 1.0 / 127.0)
                        for i, g in enumerate((ga, gb)):
                            vx("tensor_mul", st8[:, 8 + g:9 + g], st8[:, 4 + i:5 + i],
                               SCAL[:, 3:4])
                            vx("tensor_mul", st8[:, 12 + g:13 + g], st8[:, 4 + i:5 + i],
                               SCAL[:, 4:5])
                        scg_mark[ga] = scg_mark[gb] = vORD.n
                        vx("tensor_mul", st8[:, 0:2], pg8[:, 4 + ga:4 + ga + 2], st8[:, 2:4])
                        vx("tensor_mul", st8[:, 2:4], pg8[:, ga:ga + 2], st8[:, 0:2])
                        vx("tensor_scalar", st8[:, 2:4], st8[:, 2:4], -1.0, MAGIC,
                           ALU.mult, ALU.add)
                    def mk_quant(i, g):
                        def cq():
                            vx("tensor_scalar", tmpf[:, 0:D], zt[:, g, :],
                               st8[:, 0 + i:1 + i], st8[:, 2 + i:3 + i], ALU.mult, ALU.add)
                            vx("tensor_scalar", tmpf[:, 0:D], tmpf[:, 0:D], -MAGIC, 127.0,
                                                    ALU.add, ALU.min)
                            ins = vx("tensor_scalar", tmpb[:, 0:D], tmpf[:, 0:D],
                                     -128.0, None, ALU.max)
                            vQNT.inc(ins)
                        return cq
                    def mk_tr(g):
                        def ct():
                            nc.tensor.wait_ge(vQNT.h, vQNT.n)
                            if tTRX.n > 0:
                                nc.tensor.wait_ge(vTEV.h, vTEV.n)
                            ins = None
                            for jj in range(8):
                                ins = nc.tensor.transpose(ptb[:, jj * 128:(jj + 1) * 128],
                                                          tmpb[:, jj * 128:(jj + 1) * 128],
                                                          identb[:, :])
                            tTRX.inc(ins)
                            nc.vector.wait_ge(tTRX.h, tTRX.n)
                            nc.vector.wait_ge(tGU.h, tGU.n)
                            ins = vx("tensor_copy",
                                xqT[:, :, g * 128:(g + 1) * 128],
                                ptb[:, :].rearrange("p (j t) -> p j t", j=8))
                            vTEV.inc(ins)
                            xqt_mark[g] = vTEV.n
                        return ct
                    return [c1, c2, c3, mk_quant(0, ga), mk_tr(ga),
                            mk_quant(1, gb), mk_tr(gb)]

                def ln2_pair(tau_, p):
                    t0_ = 512 * tau_
                    ga, gb = 2 * p, 2 * p + 1
                    for i, g in ((0, ga), (1, gb)):
                        vx("bn_stats", ln6[:, 0:6], xz2[:, g, 0:512])
                        vx("bn_stats", ln6[:, 6:12], xz2[:, g, 512:1024])
                        vx("bn_aggr", lnS[:, 2 * i:2 * i + 2], ln6[:, :])
                    lp = lnS[:, 0:4].rearrange("p (g two) -> p two g", two=2)
                    ins = vx("tensor_scalar_add", lnS[:, 4:6], lp[:, 1, :], EPS)
                    vMSC.inc(ins)
                    nc.scalar.wait_ge(vMSC.h, vMSC.n)
                    ins = nc.scalar.activation(lnS[:, 6:8], lnS[:, 4:6], AF.Sqrt)
                    sMSC.inc(ins)
                    nc.vector.wait_ge(sMSC.h, sMSC.n)
                    vx("reciprocal", lnS[:, 6:8], lnS[:, 6:8])
                    for i, g in ((0, ga), (1, gb)):
                        ins = vx("tensor_scalar", xz2[:, g, :], xz2[:, g, :],
                                 lnS[:, 2 * i:2 * i + 1], lnS[:, 6 + i:7 + i],
                                 ALU.subtract, ALU.mult)
                        vZ2.inc(ins)
                        nc.sync.wait_ge(vZ2.h, vZ2.n)
                        nc.sync.dma_start(out[t0_ + g * 128:t0_ + (g + 1) * 128, :],
                                          xz2[:, g, :]).then_inc(dOUT.h, 16); dOUT.n += 16

                for tau in range(4):
                    t0 = 512 * tau
                    if tau == 0:
                        nc.sync.wait_ge(tPRJ.h, attn_T_prj)
                        nc.sync.wait_ge(tM1.h, attn_T_m1)
                        nc.sync.wait_ge(tM2.h, attn_T_m2)
                        nc.sync.wait_ge(tYTR.h, attn_T_ytr)
                        nc.sync.wait_ge(tVTR.h, attn_T_vtr)
                        nc.sync.wait_ge(vORD.h, attn_V_end)
                        nc.sync.wait_ge(dWQS[0].h, dWQS[0].n)
                        nc.sync.wait_ge(dWQS[1].h, dWQS[1].n)
                        nc.sync.wait_ge(dSPL[0].h, dSPL[0].n)
                        nc.sync.wait_ge(dSPL[1].h, dSPL[1].n)
                        nc.gpsimd.wait_ge(tPRJ.h, attn_T_prj)
                        nc.gpsimd.wait_ge(tM1.h, attn_T_m1)
                        nc.gpsimd.wait_ge(tM2.h, attn_T_m2)
                        nc.gpsimd.wait_ge(tYTR.h, attn_T_ytr)
                        nc.gpsimd.wait_ge(tVTR.h, attn_T_vtr)
                        nc.gpsimd.wait_ge(vORD.h, attn_V_end)
                        nc.gpsimd.wait_ge(dWQS[0].h, dWQS[0].n)
                        nc.gpsimd.wait_ge(dWQS[1].h, dWQS[1].n)
                        nc.gpsimd.wait_ge(dSPL[0].h, dSPL[0].n)
                        nc.gpsimd.wait_ge(dSPL[1].h, dSPL[1].n)
                        issue_loads(0, 0)
                        issue_loads(0, 1)
                        for st_ in mk_ln1_closures(0):
                            st_()
                        pkq = mk_ln1_closures(1)
                    else:
                        issue_xz2(tau, 1)
                        pkq = mk_ln1_closures(1)

                    def gu_slab(dst_par, ibig2, eng=None):
                        eng = eng or nc.sync
                        eng.wait_ge(tGU.h, gu_slab_mark[dst_par])
                        eng.wait_ge(dGU[dst_par].h, dGU[dst_par].n)
                        eng.dma_start(wgus[dst_par][:, :, :],
                                      wguq_d[:, :, ibig2 * 512:(ibig2 + 1) * 512]
                                      .rearrange("j p i -> p j i")
                                      ).then_inc(dGU[dst_par].h, 16)
                        dGU[dst_par].n += 16

                    def wd_slab(dst_par, sl_):
                        eng = nc.sync if dst_par == 0 else nc.gpsimd
                        eng.wait_ge(tDWN.h, dwn_slab_mark[dst_par])
                        eng.wait_ge(dWD[dst_par].h, dWD[dst_par].n)
                        eng.dma_start(
                            wds[dst_par][:, :, :],
                            wdq_d[4 * sl_:4 * (sl_ + 1), :, :]
                            .rearrange("i p d -> p i d")).then_inc(dWD[dst_par].h, 16)
                        dWD[dst_par].n += 16

                    def mk_hq_steps(g):
                        def s_scale(g=g):
                            vx("tensor_scalar_max", st8[:, 0:1], ah4[:, g:g + 1], 1e-5)
                            vx("reciprocal", st8[:, 1:2], st8[:, 0:1])
                            vx("tensor_scalar_mul", st8[:, 2:3], st8[:, 0:1], 1.0 / 127.0)
                            vx("tensor_mul", pg8[:, 8 + g:9 + g], st8[:, 2:3], SCAL[:, 5:6])
                            vx("tensor_scalar_mul", ah4[:, g:g + 1], st8[:, 1:2], 127.0)
                        yield s_scale
                        for hh in range(2):
                            def s_ts(g=g, hh=hh):
                                s0_, s1_ = hh * 2048, (hh + 1) * 2048
                                vx("tensor_scalar", tmpf[:, :], hsb[:, g, s0_:s1_],
                                   ah4[:, g:g + 1], MAGIC, ALU.mult, ALU.add)
                                vx("tensor_scalar", tmpf[:, :], tmpf[:, :], -MAGIC, 127.0,
                                   ALU.add, ALU.min)
                                ins = vx("tensor_scalar", tmpb[:, s0_:s1_], tmpf[:, :],
                                         -128.0, None, ALU.max)
                                vQNT.inc(ins)
                            yield s_ts
                        for grp in range(4):
                            def s_tr(g=g, grp=grp):
                                nc.tensor.wait_ge(vQNT.h, vQNT.n)
                                nc.tensor.wait_ge(vTEV.h, vTEV.n)
                                ins = None
                                for jj in range(8):
                                    i2 = grp * 8 + jj
                                    ins = nc.tensor.transpose(ptb[:, jj * 128:(jj + 1) * 128],
                                                              tmpb[:, i2 * 128:(i2 + 1) * 128],
                                                              identb[:, :])
                                tTRX.inc(ins)
                                nc.vector.wait_ge(tTRX.h, tTRX.n)
                                ins = vx("tensor_copy",
                                         hqT[:, grp * 8:(grp + 1) * 8, g * 128:(g + 1) * 128],
                                         ptb[:, :].rearrange("p (j t) -> p j t", j=8))
                                vTEV.inc(ins)
                            yield s_tr

                    def gu_pass(half):
                        for ibig in range(8):
                            par = ibig % 2
                            if 1 <= ibig and ibig + 1 < 8:
                                gu_slab((ibig + 1) % 2, ibig + 1)
                            sl = wgus[par]
                            nc.tensor.wait_ge(dGU[par].h, dGU[par].n)
                            for g in (2 * half, 2 * half + 1):
                                pt = PAB[g % 2]
                                if pab_sil[g % 2] > 0:
                                    nc.tensor.wait_ge(sSIL.h, pab_sil[g % 2])
                                if pab_hst[g % 2] > 0:
                                    nc.tensor.wait_ge(vHST.h, pab_hst[g % 2])
                                if ibig == 0:
                                    nc.tensor.wait_ge(vTEV.h, xqt_mark[g])
                                    if g == 2 * half:
                                        nc.tensor.wait_ge(vZ2.h, z2ev_mark[0])
                                ins = None
                                for j in range(8):
                                    nc.tensor.matmul(pt[:, 0:512],
                                                     xqT[:, j, g * 128:(g + 1) * 128],
                                                     sl[:, j, :],
                                                     start=(j == 0), stop=(j == 7))
                                    ins = nc.tensor.matmul(pt[:, 512:1024],
                                                           xqT[:, j, g * 128:(g + 1) * 128],
                                                           sl[:, 8 + j, :],
                                                           start=(j == 0), stop=(j == 7))
                                tGU.inc(ins)
                                nc.scalar.wait_ge(tGU.h, tGU.n)
                                if ibig == 0:
                                    nc.scalar.wait_ge(vORD.h, scg_mark[g])
                                if sSIL.n > 0:
                                    nc.scalar.wait_ge(vHST.h, vHST.n)
                                ins = nc.scalar.activation(gs[:, 0:512], pt[:, 0:512], AF.Silu,
                                                           scale=st8[:, 8 + g:9 + g])
                                pab_sil[g % 2] = sSIL.inc(ins)
                                nc.vector.wait_ge(sSIL.h, sSIL.n)
                                vx("scalar_tensor_tensor",
                                   hsb[:, g, ibig * 512:(ibig + 1) * 512],
                                   pt[:, 512:1024], st8[:, 12 + g:13 + g], gs[:, 0:512],
                                   ALU.mult, ALU.mult)
                                vx("tensor_reduce", st8[:, 6:7],
                                   hsb[:, g, ibig * 512:(ibig + 1) * 512],
                                   AX.X, ALU.max, apply_absolute_value=True)
                                if ibig == 0:
                                    ins = vx("tensor_copy", ah4[:, g:g + 1], st8[:, 6:7])
                                else:
                                    ins = vx("tensor_max", ah4[:, g:g + 1], ah4[:, g:g + 1], st8[:, 6:7])
                                pab_hst[g % 2] = vHST.inc(ins)
                                if pkq:
                                    pkq.pop(0)()
                            gu_slab_mark[par] = tGU.n

                    def down_pass(half, hq_mark):
                        z2p = {2 * half: PC, 2 * half + 1: PA}
                        pc_free = z2ev_mark[0]
                        for i in range(32):
                            sl4 = i // 4
                            if i > 0 and i % 4 == 0 and sl4 + 1 < 8:
                                wd_slab((sl4 + 1) % 2, sl4 + 1)
                            if i % 4 == 0:
                                nc.tensor.wait_ge(dWD[sl4 % 2].h, dWD[sl4 % 2].n)
                            if i == 0:
                                nc.tensor.wait_ge(vTEV.h, hq_mark)
                                nc.tensor.wait_ge(vZ2.h, pc_free)
                                if half == 0:
                                    nc.tensor.wait_ge(sSIL.h, sSIL.n)
                                    nc.tensor.wait_ge(vHST.h, vHST.n)
                            ins = None
                            for g in (2 * half, 2 * half + 1):
                                pt = z2p[g]
                                for dh in range(2):
                                    ins = nc.tensor.matmul(pt[:, dh * 512:(dh + 1) * 512],
                                                           hqT[:, i, g * 128:(g + 1) * 128],
                                                           wds[(i // 4) % 2][:, i % 4, dh * 512:(dh + 1) * 512],
                                                           start=(i == 0), stop=(i == 31))
                            tDWN.inc(ins)
                            if i % 4 == 3:
                                dwn_slab_mark[sl4 % 2] = tDWN.n
                            if i % 2 == 0 and pkq:
                                pkq.pop(0)()
                        nc.vector.wait_ge(tDWN.h, tDWN.n)
                        ins = None
                        for g in (2 * half, 2 * half + 1):
                            pt = z2p[g]
                            vx("tensor_scalar", tmpf[:, 0:D], zt[:, g, :],
                               pg8[:, g:g + 1], pg8[:, 4 + g:5 + g],
                               ALU.subtract, ALU.mult)
                            ins = vx("scalar_tensor_tensor",
                                     xz2[:, g, :],
                                     pt[:, :], pg8[:, 8 + g:9 + g], tmpf[:, 0:D],
                                     ALU.mult, ALU.add)
                        vZ2.inc(ins)
                        z2ev_mark[0] = vZ2.n

                    if tau == 0:
                        gu_slab(0, 0)
                        gu_slab(1, 1)
                    gu_pass(0)
                    while pkq:
                        pkq.pop(0)()
                    pkq = [s for g_ in (0, 1) for s in mk_hq_steps(g_)]
                    gu_slab(0, 0)
                    gu_slab(1, 1)
                    gu_pass(1)
                    while pkq:
                        pkq.pop(0)()
                    hq01 = vTEV.n
                    pkq = [s for g_ in (2, 3) for s in mk_hq_steps(g_)]
                    wd_slab(0, 0)
                    wd_slab(1, 1)
                    down_pass(0, hq01)
                    while pkq:
                        pkq.pop(0)()
                    hq23 = vTEV.n
                    wd_slab(0, 0)
                    wd_slab(1, 1)
                    if tau + 1 < 4:
                        issue_zt(tau + 1, 0, mark=z2ev_mark[0])
                    ln2_pair(tau, 0)
                    if tau + 1 < 4:
                        issue_xz2(tau + 1, 0)
                        pkq = mk_ln1_closures(0)
                    down_pass(1, hq23)
                    while pkq:
                        pkq.pop(0)()
                    if tau + 1 < 4:
                        gu_slab(0, 0)
                        gu_slab(1, 1)
                        issue_zt(tau + 1, 1, mark=z2ev_mark[0])
                    ln2_pair(tau, 1)

        if not (_PH & 8):
            nc.sync.wait_ge(vORD.h, vORD.n)
            nc.sync.dma_start(out[0:128, 0:8], SCAL[:, :]).then_inc(dOUT.h, 16); dOUT.n += 16
        nc.sync.wait_ge(dOUT.h, dOUT.n)
    return nc


_NC_CACHE = None


def kernel(L0_attn_x, L0_attn_Wq, L0_attn_Wk, L0_attn_Wv,
           L0_ffn_w_gate, L0_ffn_w_up, L0_ffn_w_down):
    global _NC_CACHE
    x = np.asarray(L0_attn_x, dtype=np.float32).reshape(4 * 4096, 1024)
    wqT = np.ascontiguousarray(np.asarray(L0_attn_Wq, np.float32).T)
    wkT = np.ascontiguousarray(np.asarray(L0_attn_Wk, np.float32).T)
    wvT = np.ascontiguousarray(np.asarray(L0_attn_Wv, np.float32).T)
    wgT = np.ascontiguousarray(np.asarray(L0_ffn_w_gate, np.float32).T)
    wuT = np.ascontiguousarray(np.asarray(L0_ffn_w_up, np.float32).T)
    wdT = np.ascontiguousarray(np.asarray(L0_ffn_w_down, np.float32).T)

    idb = np.eye(128, dtype=np.float32).astype(BD)
    idf = np.eye(128, dtype=np.float32)
    onesr = np.ones((1, 128), np.float32)
    onesc = np.ones((128, 1), np.float32)

    if _NC_CACHE is None:
        _NC_CACHE = build_nc()
    nc = _NC_CACHE

    in_maps = []
    for c in range(NCORE):
        rows = x[c * T:(c + 1) * T]
        in_maps.append({
            "xT": np.ascontiguousarray(rows.T),
            "xN": np.ascontiguousarray(rows),
            "wqT": wqT, "wkT": wkT, "wvT": wvT,
            "wgT": wgT, "wuT": wuT, "wdT": wdT,
            "idb": idb, "idf": idf,
            "onesr": onesr, "onesc": onesc,
        })
    res = run_bass_kernel_spmd(nc, in_maps, core_ids=list(range(NCORE)))
    full = np.concatenate([res.results[c]["out"] for c in range(NCORE)], axis=0)
    return full.reshape(4, 4096, 1024)


# version 26
# speedup vs baseline: 1.0032x; 1.0032x over previous
import os
import numpy as np
import ml_dtypes
from contextlib import ExitStack

import concourse.bass as bass
import concourse.mybir as mybir
from concourse.bass_utils import run_bass_kernel_spmd

F32 = mybir.dt.float32
BF16 = mybir.dt.bfloat16
AF = mybir.ActivationFunctionType
ALU = mybir.AluOpType
AX = mybir.AxisListType

SHIFT = 44.0
MAGIC = 12582912.0
EPS = 1e-5
NCORE = 8
T = 2048
NPAIR = 4
NQG = 4
NKC = 32
D = 1024
INNER = 4096
NB_ELEM = 4096 * 1024

BD = ml_dtypes.bfloat16
_PH0 = 0
_PH = int(os.environ.get("BASSK_PH", "15"))
_P1L = int(os.environ.get("BASSK_P1L", "9"))


class Ctr:
    def __init__(self, stack, nc, name):
        self.h = stack.enter_context(nc.semaphore(name))
        self.n = 0

    def inc(self, instr, k=1):
        instr.then_inc(self.h, k)
        self.n += k
        return self.n


def build_nc():
    nc = bass.Bass("TRN2", target_bir_lowering=False)

    xT = nc.dram_tensor("xT", [D, T], F32, kind="ExternalInput").ap()
    xN = nc.dram_tensor("xN", [T, D], F32, kind="ExternalInput").ap()
    wqT = nc.dram_tensor("wqT", [D, D], F32, kind="ExternalInput").ap()
    wkT = nc.dram_tensor("wkT", [D, D], F32, kind="ExternalInput").ap()
    wvT = nc.dram_tensor("wvT", [D, D], F32, kind="ExternalInput").ap()
    wgT = nc.dram_tensor("wgT", [D, INNER], F32, kind="ExternalInput").ap()
    wuT = nc.dram_tensor("wuT", [D, INNER], F32, kind="ExternalInput").ap()
    wdT = nc.dram_tensor("wdT", [INNER, D], F32, kind="ExternalInput").ap()
    idb = nc.dram_tensor("idb", [128, 128], BF16, kind="ExternalInput").ap()
    idf = nc.dram_tensor("idf", [128, 128], F32, kind="ExternalInput").ap()
    onesr = nc.dram_tensor("onesr", [1, 128], F32, kind="ExternalInput").ap()
    onesc = nc.dram_tensor("onesc", [128, 1], F32, kind="ExternalInput").ap()
    out = nc.dram_tensor("out", [T, D], F32, kind="ExternalOutput").ap()

    y_d = nc.dram_tensor("y_d", [T, D], F32, kind="Internal").ap()
    wguq_d = nc.dram_tensor("wguq_d", [16, 128, INNER], BF16, kind="Internal").ap()
    wdq_d = nc.dram_tensor("wdq_d", [32, 128, D], BF16, kind="Internal").ap()
    cc_in = nc.dram_tensor("cc_in", [1, 2], F32, kind="Internal").ap()
    cc_out = nc.dram_tensor("cc_out", [1, 2], F32, kind="Internal").ap()

    st = ExitStack()
    with st:
        C = lambda name: Ctr(st, nc, name)
        dC = C("dC"); dWA = [C("dWA0"), C("dWA1"), C("dWA2")]; dXF = C("dXF"); dXT = C("dXT")
        dBQ = C("dBQ"); dBK = C("dBK"); dSPL = [C("dSPL0"), C("dSPL1")]
        dWP = [C("dWP0"), C("dWP1"), C("dWP2"), C("dWP3")]; dWQS = [C("dWQS0"), C("dWQS1")]; dCC = C("dCC")
        dY = C("dY"); dYb = C("dYb"); dXN = C("dXN"); dXNb = C("dXNb"); dGU = [C("dGU0"), C("dGU1")]; dWD = [C("dWD0"), C("dWD1")]; dOUT = C("dOUT")
        tM1 = C("tM1"); tM2 = C("tM2"); sEXP = C("sEXP")
        tYTR = C("tYTR"); tPRJ = C("tPRJ"); tVTR = C("tVTR")
        gCC = C("gCC"); sMSC = C("sMSC"); tMSC = C("tMSC")
        tGU = C("tGU"); sSIL = C("sSIL"); sOCP = C("sOCP")
        tTRX = C("tTRX"); tDWN = C("tDWN")
        vORD = C("vORD")
        gORD = C("gORD")

        class VMark:
            def __init__(self):
                self.h = vORD.h
                self.n = 0
            def inc(self, ins):
                self.n = vORD.n
                return self.n

        vOCP = VMark(); vYST = VMark(); vPEV = VMark(); vVEV = VMark()
        vSTA = VMark(); vMSC = VMark(); vHST = VMark(); vQNT = VMark()
        vTEV = VMark(); vZ2 = VMark()

        def gx(_m, *a, **k):
            if gORD.n:
                nc.gpsimd.wait_ge(gORD.h, gORD.n)
            ins = getattr(nc.gpsimd, _m)(*a, **k)
            gORD.inc(ins)
            return ins

        def vx(_m, *a, **k):
            if vORD.n:
                nc.vector.wait_ge(vORD.h, vORD.n)
            ins = getattr(nc.vector, _m)(*a, **k)
            vORD.inc(ins)
            return ins

        sb = lambda *a: st.enter_context(nc.sbuf_tensor(*a))
        identb = sb("identb", [128, 128], BF16)
        identf = sb("identf", [128, 128], F32)
        ones_r = sb("ones_r", [1, 128], F32)
        ones_c = sb("ones_c", [128, 1], F32)
        expb = sb("expb", [128, 1], F32)
        SCAL = sb("SCAL", [128, 8], F32)
        sacc = sb("sacc", [128, 1], F32)
        ccs = sb("ccs", [1, 4], F32)
        sqacc = sb("sqacc", [128, 1], F32)

        for (dst_t, src_t) in ((identb, idb), (identf, idf),
                               (ones_r, onesr), (ones_c, onesc)):
            nc.sync.dma_start(dst_t[:, :], src_t).then_inc(dC.h, 16); dC.n += 16
        vx("memset", expb[:, :], -SHIFT)
        vx("memset", sacc[:, :], 0.0)
        ins = vx("memset", sqacc[:, :], 0.0)
        vMSC.inc(ins)

        wprep_vqnt = 0
        wprep_dwqs = 0
        if _PH & 2:
            with ExitStack() as ph1:
                wq_sb = ph1.enter_context(nc.sbuf_tensor("wq_sb", [128, 8, D], BF16))
                wk_sb = ph1.enter_context(nc.sbuf_tensor("wk_sb", [128, 8, D], BF16))
                wv_sb = ph1.enter_context(nc.sbuf_tensor("wv_sb", [128, 8, D], BF16))
                xfm0 = ph1.enter_context(nc.sbuf_tensor("xfm0", [128, 8, 512], BF16))
                xtok = ph1.enter_context(nc.sbuf_tensor("xtok", [128, 4, D], F32))
                QzA0 = ph1.enter_context(nc.sbuf_tensor("QzA0", [128, 4096], BF16))
                QzA1 = ph1.enter_context(nc.sbuf_tensor("QzA1", [128, 4096], BF16))
                QzB0 = ph1.enter_context(nc.sbuf_tensor("QzB0", [128, 4096], BF16))
                QzB1 = ph1.enter_context(nc.sbuf_tensor("QzB1", [128, 4096], BF16))
                Kt0 = ph1.enter_context(nc.sbuf_tensor("Kt0", [128, 4096], BF16))
                Kt1 = ph1.enter_context(nc.sbuf_tensor("Kt1", [128, 4096], BF16))
                Vt0 = ph1.enter_context(nc.sbuf_tensor("Vt0", [128, 8, 512], BF16))
                bQK = ph1.enter_context(nc.sbuf_tensor("bQK", [128, 8, 512], BF16))
                Vp0 = ph1.enter_context(nc.sbuf_tensor("Vp0", [128, 32, 68], BF16))
                Vp1 = ph1.enter_context(nc.sbuf_tensor("Vp1", [128, 32, 68], BF16))
                P0 = ph1.enter_context(nc.sbuf_tensor("P0", [128, 1024], BF16))
                P1 = ph1.enter_context(nc.sbuf_tensor("P1", [128, 1024], BF16))
                P2 = ph1.enter_context(nc.sbuf_tensor("P2", [128, 1024], BF16))
                Osb = ph1.enter_context(nc.sbuf_tensor("Osb", [128, 1024], F32))
                yT0 = ph1.enter_context(nc.sbuf_tensor("yT0", [128, 4, D], F32))
                yT1 = ph1.enter_context(nc.sbuf_tensor("yT1", [128, 4, D], F32))
                rec = ph1.enter_context(nc.sbuf_tensor("rec", [128, 8], F32))
                wpb0 = ph1.enter_context(nc.sbuf_tensor("wpb0", [128, 512], F32))
                wpb1 = ph1.enter_context(nc.sbuf_tensor("wpb1", [128, 512], F32))
                wpb2 = ph1.enter_context(nc.sbuf_tensor("wpb2", [128, 512], F32))
                wpb3 = ph1.enter_context(nc.sbuf_tensor("wpb3", [128, 512], F32))
                wpq0 = ph1.enter_context(nc.sbuf_tensor("wpq0", [128, 512], BF16))
                wpq1 = ph1.enter_context(nc.sbuf_tensor("wpq1", [128, 512], BF16))
                wcol = ph1.enter_context(nc.sbuf_tensor("wcol", [128, 4], F32))
                wsc = ph1.enter_context(nc.sbuf_tensor("wsc", [1, 4], F32))
                S0 = ph1.enter_context(nc.psum_tensor("S0", [128, 1024], F32))
                S1 = ph1.enter_context(nc.psum_tensor("S1", [128, 1024], F32))
                Ops = ph1.enter_context(nc.psum_tensor("Ops", [128, 1024], F32))
                yTp = ph1.enter_context(nc.psum_tensor("yTp", [128, 512], F32))
                Vps = ph1.enter_context(nc.psum_tensor("Vps", [128, 1024], BF16))

                xfms = [xfm0, xfm0]
                QzAs = [QzA0, QzA1]; QzBs = [QzB0, QzB1]
                Kts = [Kt0, Kt1]; Vts = [Vt0, Vt0]
                Vps_ = [Vp0, Vp1]; Ps = [P0, P1, P2]; yTs = [yT0, yT1]
                Ss = [S0, S1]

                nc.gpsimd.wait_ge(vQNT.h, wprep_vqnt)

                nc.gpsimd.dma_start(wq_sb[:, :, :],
                                    wqT.rearrange("(j p) k -> p j k", p=128)).then_inc(dWA[0].h, 16); dWA[0].n += 16
                nc.gpsimd.dma_start(xfm0[:, :, :],
                                    xT.rearrange("(j p) t -> p j t", p=128)[:, :, 0:512]).then_inc(dXF.h, 16); dXF.n += 16
                for (wi, (w_sb, w_hbm)) in enumerate(((wk_sb, wkT), (wv_sb, wvT))):
                    nc.gpsimd.dma_start(w_sb[:, :, :],
                                        w_hbm.rearrange("(j p) k -> p j k", p=128)).then_inc(dWA[1 + wi].h, 16); dWA[1 + wi].n += 16
                vx("memset", Vp0[:, :, 64:65], 1.0)
                vx("memset", Vp1[:, :, 64:65], 1.0)
                vx("memset", QzA0[64:128, :], 0.0)
                vx("memset", QzA1[64:128, :], 0.0)
                vx("memset", QzB0[0:64, :], 0.0)
                ins = vx("memset", QzB1[0:64, :], 0.0)
                vMSC.inc(ins)

                nc.sync.wait_ge(vQNT.h, wprep_vqnt)

                nc.sync.dma_start(xtok[:, :, :],
                                  xN[0:512, :].rearrange("(g p) d -> p g d", p=128)).then_inc(dXT.h, 16); dXT.n += 16

                exp_cnt = 0
                m1_cnt = 0
                m2_cnt = 0
                slot_evt = [0] * 4
                exp_at_pair = []
                pending_stats = []
                m2_at_block = {}
                stt_at_pair = []
                spill_at_pair = []
                vev_at_block = {}

                def pslot(k):
                    t_ = Ss[(k // 2) % 2]
                    return t_, 512 * (k % 2)

                wbufs = [wpb0, wpb1, wpb2, wpb3]
                wspecs = [
                    ("g", wgT, wguq_d, 8, INNER, 0, 3),
                    ("u", wuT, wguq_d, 8, INNER, 1, 4),
                    ("d", wdT, wdq_d, 32, D, 2, 5),
                ]
                wspec_by = {sp[0]: sp for sp in wspecs}
                wsub = {}
                for (nm, srcw, dstw, nch, wdt, s_slot, m_slot) in wspecs:
                    subs = []
                    for c in range(nch):
                        for c0 in range(0, wdt, 512):
                            subs.append((c, c0, 512))
                    wsub[nm] = subs
                wstream = []
                for kind, nm in (("A", "g"), ("A", "u"), ("B", "g"),
                                 ("A", "d"), ("B", "u"), ("B", "d")):
                    for sub in wsub[nm]:
                        wstream.append((kind, nm, sub))
                segA_last = {}
                segA_first = {}
                segB_first = {}
                for k, (kind, nm, sub) in enumerate(wstream):
                    if kind == "A":
                        segA_last[nm] = k
                        segA_first.setdefault(nm, k)
                    else:
                        segB_first.setdefault(nm, k)
                fin_mark = {}
                wprocmark = {}
                wdma_issued = [False] * (len(wstream) + 1)
                wpqs = [wpq0, wpq1]
                wqs_mark = [0, 0]
                bq_idx = [0]

                def w_issue_dma(k):
                    if k >= len(wstream) or wdma_issued[k]:
                        return
                    wdma_issued[k] = True
                    (kind, nm, (c, c0, w_)) = wstream[k]
                    srcw = wspec_by[nm][1]
                    par = k % 4
                    dw = dWP[par]
                    if k >= 4:
                        (eng, val) = wprocmark[k - 4]
                        nc.gpsimd.wait_ge(vORD.h if eng == "V" else gORD.h, val)
                    nc.gpsimd.dma_start(wbufs[par][:, 0:w_],
                                        srcw[c * 128:(c + 1) * 128, c0:c0 + w_]).then_inc(dw.h, 16)
                    dw.n += 16

                def w_step(k):
                    (kind, nm, (c, c0, w_)) = wstream[k]
                    (nm2, srcw, dstw, nch, wdt, s_slot, m_slot) = wspec_by[nm]
                    par = k % 4
                    dw = dWP[par]
                    b = wbufs[par]
                    need = 16 * (k // 4 + 1)
                    if kind == "A":
                        ac = {"g": 0, "u": 1, "d": 2}[nm]
                        nc.vector.wait_ge(dw.h, need)
                        if k == segA_first[nm]:
                            vx("memset", wcol[:, ac:ac + 1], 0.0)
                        vx("tensor_reduce", wcol[:, 3:4], b[:, 0:w_], AX.X, ALU.add,
                           apply_absolute_value=True)
                        ins = vx("tensor_add", wcol[:, ac:ac + 1], wcol[:, ac:ac + 1],
                                 wcol[:, 3:4])
                        wprocmark[k] = ("V", vORD.n)
                    else:
                        nc.vector.wait_ge(dw.h, need)
                        if k == segB_first[nm]:
                            nc.vector.wait_ge(vMSC.h, fin_mark[nm])
                        vx("tensor_scalar", b[:, 0:w_], b[:, 0:w_],
                           SCAL[:, s_slot:s_slot + 1], MAGIC, ALU.mult, ALU.add)
                        vx("tensor_scalar", b[:, 0:w_], b[:, 0:w_], -MAGIC, 1.0,
                           ALU.add, ALU.min)
                        bp = bq_idx[0] % 2
                        bq_idx[0] += 1
                        wpq = wpqs[bp]
                        dq = dWQS[bp]
                        if wqs_mark[bp] > 0:
                            nc.vector.wait_ge(dq.h, wqs_mark[bp])
                        ins = vx("tensor_scalar", wpq[:, 0:w_], b[:, 0:w_], -1.0, None, ALU.max)
                        nc.sync.wait_ge(vORD.h, vORD.n)
                        row_off = 8 if nm == "u" else 0
                        nc.sync.dma_start(dstw[row_off + c, :, c0:c0 + w_],
                                            wpq[:, 0:w_]).then_inc(dq.h, 16)
                        dq.n += 16
                        wqs_mark[bp] = dq.n
                        wprocmark[k] = ("V", vORD.n)
                    w_issue_dma(k + 1)
                    w_issue_dma(k + 2)

                def w_finalize(nm, ytp_guard):
                    (nm2, srcw, dstw, nch, wdt, s_slot, m_slot) = wspec_by[nm]
                    nel = float(nch * 128 * wdt)
                    (eng, val) = wprocmark[segA_last[nm]]
                    nc.tensor.wait_ge(vORD.h, val)
                    if ytp_guard > 0:
                        nc.tensor.wait_ge(vYST.h, ytp_guard)
                    ac = {"g": 0, "u": 1, "d": 2}[nm]
                    ins = nc.tensor.matmul(yTp[0:1, 0:1], wcol[:, ac:ac + 1], ones_c[:, :],
                                           start=True, stop=True)
                    tMSC.inc(ins)
                    nc.vector.wait_ge(tMSC.h, tMSC.n)
                    vx("tensor_scalar", wsc[0:1, 0:1], yTp[0:1, 0:1],
                       1.0 / nel, 1e-5, ALU.mult, ALU.max)
                    ins = vx("reciprocal", wsc[0:1, 1:2], wsc[0:1, 0:1])
                    vMSC.inc(ins)
                    nc.tensor.wait_ge(vMSC.h, vMSC.n)
                    nc.tensor.matmul(yTp[:, 1:2], ones_r[:, :], wsc[0:1, 0:1],
                                     start=True, stop=True)
                    ins = nc.tensor.matmul(yTp[:, 2:3], ones_r[:, :], wsc[0:1, 1:2],
                                           start=True, stop=True)
                    tMSC.inc(ins)
                    nc.vector.wait_ge(tMSC.h, tMSC.n)
                    vx("tensor_copy", SCAL[:, m_slot:m_slot + 1], yTp[:, 1:2])
                    ins = vx("tensor_copy", SCAL[:, s_slot:s_slot + 1], yTp[:, 2:3])
                    vMSC.inc(ins)
                    fin_mark[nm] = vMSC.n

                wsched = {0: list(range(0, 96)), 1: list(range(96, 192)),
                          2: list(range(192, 304)), 3: list(range(304, 384))}
                wfin = {1: "g", 2: "u", 3: "d"}
                if not (_PH & 1):
                    wsched = {0: [], 1: [], 2: [], 3: []}
                    wfin = {}
                else:
                    w_issue_dma(0)
                    w_issue_dma(1)
                    w_issue_dma(2)

                deferred_evac = None
                stats_osb_mark = [0]
                for beta in range(NPAIR):
                    xf = xfms[beta % 2]
                    QzA, QzB = QzAs[beta % 2], QzBs[beta % 2]
                    Kt, Vt = Kts[beta % 2], Vts[beta % 2]
                    yT = yTs[beta % 2]

                    if beta in wfin:
                        w_finalize(wfin[beta], stt_at_pair[beta - 1])
                    wq_todo = [(lambda k=k: w_step(k)) for k in wsched.get(beta, [])]
                    wq_point = [0]
                    while pending_stats:
                        wq_todo.insert(0, pending_stats.pop())
                    _npts = 52 if beta == NPAIR - 1 else 64
                    _nw = len(wq_todo)
                    wq_counts = [_nw * (i + 1) // _npts - _nw * i // _npts
                                 for i in range(_npts)] + [0] * 16
                    def wq_drain():
                        take = wq_counts[wq_point[0]]
                        wq_point[0] += 1
                        for _ in range(take):
                            if wq_todo:
                                wq_todo.pop(0)()

                    nc.tensor.wait_ge(dXF.h, 16 * (beta + 1))
                    if beta == 0:
                        nc.tensor.wait_ge(vMSC.h, vMSC.n)
                        nc.tensor.wait_ge(dC.h, dC.n)
                    if beta >= 1:
                        nc.vector.wait_ge(dBQ.h, dBQ.n)
                        nc.vector.wait_ge(dBK.h, dBK.n)
                        if beta >= 2:
                            nc.vector.wait_ge(sEXP.h, exp_at_pair[beta - 2])
                    kslot = 0
                    for (wsb, which) in ((wq_sb, "q"), (wk_sb, "k"), (wv_sb, "v")):
                        if beta == 0:
                            nc.tensor.wait_ge(dWA[{"q": 0, "k": 1, "v": 2}[which]].h, 16)
                        for i in range(8):
                            pt, off = pslot(kslot)
                            if beta > 0 and kslot < 4:
                                nc.tensor.wait_ge(sEXP.h, exp_at_pair[beta - 1])
                            if slot_evt[kslot % 4] > 0:
                                nc.tensor.wait_ge(vPEV.h, slot_evt[kslot % 4])
                            ins = None
                            for j in range(8):
                                ins = nc.tensor.matmul(pt[:, off:off + 512],
                                                       wsb[:, j, i * 128:(i + 1) * 128],
                                                       xf[:, j, :],
                                                       start=(j == 0), stop=(j == 7))
                            tPRJ.inc(ins)
                            nc.vector.wait_ge(tPRJ.h, tPRJ.n)
                            if which == "v":
                                ins = vx("tensor_copy", Vt[:, i, :], pt[:, off:off + 512])
                            elif which == "k":
                                vx("tensor_copy", Kt[0:64, (2 * i) * 256:(2 * i) * 256 + 256],
                                   pt[0:64, off:off + 256])
                                vx("tensor_copy", Kt[64:128, (2 * i + 1) * 256:(2 * i + 1) * 256 + 256],
                                   pt[64:128, off + 256:off + 512])
                                vx("tensor_copy", bQK[0:64, i, 256:512], pt[0:64, off + 256:off + 512])
                                ins = vx("tensor_copy", bQK[64:128, i, 256:512], pt[64:128, off:off + 256])
                            else:
                                vx("tensor_copy", QzA[0:64, (2 * i) * 256:(2 * i) * 256 + 256],
                                   pt[0:64, off:off + 256])
                                vx("tensor_copy", QzB[64:128, (2 * i + 1) * 256:(2 * i + 1) * 256 + 256],
                                   pt[64:128, off + 256:off + 512])
                                vx("tensor_copy", bQK[0:64, i, 0:256], pt[0:64, off + 256:off + 512])
                                ins = vx("tensor_copy", bQK[64:128, i, 0:256], pt[64:128, off:off + 256])
                            slot_evt[kslot % 4] = vPEV.inc(ins)
                            kslot += 1
                    nc.sync.wait_ge(vPEV.h, vPEV.n)
                    dst_e = Kt[64:128, :].rearrange("p (i q s) -> p i q s", q=2, s=256)[:, :, 0, :]
                    nc.sync.dma_start(dst_e, bQK[0:64, :, 256:512]).then_inc(dBK.h, 16); dBK.n += 16
                    dst_o = Kt[0:64, :].rearrange("p (i q s) -> p i q s", q=2, s=256)[:, :, 1, :]
                    nc.sync.dma_start(dst_o, bQK[64:128, :, 256:512]).then_inc(dBK.h, 16); dBK.n += 16
                    dst_e = QzB[64:128, :].rearrange("p (i q s) -> p i q s", q=2, s=256)[:, :, 0, :]
                    nc.sync.dma_start(dst_e, bQK[0:64, :, 0:256]).then_inc(dBQ.h, 16); dBQ.n += 16
                    dst_o = QzA[0:64, :].rearrange("p (i q s) -> p i q s", q=2, s=256)[:, :, 1, :]
                    nc.sync.dma_start(dst_o, bQK[64:128, :, 0:256]).then_inc(dBQ.h, 16); dBQ.n += 16

                    nc.tensor.wait_ge(vPEV.h, vPEV.n)
                    for h in (range(2) if _P1L >= 2 else []):
                        Vp = Vps_[h]
                        prev_m2 = m2_at_block.get((beta - 1, h), 0)
                        for batch in range(2):
                            nc.tensor.wait_ge(vVEV.h, vVEV.n)
                            ins = None
                            for s in range(8):
                                i = batch * 4 + s // 2
                                kh = s % 2
                                ins = nc.tensor.transpose(
                                    Vps[:, s * 128:(s + 1) * 128],
                                    Vt[:, i, h * 256 + kh * 128:h * 256 + (kh + 1) * 128],
                                    identb[:, :])
                            tVTR.inc(ins)
                            nc.vector.wait_ge(tVTR.h, tVTR.n)
                            if prev_m2 > 0 and batch == 0:
                                nc.vector.wait_ge(tM2.h, prev_m2)
                            srcv = Vps[:, :].rearrange("p (a b c) -> p a b c", b=2, c=128)
                            dstv = Vp[:, 16 * batch:16 * batch + 16, 0:64].rearrange(
                                "p (a r) c -> p a r c", r=4)
                            vx("tensor_copy", dstv[:, :, 0:2, :], srcv[:, :, :, 0:64])
                            ins = vx("tensor_copy", dstv[:, :, 2:4, :], srcv[:, :, :, 64:128])
                            vVEV.inc(ins)
                        vev_at_block[(beta, h)] = vVEV.n

                    def mk_evac(beta, h, qg, yT):
                        def run():
                            for w in range(2):
                                nc.tensor.wait_ge(sOCP.h, sOCP.n)
                                if tYTR.n >= 1:
                                    nc.tensor.wait_ge(vYST.h, vYST.n)
                                ins = None
                                for pp in range(4):
                                    p = w * 4 + pp
                                    ins = nc.tensor.transpose(yTp[:, pp * 128:pp * 128 + 65],
                                                              Osb[0:65, p * 128:(p + 1) * 128],
                                                              identf[0:65, 0:65])
                                tYTR.inc(ins)
                                nc.vector.wait_ge(tYTR.h, tYTR.n)
                                if beta == 0 and h == 0 and qg == 0 and w == 0:
                                    nc.vector.wait_ge(dXT.h, dXT.n)
                                if beta >= 1 and h == 0 and qg == 0 and w == 0:
                                    if beta >= 2:
                                        pp_, vv_ = spill_at_pair[beta - 2]
                                        nc.vector.wait_ge(dSPL[pp_].h, vv_)
                                    nc.vector.wait_ge(dXT.h, 16 * (beta + 1))
                                ins = None
                                for pp in range(4):
                                    p = w * 4 + pp
                                    qk = 4 * qg + p // 2
                                    g = 2 * h + (p % 2)
                                    c0 = pp * 128
                                    vx("reciprocal", rec[:, p:p + 1], yTp[:, c0 + 64:c0 + 65])
                                    ins = vx("scalar_tensor_tensor",
                                        yT[:, g, qk * 64:(qk + 1) * 64],
                                        yTp[:, c0:c0 + 64], rec[:, p:p + 1],
                                        xtok[:, g, qk * 64:(qk + 1) * 64],
                                        ALU.mult, ALU.add)
                                vYST.inc(ins)
                            if beta == NPAIR - 1 and qg == NQG - 1:
                                mk_stats_g(beta, yT, 2 * h)()
                                mk_stats_g(beta, yT, 2 * h + 1)()
                        return run
                    for h in (range(2) if _P1L >= 3 else []):
                        o = h * 64
                        Vp = Vps_[h]
                        nc.tensor.wait_ge(vVEV.h, vev_at_block[(beta, h)])
                        if h == 0:
                            nc.tensor.wait_ge(dBQ.h, dBQ.n)
                            nc.tensor.wait_ge(dBK.h, dBK.n)
                        Qz = QzA if h == 0 else QzB
                        for qg in range(NQG):
                            LAG = 2
                            e_base = exp_cnt
                            for kc in range(NKC + LAG):
                                if kc == 1 and deferred_evac is not None:
                                    deferred_evac()
                                    deferred_evac = None
                                if kc < NKC:
                                    e = exp_cnt + 1
                                    si = (e - 1) % 2
                                    Sb = Ss[si]
                                    if e > 2:
                                        nc.tensor.wait_ge(sEXP.h, e - 2)
                                    nc.tensor.matmul(Sb[:, 0:512],
                                                     Kt[:, kc * 128:(kc + 1) * 128],
                                                     Qz[:, qg * 1024:qg * 1024 + 512],
                                                     start=True, stop=True)
                                    ins = nc.tensor.matmul(Sb[:, 512:1024],
                                                           Kt[:, kc * 128:(kc + 1) * 128],
                                                           Qz[:, qg * 1024 + 512:(qg + 1) * 1024],
                                                           start=True, stop=True)
                                    m1_cnt = tM1.inc(ins)
                                    nc.scalar.wait_ge(tM1.h, m1_cnt)
                                    if e > 3:
                                        nc.scalar.wait_ge(tM2.h, e - 3)
                                    if e == 1:
                                        nc.scalar.wait_ge(dC.h, dC.n)
                                    ins = nc.scalar.activation(Ps[(e - 1) % 3][:, :], Sb[:, :], AF.Exp,
                                                               bias=expb[:, 0:1], scale=1.0)
                                    exp_cnt = sEXP.inc(ins)
                                if _P1L < 4:
                                    continue
                                kc2 = kc - LAG
                                if kc2 < 0:
                                    continue
                                x2 = e_base + kc2 + 1
                                if kc >= NKC:
                                    nc.tensor.wait_ge(sEXP.h, x2)
                                if kc2 == 0:
                                    nc.tensor.wait_ge(sOCP.h, sOCP.n)
                                Pb = Ps[(x2 - 1) % 3]
                                nc.tensor.matmul(Ops[0:65, 0:512], Vp[:, kc2, 0:65],
                                                 Pb[:, 0:512],
                                                 start=(kc2 == 0), stop=(kc2 == NKC - 1))
                                ins = nc.tensor.matmul(Ops[0:65, 512:1024], Vp[:, kc2, 0:65],
                                                       Pb[:, 512:1024],
                                                       start=(kc2 == 0), stop=(kc2 == NKC - 1))
                                m2_cnt = tM2.inc(ins)
                                if kc2 % 4 == 3:
                                    wq_drain()
                            if _P1L < 5:
                                continue
                            nc.scalar.wait_ge(tM2.h, m2_cnt)
                            if stats_osb_mark[0] > 0:
                                nc.scalar.wait_ge(vORD.h, stats_osb_mark[0])
                            ins = nc.scalar.activation(Osb[0:65, :], Ops[0:65, :], AF.Copy)
                            sOCP.inc(ins)
                            deferred_evac = mk_evac(beta, h, qg, yT)
                        m2_at_block[(beta, h)] = m2_cnt
                    if deferred_evac is not None:
                        deferred_evac()
                        deferred_evac = None
                    exp_at_pair.append(exp_cnt)

                    def mk_stats_g(beta, yT, g):
                        def run():
                            vx("tensor_mul", Osb[:, 0:1024], yT[:, g, :], yT[:, g, :])
                            vx("tensor_reduce", rec[:, 0:1], yT[:, g, :], AX.X, ALU.add)
                            vx("tensor_add", sacc[:, :], sacc[:, :], rec[:, 0:1])
                            vx("tensor_reduce", rec[:, 1:2], Osb[:, 0:1024], AX.X, ALU.add)
                            ins = vx("tensor_add", sqacc[:, :], sqacc[:, :], rec[:, 1:2])
                            stats_osb_mark[0] = vORD.n
                            if g == 3:
                                vSTA.inc(ins)
                                dsp = dSPL[beta % 2]
                                nc.sync.wait_ge(vSTA.h, vSTA.n)
                                nc.sync.dma_start(
                                    y_d[512 * beta:512 * (beta + 1), :].rearrange("(g p) d -> p g d", p=128),
                                    yT[:, :, :]).then_inc(dsp.h, 16); dsp.n += 16
                                spill_at_pair.append((beta % 2, dsp.n))
                        return run
                    if _P1L >= 6 and beta < NPAIR - 1:
                        for g in range(4):
                            pending_stats.append(mk_stats_g(beta, yT, g))
                    stt_at_pair.append(vYST.n)

                    if beta + 1 < NPAIR:
                        t0 = 512 * (beta + 1)
                        nc.gpsimd.wait_ge(tPRJ.h, tPRJ.n)
                        nc.gpsimd.dma_start(
                            xfms[(beta + 1) % 2][:, :, :],
                            xT.rearrange("(j p) t -> p j t", p=128)[:, :, t0:t0 + 512]).then_inc(dXF.h, 16); dXF.n += 16
                        nc.sync.wait_ge(vYST.h, stt_at_pair[beta])
                        nc.sync.dma_start(
                            xtok[:, :, :],
                            xN[t0:t0 + 512, :].rearrange("(g p) d -> p g d", p=128)).then_inc(dXT.h, 16); dXT.n += 16

                while pending_stats:
                    pending_stats.pop(0)()

        phase2_end = vMSC.n
        attn_T_prj = tPRJ.n
        attn_T_m1 = tM1.n
        attn_T_m2 = tM2.n
        attn_T_ytr = tYTR.n
        attn_T_vtr = tVTR.n
        attn_V_end = vSTA.n
        if _PH & 4:
            with (
                nc.psum_tensor("pb2", [128, 512], F32) as pb2,
            ):
                nc.tensor.wait_ge(vSTA.h, vSTA.n)
                nc.tensor.matmul(pb2[0:1, 0:1], sacc[:, :], ones_c[:, :], start=True, stop=True)
                ins = nc.tensor.matmul(pb2[0:1, 1:2], sqacc[:, :], ones_c[:, :],
                                       start=True, stop=True)
                tMSC.inc(ins)
                nc.vector.wait_ge(tMSC.h, tMSC.n)
                ins = vx("tensor_copy", ccs[0:1, 0:2], pb2[0:1, 0:2])
                vMSC.inc(ins)
                nc.sync.wait_ge(vMSC.h, vMSC.n)
                nc.sync.dma_start(cc_in, ccs[0:1, 0:2]).then_inc(dCC.h, 16); dCC.n += 16
                nc.gpsimd.wait_ge(dCC.h, dCC.n)
                ins = nc.gpsimd.collective_compute(
                    "AllReduce", ALU.add,
                    replica_groups=[[0, 1], [2, 3], [4, 5], [6, 7]],
                    ins=[cc_in], outs=[cc_out],
                )
                gCC.inc(ins)
                nc.sync.wait_ge(gCC.h, gCC.n)
                nc.sync.dma_start(ccs[0:1, 0:2], cc_out).then_inc(dCC.h, 16); dCC.n += 16
                nc.vector.wait_ge(dCC.h, dCC.n)
                vx("tensor_scalar_mul", ccs[0:1, 0:1], ccs[0:1, 0:1], 1.0 / NB_ELEM)
                vx("tensor_scalar_mul", ccs[0:1, 1:2], ccs[0:1, 1:2], 1.0 / NB_ELEM)
                vx("tensor_mul", ccs[0:1, 2:3], ccs[0:1, 0:1], ccs[0:1, 0:1])
                vx("tensor_sub", ccs[0:1, 1:2], ccs[0:1, 1:2], ccs[0:1, 2:3])
                ins = vx("tensor_scalar_add", ccs[0:1, 1:2], ccs[0:1, 1:2], EPS)
                vMSC.inc(ins)
                nc.scalar.wait_ge(vMSC.h, vMSC.n)
                ins = nc.scalar.activation(ccs[0:1, 2:3], ccs[0:1, 1:2], AF.Sqrt)
                sMSC.inc(ins)
                nc.vector.wait_ge(sMSC.h, sMSC.n)
                ins = vx("reciprocal", ccs[0:1, 3:4], ccs[0:1, 2:3])
                vMSC.inc(ins)
                nc.tensor.wait_ge(vMSC.h, vMSC.n)
                ins = nc.tensor.matmul(pb2[:, 3:4], ones_r[:, :], ccs[0:1, 3:4],
                                       start=True, stop=True)
                tMSC.inc(ins)
                nc.vector.wait_ge(tMSC.h, tMSC.n)
                ins = vx("tensor_copy", SCAL[:, 6:7], pb2[:, 3:4])
                vMSC.inc(ins)
            phase2_end = vMSC.n

        if _PH & 8:
            with ExitStack() as ph3:
                xz2 = ph3.enter_context(nc.sbuf_tensor("xz2", [128, 4, D], F32))
                zt = ph3.enter_context(nc.sbuf_tensor("zt", [128, 4, D], F32))
                tmpf = ph3.enter_context(nc.sbuf_tensor("tmpf", [128, 2048], F32))
                tmpb = ph3.enter_context(nc.sbuf_tensor("tmpb", [128, INNER], BF16))
                xqT = ph3.enter_context(nc.sbuf_tensor("xqT", [128, 8, 512], BF16))
                hsb = ph3.enter_context(nc.sbuf_tensor("hsb", [128, 4, INNER], F32))
                hqT = ph3.enter_context(nc.sbuf_tensor("hqT", [128, 32, 512], BF16))
                wgu0 = ph3.enter_context(nc.sbuf_tensor("wgu0", [128, 16, 512], BF16))
                wgu1 = ph3.enter_context(nc.sbuf_tensor("wgu1", [128, 16, 512], BF16))
                wd0 = ph3.enter_context(nc.sbuf_tensor("wd0", [128, 4, 1024], BF16))
                wd1 = ph3.enter_context(nc.sbuf_tensor("wd1", [128, 4, 1024], BF16))
                gs = ph3.enter_context(nc.sbuf_tensor("gs", [128, 512], F32))
                st8 = ph3.enter_context(nc.sbuf_tensor("st8", [128, 16], F32))
                pg8 = ph3.enter_context(nc.sbuf_tensor("pg8", [128, 12], F32))
                ah4 = ph3.enter_context(nc.sbuf_tensor("ah4", [128, 4], F32))
                ln6 = ph3.enter_context(nc.sbuf_tensor("ln6", [128, 12], F32))
                ln8 = ph3.enter_context(nc.sbuf_tensor("ln8", [128, 8], F32))
                lnS = ph3.enter_context(nc.sbuf_tensor("lnS", [128, 8], F32))
                PA = ph3.enter_context(nc.psum_tensor("PA", [128, 1024], F32))
                PB = ph3.enter_context(nc.psum_tensor("PB", [128, 1024], F32))
                PC = ph3.enter_context(nc.psum_tensor("PC", [128, 1024], F32))
                ptb = ph3.enter_context(nc.psum_tensor("ptb", [128, 1024], BF16))

                wgus = [wgu0, wgu1]
                wds = [wd0, wd1]
                gu_slab_mark = [0, 0]
                dwn_slab_mark = [0, 0]
                z2ev_mark = [0]
                PAB = [PA, PB]
                pab_sil = [0, 0]
                pab_hst = [0, 0]
                pab_z2 = 0

                xqt_mark = {}
                scg_mark = {}
                pkq = []

                def issue_zt(tau_, pairg, mark=None):
                    r0 = 512 * tau_ + 256 * pairg
                    ga = 2 * pairg
                    dyp = dY if pairg == 0 else dYb
                    nc.gpsimd.wait_ge(vZ2.h, vZ2.n if mark is None else mark)
                    nc.gpsimd.dma_start(
                        zt[:, ga:ga + 2, :],
                        y_d[r0:r0 + 256, :].rearrange("(g p) d -> p g d", p=128)
                    ).then_inc(dyp.h, 16); dyp.n += 16

                def issue_xz2(tau_, pairg):
                    r0 = 512 * tau_ + 256 * pairg
                    ga = 2 * pairg
                    dxp = dXN if pairg == 0 else dXNb
                    nc.gpsimd.wait_ge(dOUT.h, dOUT.n)
                    nc.gpsimd.dma_start(
                        xz2[:, ga:ga + 2, :],
                        xN[r0:r0 + 256, :].rearrange("(g p) d -> p g d", p=128)
                    ).then_inc(dxp.h, 16); dxp.n += 16

                def issue_loads(tau_, pairg):
                    issue_zt(tau_, pairg)
                    issue_xz2(tau_, pairg)

                def mk_ln1_closures(pairg):
                    ga, gb = 2 * pairg, 2 * pairg + 1
                    (dyp, dxp) = (dY, dXN) if pairg == 0 else (dYb, dXNb)
                    def c1():
                        nc.vector.wait_ge(dyp.h, dyp.n)
                        nc.vector.wait_ge(dxp.h, dxp.n)
                        for g in (ga, gb):
                            vx("scalar_tensor_tensor", zt[:, g, :], zt[:, g, :],
                               SCAL[:, 6:7], xz2[:, g, :], ALU.mult, ALU.add)
                        for g in (ga, gb):
                            vx("bn_stats", ln6[:, 0:6], zt[:, g, 0:512])
                            vx("bn_stats", ln6[:, 6:12], zt[:, g, 512:1024])
                            vx("bn_aggr", ln8[:, 2 * g:2 * g + 2], ln6[:, :])
                    def c2():
                        for i, g in ((0, ga), (1, gb)):
                            vx("tensor_reduce", st8[:, 0 + i:1 + i], zt[:, g, :], AX.X, ALU.max)
                            vx("tensor_reduce", st8[:, 2 + i:3 + i], zt[:, g, :], AX.X, ALU.min)
                        lpr = ln8[:, 2 * ga:2 * ga + 4].rearrange(
                            "p (g two) -> p two g", two=2)
                        vx("tensor_copy", pg8[:, ga:ga + 2], lpr[:, 0, :])
                        ins = vx("tensor_scalar_add", st8[:, 4:6], lpr[:, 1, :], EPS)
                        vMSC.inc(ins)
                        nc.scalar.wait_ge(vMSC.h, vMSC.n)
                        ins = nc.scalar.activation(pg8[:, 4 + ga:4 + ga + 2], st8[:, 4:6],
                                                   AF.Sqrt)
                        sMSC.inc(ins)
                    def c3():
                        nc.vector.wait_ge(sMSC.h, sMSC.n)
                        vx("reciprocal", pg8[:, 4 + ga:4 + ga + 2], pg8[:, 4 + ga:4 + ga + 2])
                        vx("tensor_sub", st8[:, 0:2], st8[:, 0:2], pg8[:, ga:ga + 2])
                        vx("tensor_sub", st8[:, 2:4], pg8[:, ga:ga + 2], st8[:, 2:4])
                        vx("tensor_max", st8[:, 0:2], st8[:, 0:2], st8[:, 2:4])
                        vx("tensor_mul", st8[:, 0:2], st8[:, 0:2], pg8[:, 4 + ga:4 + ga + 2])
                        vx("tensor_scalar_max", st8[:, 0:2], st8[:, 0:2], 1e-5)
                        vx("reciprocal", st8[:, 2:4], st8[:, 0:2])
                        vx("tensor_scalar_mul", st8[:, 2:4], st8[:, 2:4], 127.0)
                        vx("tensor_scalar_mul", st8[:, 4:6], st8[:, 0:2], 1.0 / 127.0)
                        for i, g in enumerate((ga, gb)):
                            vx("tensor_mul", st8[:, 8 + g:9 + g], st8[:, 4 + i:5 + i],
                               SCAL[:, 3:4])
                            vx("tensor_mul", st8[:, 12 + g:13 + g], st8[:, 4 + i:5 + i],
                               SCAL[:, 4:5])
                        scg_mark[ga] = scg_mark[gb] = vORD.n
                        vx("tensor_mul", st8[:, 0:2], pg8[:, 4 + ga:4 + ga + 2], st8[:, 2:4])
                        vx("tensor_mul", st8[:, 2:4], pg8[:, ga:ga + 2], st8[:, 0:2])
                        vx("tensor_scalar", st8[:, 2:4], st8[:, 2:4], -1.0, MAGIC,
                           ALU.mult, ALU.add)
                    def mk_quant(i, g):
                        def cq():
                            vx("tensor_scalar", tmpf[:, 0:D], zt[:, g, :],
                               st8[:, 0 + i:1 + i], st8[:, 2 + i:3 + i], ALU.mult, ALU.add)
                            vx("tensor_scalar", tmpf[:, 0:D], tmpf[:, 0:D], -MAGIC, 127.0,
                                                    ALU.add, ALU.min)
                            ins = vx("tensor_scalar", tmpb[:, 0:D], tmpf[:, 0:D],
                                     -128.0, None, ALU.max)
                            vQNT.inc(ins)
                        return cq
                    def mk_tr(g):
                        def ct():
                            nc.tensor.wait_ge(vQNT.h, vQNT.n)
                            if tTRX.n > 0:
                                nc.tensor.wait_ge(vTEV.h, vTEV.n)
                            ins = None
                            for jj in range(8):
                                ins = nc.tensor.transpose(ptb[:, jj * 128:(jj + 1) * 128],
                                                          tmpb[:, jj * 128:(jj + 1) * 128],
                                                          identb[:, :])
                            tTRX.inc(ins)
                            nc.vector.wait_ge(tTRX.h, tTRX.n)
                            nc.vector.wait_ge(tGU.h, tGU.n)
                            ins = vx("tensor_copy",
                                xqT[:, :, g * 128:(g + 1) * 128],
                                ptb[:, :].rearrange("p (j t) -> p j t", j=8))
                            vTEV.inc(ins)
                            xqt_mark[g] = vTEV.n
                        return ct
                    return [c1, c2, c3, mk_quant(0, ga), mk_tr(ga),
                            mk_quant(1, gb), mk_tr(gb)]

                def ln2_pair(tau_, p):
                    t0_ = 512 * tau_
                    ga, gb = 2 * p, 2 * p + 1
                    for i, g in ((0, ga), (1, gb)):
                        vx("bn_stats", ln6[:, 0:6], xz2[:, g, 0:512])
                        vx("bn_stats", ln6[:, 6:12], xz2[:, g, 512:1024])
                        vx("bn_aggr", lnS[:, 2 * i:2 * i + 2], ln6[:, :])
                    lp = lnS[:, 0:4].rearrange("p (g two) -> p two g", two=2)
                    ins = vx("tensor_scalar_add", lnS[:, 4:6], lp[:, 1, :], EPS)
                    vMSC.inc(ins)
                    nc.scalar.wait_ge(vMSC.h, vMSC.n)
                    ins = nc.scalar.activation(lnS[:, 6:8], lnS[:, 4:6], AF.Sqrt)
                    sMSC.inc(ins)
                    nc.vector.wait_ge(sMSC.h, sMSC.n)
                    vx("reciprocal", lnS[:, 6:8], lnS[:, 6:8])
                    for i, g in ((0, ga), (1, gb)):
                        ins = vx("tensor_scalar", xz2[:, g, :], xz2[:, g, :],
                                 lnS[:, 2 * i:2 * i + 1], lnS[:, 6 + i:7 + i],
                                 ALU.subtract, ALU.mult)
                        vZ2.inc(ins)
                        nc.sync.wait_ge(vZ2.h, vZ2.n)
                        nc.sync.dma_start(out[t0_ + g * 128:t0_ + (g + 1) * 128, :],
                                          xz2[:, g, :]).then_inc(dOUT.h, 16); dOUT.n += 16

                for tau in range(4):
                    t0 = 512 * tau
                    if tau == 0:
                        nc.sync.wait_ge(tPRJ.h, attn_T_prj)
                        nc.sync.wait_ge(tM1.h, attn_T_m1)
                        nc.sync.wait_ge(tM2.h, attn_T_m2)
                        nc.sync.wait_ge(tYTR.h, attn_T_ytr)
                        nc.sync.wait_ge(tVTR.h, attn_T_vtr)
                        nc.sync.wait_ge(vORD.h, attn_V_end)
                        nc.sync.wait_ge(dWQS[0].h, dWQS[0].n)
                        nc.sync.wait_ge(dWQS[1].h, dWQS[1].n)
                        nc.sync.wait_ge(dSPL[0].h, dSPL[0].n)
                        nc.sync.wait_ge(dSPL[1].h, dSPL[1].n)
                        nc.gpsimd.wait_ge(tPRJ.h, attn_T_prj)
                        nc.gpsimd.wait_ge(tM1.h, attn_T_m1)
                        nc.gpsimd.wait_ge(tM2.h, attn_T_m2)
                        nc.gpsimd.wait_ge(tYTR.h, attn_T_ytr)
                        nc.gpsimd.wait_ge(tVTR.h, attn_T_vtr)
                        nc.gpsimd.wait_ge(vORD.h, attn_V_end)
                        nc.gpsimd.wait_ge(dWQS[0].h, dWQS[0].n)
                        nc.gpsimd.wait_ge(dWQS[1].h, dWQS[1].n)
                        nc.gpsimd.wait_ge(dSPL[0].h, dSPL[0].n)
                        nc.gpsimd.wait_ge(dSPL[1].h, dSPL[1].n)
                        issue_loads(0, 0)
                        issue_loads(0, 1)
                        for st_ in mk_ln1_closures(0):
                            st_()
                        pkq = mk_ln1_closures(1)
                    else:
                        issue_xz2(tau, 1)
                        pkq = mk_ln1_closures(1)

                    def gu_slab(dst_par, ibig2, eng=None):
                        eng = eng or nc.sync
                        eng.wait_ge(tGU.h, gu_slab_mark[dst_par])
                        eng.wait_ge(dGU[dst_par].h, dGU[dst_par].n)
                        eng.dma_start(wgus[dst_par][:, :, :],
                                      wguq_d[:, :, ibig2 * 512:(ibig2 + 1) * 512]
                                      .rearrange("j p i -> p j i")
                                      ).then_inc(dGU[dst_par].h, 16)
                        dGU[dst_par].n += 16

                    def wd_slab(dst_par, sl_):
                        eng = nc.sync
                        eng.wait_ge(tDWN.h, dwn_slab_mark[dst_par])
                        eng.wait_ge(dWD[dst_par].h, dWD[dst_par].n)
                        eng.dma_start(
                            wds[dst_par][:, :, :],
                            wdq_d[4 * sl_:4 * (sl_ + 1), :, :]
                            .rearrange("i p d -> p i d")).then_inc(dWD[dst_par].h, 16)
                        dWD[dst_par].n += 16

                    def mk_hq_steps(g):
                        def s_scale(g=g):
                            vx("tensor_scalar_max", st8[:, 0:1], ah4[:, g:g + 1], 1e-5)
                            vx("reciprocal", st8[:, 1:2], st8[:, 0:1])
                            vx("tensor_scalar_mul", st8[:, 2:3], st8[:, 0:1], 1.0 / 127.0)
                            vx("tensor_mul", pg8[:, 8 + g:9 + g], st8[:, 2:3], SCAL[:, 5:6])
                            vx("tensor_scalar_mul", ah4[:, g:g + 1], st8[:, 1:2], 127.0)
                        yield s_scale
                        for hh in range(2):
                            def s_ts(g=g, hh=hh):
                                s0_, s1_ = hh * 2048, (hh + 1) * 2048
                                vx("tensor_scalar", tmpf[:, :], hsb[:, g, s0_:s1_],
                                   ah4[:, g:g + 1], MAGIC, ALU.mult, ALU.add)
                                vx("tensor_scalar", tmpf[:, :], tmpf[:, :], -MAGIC, 127.0,
                                   ALU.add, ALU.min)
                                ins = vx("tensor_scalar", tmpb[:, s0_:s1_], tmpf[:, :],
                                         -128.0, None, ALU.max)
                                vQNT.inc(ins)
                            yield s_ts
                        for grp in range(4):
                            def s_tr(g=g, grp=grp):
                                nc.tensor.wait_ge(vQNT.h, vQNT.n)
                                nc.tensor.wait_ge(vTEV.h, vTEV.n)
                                ins = None
                                for jj in range(8):
                                    i2 = grp * 8 + jj
                                    ins = nc.tensor.transpose(ptb[:, jj * 128:(jj + 1) * 128],
                                                              tmpb[:, i2 * 128:(i2 + 1) * 128],
                                                              identb[:, :])
                                tTRX.inc(ins)
                                nc.vector.wait_ge(tTRX.h, tTRX.n)
                                ins = vx("tensor_copy",
                                         hqT[:, grp * 8:(grp + 1) * 8, g * 128:(g + 1) * 128],
                                         ptb[:, :].rearrange("p (j t) -> p j t", j=8))
                                vTEV.inc(ins)
                            yield s_tr

                    def gu_pass(half):
                        for ibig in range(8):
                            par = ibig % 2
                            if 1 <= ibig and ibig + 1 < 8:
                                gu_slab((ibig + 1) % 2, ibig + 1)
                            sl = wgus[par]
                            nc.tensor.wait_ge(dGU[par].h, dGU[par].n)
                            for g in (2 * half, 2 * half + 1):
                                pt = PAB[g % 2]
                                if pab_sil[g % 2] > 0:
                                    nc.tensor.wait_ge(sSIL.h, pab_sil[g % 2])
                                if pab_hst[g % 2] > 0:
                                    nc.tensor.wait_ge(vHST.h, pab_hst[g % 2])
                                if ibig == 0:
                                    nc.tensor.wait_ge(vTEV.h, xqt_mark[g])
                                    if g == 2 * half:
                                        nc.tensor.wait_ge(vZ2.h, z2ev_mark[0])
                                ins = None
                                for j in range(8):
                                    nc.tensor.matmul(pt[:, 0:512],
                                                     xqT[:, j, g * 128:(g + 1) * 128],
                                                     sl[:, j, :],
                                                     start=(j == 0), stop=(j == 7))
                                    ins = nc.tensor.matmul(pt[:, 512:1024],
                                                           xqT[:, j, g * 128:(g + 1) * 128],
                                                           sl[:, 8 + j, :],
                                                           start=(j == 0), stop=(j == 7))
                                tGU.inc(ins)
                                nc.scalar.wait_ge(tGU.h, tGU.n)
                                if ibig == 0:
                                    nc.scalar.wait_ge(vORD.h, scg_mark[g])
                                if sSIL.n > 0:
                                    nc.scalar.wait_ge(vHST.h, vHST.n)
                                ins = nc.scalar.activation(gs[:, 0:512], pt[:, 0:512], AF.Silu,
                                                           scale=st8[:, 8 + g:9 + g])
                                pab_sil[g % 2] = sSIL.inc(ins)
                                nc.vector.wait_ge(sSIL.h, sSIL.n)
                                vx("scalar_tensor_tensor",
                                   hsb[:, g, ibig * 512:(ibig + 1) * 512],
                                   pt[:, 512:1024], st8[:, 12 + g:13 + g], gs[:, 0:512],
                                   ALU.mult, ALU.mult)
                                vx("tensor_reduce", st8[:, 6:7],
                                   hsb[:, g, ibig * 512:(ibig + 1) * 512],
                                   AX.X, ALU.max, apply_absolute_value=True)
                                if ibig == 0:
                                    ins = vx("tensor_copy", ah4[:, g:g + 1], st8[:, 6:7])
                                else:
                                    ins = vx("tensor_max", ah4[:, g:g + 1], ah4[:, g:g + 1], st8[:, 6:7])
                                pab_hst[g % 2] = vHST.inc(ins)
                                if pkq:
                                    pkq.pop(0)()
                            gu_slab_mark[par] = tGU.n

                    def down_pass(half, hq_mark):
                        z2p = {2 * half: PC, 2 * half + 1: PA}
                        pc_free = z2ev_mark[0]
                        for i in range(32):
                            sl4 = i // 4
                            if i > 0 and i % 4 == 0 and sl4 + 1 < 8:
                                wd_slab((sl4 + 1) % 2, sl4 + 1)
                            if i % 4 == 0:
                                nc.tensor.wait_ge(dWD[sl4 % 2].h, dWD[sl4 % 2].n)
                            if i == 0:
                                nc.tensor.wait_ge(vTEV.h, hq_mark)
                                nc.tensor.wait_ge(vZ2.h, pc_free)
                                if half == 0:
                                    nc.tensor.wait_ge(sSIL.h, sSIL.n)
                                    nc.tensor.wait_ge(vHST.h, vHST.n)
                            ins = None
                            for g in (2 * half, 2 * half + 1):
                                pt = z2p[g]
                                for dh in range(2):
                                    ins = nc.tensor.matmul(pt[:, dh * 512:(dh + 1) * 512],
                                                           hqT[:, i, g * 128:(g + 1) * 128],
                                                           wds[(i // 4) % 2][:, i % 4, dh * 512:(dh + 1) * 512],
                                                           start=(i == 0), stop=(i == 31))
                            tDWN.inc(ins)
                            if i % 4 == 3:
                                dwn_slab_mark[sl4 % 2] = tDWN.n
                            if i % 2 == 0 and pkq:
                                pkq.pop(0)()
                        nc.vector.wait_ge(tDWN.h, tDWN.n)
                        ins = None
                        for g in (2 * half, 2 * half + 1):
                            pt = z2p[g]
                            vx("tensor_scalar", tmpf[:, 0:D], zt[:, g, :],
                               pg8[:, g:g + 1], pg8[:, 4 + g:5 + g],
                               ALU.subtract, ALU.mult)
                            ins = vx("scalar_tensor_tensor",
                                     xz2[:, g, :],
                                     pt[:, :], pg8[:, 8 + g:9 + g], tmpf[:, 0:D],
                                     ALU.mult, ALU.add)
                        vZ2.inc(ins)
                        z2ev_mark[0] = vZ2.n

                    if tau == 0:
                        gu_slab(0, 0)
                        gu_slab(1, 1)
                    gu_pass(0)
                    while pkq:
                        pkq.pop(0)()
                    pkq = [s for g_ in (0, 1) for s in mk_hq_steps(g_)]
                    gu_slab(0, 0)
                    gu_slab(1, 1)
                    gu_pass(1)
                    while pkq:
                        pkq.pop(0)()
                    hq01 = vTEV.n
                    pkq = [s for g_ in (2, 3) for s in mk_hq_steps(g_)]
                    wd_slab(0, 0)
                    wd_slab(1, 1)
                    down_pass(0, hq01)
                    while pkq:
                        pkq.pop(0)()
                    hq23 = vTEV.n
                    wd_slab(0, 0)
                    wd_slab(1, 1)
                    if tau + 1 < 4:
                        issue_zt(tau + 1, 0, mark=z2ev_mark[0])
                    ln2_pair(tau, 0)
                    if tau + 1 < 4:
                        issue_xz2(tau + 1, 0)
                        pkq = mk_ln1_closures(0)
                    down_pass(1, hq23)
                    while pkq:
                        pkq.pop(0)()
                    if tau + 1 < 4:
                        gu_slab(0, 0)
                        gu_slab(1, 1)
                        issue_zt(tau + 1, 1, mark=z2ev_mark[0])
                    ln2_pair(tau, 1)

        if not (_PH & 8):
            nc.sync.wait_ge(vORD.h, vORD.n)
            nc.sync.dma_start(out[0:128, 0:8], SCAL[:, :]).then_inc(dOUT.h, 16); dOUT.n += 16
        nc.sync.wait_ge(dOUT.h, dOUT.n)
    return nc


_NC_CACHE = None


def kernel(L0_attn_x, L0_attn_Wq, L0_attn_Wk, L0_attn_Wv,
           L0_ffn_w_gate, L0_ffn_w_up, L0_ffn_w_down):
    global _NC_CACHE
    x = np.asarray(L0_attn_x, dtype=np.float32).reshape(4 * 4096, 1024)
    wqT = np.ascontiguousarray(np.asarray(L0_attn_Wq, np.float32).T)
    wkT = np.ascontiguousarray(np.asarray(L0_attn_Wk, np.float32).T)
    wvT = np.ascontiguousarray(np.asarray(L0_attn_Wv, np.float32).T)
    wgT = np.ascontiguousarray(np.asarray(L0_ffn_w_gate, np.float32).T)
    wuT = np.ascontiguousarray(np.asarray(L0_ffn_w_up, np.float32).T)
    wdT = np.ascontiguousarray(np.asarray(L0_ffn_w_down, np.float32).T)

    idb = np.eye(128, dtype=np.float32).astype(BD)
    idf = np.eye(128, dtype=np.float32)
    onesr = np.ones((1, 128), np.float32)
    onesc = np.ones((128, 1), np.float32)

    if _NC_CACHE is None:
        _NC_CACHE = build_nc()
    nc = _NC_CACHE

    in_maps = []
    for c in range(NCORE):
        rows = x[c * T:(c + 1) * T]
        in_maps.append({
            "xT": np.ascontiguousarray(rows.T),
            "xN": np.ascontiguousarray(rows),
            "wqT": wqT, "wkT": wkT, "wvT": wvT,
            "wgT": wgT, "wuT": wuT, "wdT": wdT,
            "idb": idb, "idf": idf,
            "onesr": onesr, "onesc": onesc,
        })
    res = run_bass_kernel_spmd(nc, in_maps, core_ids=list(range(NCORE)))
    full = np.concatenate([res.results[c]["out"] for c in range(NCORE)], axis=0)
    return full.reshape(4, 4096, 1024)


# version 29
# speedup vs baseline: 1.0562x; 1.0528x over previous
import os
import numpy as np
import ml_dtypes
from contextlib import ExitStack

import concourse.bass as bass
import concourse.mybir as mybir
from concourse.bass_utils import run_bass_kernel_spmd

F32 = mybir.dt.float32
BF16 = mybir.dt.bfloat16
AF = mybir.ActivationFunctionType
ALU = mybir.AluOpType
AX = mybir.AxisListType

SHIFT = 44.0
MAGIC = 12582912.0
EPS = 1e-5
NCORE = 8
T = 2048
NPAIR = 4
NQG = 4
NKC = 32
D = 1024
INNER = 4096
NB_ELEM = 4096 * 1024

BD = ml_dtypes.bfloat16
_PH0 = 0
_PH = int(os.environ.get("BASSK_PH", "15"))
_P1L = int(os.environ.get("BASSK_P1L", "9"))


class Ctr:
    def __init__(self, stack, nc, name):
        self.h = stack.enter_context(nc.semaphore(name))
        self.n = 0

    def inc(self, instr, k=1):
        instr.then_inc(self.h, k)
        self.n += k
        return self.n


def build_nc():
    nc = bass.Bass("TRN2", target_bir_lowering=False)

    xT = nc.dram_tensor("xT", [D, T], F32, kind="ExternalInput").ap()
    xN = nc.dram_tensor("xN", [T, D], F32, kind="ExternalInput").ap()
    wqT = nc.dram_tensor("wqT", [D, D], F32, kind="ExternalInput").ap()
    wkT = nc.dram_tensor("wkT", [D, D], F32, kind="ExternalInput").ap()
    wvT = nc.dram_tensor("wvT", [D, D], F32, kind="ExternalInput").ap()
    wgT = nc.dram_tensor("wgT", [D, INNER], F32, kind="ExternalInput").ap()
    wuT = nc.dram_tensor("wuT", [D, INNER], F32, kind="ExternalInput").ap()
    wdT = nc.dram_tensor("wdT", [INNER, D], F32, kind="ExternalInput").ap()
    idb = nc.dram_tensor("idb", [128, 128], BF16, kind="ExternalInput").ap()
    idf = nc.dram_tensor("idf", [128, 128], F32, kind="ExternalInput").ap()
    onesr = nc.dram_tensor("onesr", [1, 128], F32, kind="ExternalInput").ap()
    onesc = nc.dram_tensor("onesc", [128, 1], F32, kind="ExternalInput").ap()
    out = nc.dram_tensor("out", [T, D], F32, kind="ExternalOutput").ap()

    y_d = nc.dram_tensor("y_d", [T, D], F32, kind="Internal").ap()
    wguq_d = nc.dram_tensor("wguq_d", [16, 128, INNER], BF16, kind="Internal").ap()
    wdq_d = nc.dram_tensor("wdq_d", [32, 128, D], BF16, kind="Internal").ap()
    cc_in = nc.dram_tensor("cc_in", [1, 2], F32, kind="Internal").ap()
    cc_out = nc.dram_tensor("cc_out", [1, 2], F32, kind="Internal").ap()

    st = ExitStack()
    with st:
        C = lambda name: Ctr(st, nc, name)
        dC = C("dC"); dWA = [C("dWA0"), C("dWA1"), C("dWA2")]; dXF = C("dXF"); dXT = C("dXT")
        dBQ = C("dBQ"); dBK = C("dBK"); dSPL = [C("dSPL0"), C("dSPL1")]
        dWP = [C("dWP0"), C("dWP1"), C("dWP2"), C("dWP3")]; dWQS = [C("dWQS0"), C("dWQS1")]; dCC = C("dCC")
        dY = C("dY"); dYb = C("dYb"); dXN = C("dXN"); dXNb = C("dXNb"); dGU = [C("dGU0"), C("dGU1")]; dWD = [C("dWD0"), C("dWD1")]; dOUT = C("dOUT")
        tM1 = C("tM1"); tM2 = C("tM2"); sEXP = C("sEXP")
        tYTR = C("tYTR"); tPRJ = C("tPRJ"); tVTR = C("tVTR")
        gCC = C("gCC"); sMSC = C("sMSC"); tMSC = C("tMSC")
        tGU = C("tGU"); sSIL = C("sSIL"); sOCP = C("sOCP")
        tTRX = C("tTRX"); tDWN = C("tDWN")
        vORD = C("vORD")
        gORD = C("gORD")

        class VMark:
            def __init__(self):
                self.h = vORD.h
                self.n = 0
            def inc(self, ins):
                self.n = vORD.n
                return self.n

        vOCP = VMark(); vYST = VMark(); vPEV = VMark(); vVEV = VMark()
        vSTA = VMark(); vMSC = VMark(); vHST = VMark(); vQNT = VMark()
        vTEV = VMark(); vZ2 = VMark()

        def gx(_m, *a, **k):
            if gORD.n:
                nc.gpsimd.wait_ge(gORD.h, gORD.n)
            ins = getattr(nc.gpsimd, _m)(*a, **k)
            gORD.inc(ins)
            return ins

        def vx(_m, *a, **k):
            if vORD.n:
                nc.vector.wait_ge(vORD.h, vORD.n)
            ins = getattr(nc.vector, _m)(*a, **k)
            vORD.inc(ins)
            return ins

        sb = lambda *a: st.enter_context(nc.sbuf_tensor(*a))
        identb = sb("identb", [128, 128], BF16)
        identf = sb("identf", [128, 128], F32)
        ones_r = sb("ones_r", [1, 128], F32)
        ones_c = sb("ones_c", [128, 1], F32)
        expb = sb("expb", [128, 1], F32)
        SCAL = sb("SCAL", [128, 8], F32)
        sacc = sb("sacc", [128, 1], F32)
        ccs = sb("ccs", [1, 4], F32)
        sqacc = sb("sqacc", [128, 1], F32)

        for (dst_t, src_t) in ((identb, idb), (identf, idf),
                               (ones_r, onesr), (ones_c, onesc)):
            nc.sync.dma_start(dst_t[:, :], src_t).then_inc(dC.h, 16); dC.n += 16
        vx("memset", expb[:, :], -SHIFT)
        vx("memset", sacc[:, :], 0.0)
        ins = vx("memset", sqacc[:, :], 0.0)
        vMSC.inc(ins)

        wprep_vqnt = 0
        wprep_dwqs = 0
        if _PH & 2:
            with ExitStack() as ph1:
                wq_sb = ph1.enter_context(nc.sbuf_tensor("wq_sb", [128, 8, D], BF16))
                wk_sb = ph1.enter_context(nc.sbuf_tensor("wk_sb", [128, 8, D], BF16))
                wv_sb = ph1.enter_context(nc.sbuf_tensor("wv_sb", [128, 8, D], BF16))
                xfm0 = ph1.enter_context(nc.sbuf_tensor("xfm0", [128, 8, 512], BF16))
                xtok = ph1.enter_context(nc.sbuf_tensor("xtok", [128, 4, D], F32))
                QzA0 = ph1.enter_context(nc.sbuf_tensor("QzA0", [128, 4096], BF16))
                QzA1 = ph1.enter_context(nc.sbuf_tensor("QzA1", [128, 4096], BF16))
                QzB0 = ph1.enter_context(nc.sbuf_tensor("QzB0", [128, 4096], BF16))
                QzB1 = ph1.enter_context(nc.sbuf_tensor("QzB1", [128, 4096], BF16))
                Kt0 = ph1.enter_context(nc.sbuf_tensor("Kt0", [128, 4096], BF16))
                Kt1 = ph1.enter_context(nc.sbuf_tensor("Kt1", [128, 4096], BF16))
                Vt0 = ph1.enter_context(nc.sbuf_tensor("Vt0", [128, 8, 512], BF16))
                bQK = ph1.enter_context(nc.sbuf_tensor("bQK", [128, 8, 512], BF16))
                Vp0 = ph1.enter_context(nc.sbuf_tensor("Vp0", [128, 32, 68], BF16))
                Vp1 = ph1.enter_context(nc.sbuf_tensor("Vp1", [128, 32, 68], BF16))
                P0 = ph1.enter_context(nc.sbuf_tensor("P0", [128, 1024], BF16))
                P1 = ph1.enter_context(nc.sbuf_tensor("P1", [128, 1024], BF16))
                P2 = ph1.enter_context(nc.sbuf_tensor("P2", [128, 1024], BF16))
                Osb = ph1.enter_context(nc.sbuf_tensor("Osb", [128, 1024], F32))
                yT0 = ph1.enter_context(nc.sbuf_tensor("yT0", [128, 4, D], F32))
                yT1 = ph1.enter_context(nc.sbuf_tensor("yT1", [128, 4, D], F32))
                rec = ph1.enter_context(nc.sbuf_tensor("rec", [128, 8], F32))
                wpb0 = ph1.enter_context(nc.sbuf_tensor("wpb0", [128, 512], F32))
                wpb1 = ph1.enter_context(nc.sbuf_tensor("wpb1", [128, 512], F32))
                wpb2 = ph1.enter_context(nc.sbuf_tensor("wpb2", [128, 512], F32))
                wpb3 = ph1.enter_context(nc.sbuf_tensor("wpb3", [128, 512], F32))
                wpq0 = ph1.enter_context(nc.sbuf_tensor("wpq0", [128, 512], BF16))
                wpq1 = ph1.enter_context(nc.sbuf_tensor("wpq1", [128, 512], BF16))
                wcol = ph1.enter_context(nc.sbuf_tensor("wcol", [128, 4], F32))
                wsc = ph1.enter_context(nc.sbuf_tensor("wsc", [1, 4], F32))
                S0 = ph1.enter_context(nc.psum_tensor("S0", [128, 1024], F32))
                S1 = ph1.enter_context(nc.psum_tensor("S1", [128, 1024], F32))
                Ops = ph1.enter_context(nc.psum_tensor("Ops", [128, 1024], F32))
                yTp = ph1.enter_context(nc.psum_tensor("yTp", [128, 512], F32))
                Vps = ph1.enter_context(nc.psum_tensor("Vps", [128, 1024], BF16))

                xfms = [xfm0, xfm0]
                QzAs = [QzA0, QzA1]; QzBs = [QzB0, QzB1]
                Kts = [Kt0, Kt1]; Vts = [Vt0, Vt0]
                Vps_ = [Vp0, Vp1]; Ps = [P0, P1, P2]; yTs = [yT0, yT1]
                Ss = [S0, S1]

                nc.gpsimd.wait_ge(vQNT.h, wprep_vqnt)

                nc.gpsimd.dma_start(wq_sb[:, :, :],
                                    wqT.rearrange("(j p) k -> p j k", p=128)).then_inc(dWA[0].h, 16); dWA[0].n += 16
                nc.gpsimd.dma_start(xfm0[:, :, :],
                                    xT.rearrange("(j p) t -> p j t", p=128)[:, :, 0:512]).then_inc(dXF.h, 16); dXF.n += 16
                for (wi, (w_sb, w_hbm)) in enumerate(((wk_sb, wkT), (wv_sb, wvT))):
                    nc.gpsimd.dma_start(w_sb[:, :, :],
                                        w_hbm.rearrange("(j p) k -> p j k", p=128)).then_inc(dWA[1 + wi].h, 16); dWA[1 + wi].n += 16
                vx("memset", Vp0[:, :, 64:65], 1.0)
                vx("memset", Vp1[:, :, 64:65], 1.0)
                vx("memset", QzA0[64:128, :], 0.0)
                vx("memset", QzA1[64:128, :], 0.0)
                vx("memset", QzB0[0:64, :], 0.0)
                ins = vx("memset", QzB1[0:64, :], 0.0)
                vMSC.inc(ins)

                nc.sync.wait_ge(vQNT.h, wprep_vqnt)

                nc.sync.dma_start(xtok[:, :, :],
                                  xN[0:512, :].rearrange("(g p) d -> p g d", p=128)).then_inc(dXT.h, 16); dXT.n += 16

                exp_cnt = 0
                m1_cnt = 0
                m2_cnt = 0
                slot_evt = [0] * 4
                exp_at_pair = []
                pending_stats = []
                m2_at_block = {}
                stt_at_pair = []
                spill_at_pair = []
                vev_at_block = {}

                def pslot(k):
                    t_ = Ss[(k // 2) % 2]
                    return t_, 512 * (k % 2)

                wbufs = [wpb0, wpb1, wpb2, wpb3]
                wspecs = [
                    ("g", wgT, wguq_d, 8, INNER, 0, 3),
                    ("u", wuT, wguq_d, 8, INNER, 1, 4),
                    ("d", wdT, wdq_d, 32, D, 2, 5),
                ]
                wspec_by = {sp[0]: sp for sp in wspecs}
                wsub = {}
                for (nm, srcw, dstw, nch, wdt, s_slot, m_slot) in wspecs:
                    subs = []
                    for c in range(nch):
                        for c0 in range(0, wdt, 512):
                            subs.append((c, c0, 512))
                    wsub[nm] = subs
                wstream = []
                for kind, nm in (("A", "g"), ("A", "u"), ("B", "g"),
                                 ("A", "d"), ("B", "u"), ("B", "d")):
                    for sub in wsub[nm]:
                        wstream.append((kind, nm, sub))
                segA_last = {}
                segA_first = {}
                segB_first = {}
                for k, (kind, nm, sub) in enumerate(wstream):
                    if kind == "A":
                        segA_last[nm] = k
                        segA_first.setdefault(nm, k)
                    else:
                        segB_first.setdefault(nm, k)
                fin_mark = {}
                wprocmark = {}
                wdma_issued = [False] * (len(wstream) + 1)
                wpqs = [wpq0, wpq1]
                wqs_mark = [0, 0]
                bq_idx = [0]

                def w_issue_dma(k):
                    if k >= len(wstream) or wdma_issued[k]:
                        return
                    wdma_issued[k] = True
                    (kind, nm, (c, c0, w_)) = wstream[k]
                    srcw = wspec_by[nm][1]
                    par = k % 4
                    dw = dWP[par]
                    if k >= 4:
                        (eng, val) = wprocmark[k - 4]
                        nc.gpsimd.wait_ge(vORD.h if eng == "V" else gORD.h, val)
                    nc.gpsimd.dma_start(wbufs[par][:, 0:w_],
                                        srcw[c * 128:(c + 1) * 128, c0:c0 + w_]).then_inc(dw.h, 16)
                    dw.n += 16

                def w_step(k):
                    (kind, nm, (c, c0, w_)) = wstream[k]
                    (nm2, srcw, dstw, nch, wdt, s_slot, m_slot) = wspec_by[nm]
                    par = k % 4
                    dw = dWP[par]
                    b = wbufs[par]
                    need = 16 * (k // 4 + 1)
                    if kind == "A":
                        ac = {"g": 0, "u": 1, "d": 2}[nm]
                        nc.vector.wait_ge(dw.h, need)
                        if k == segA_first[nm]:
                            vx("memset", wcol[:, ac:ac + 1], 0.0)
                        vx("tensor_reduce", wcol[:, 3:4], b[:, 0:w_], AX.X, ALU.add,
                           apply_absolute_value=True)
                        ins = vx("tensor_add", wcol[:, ac:ac + 1], wcol[:, ac:ac + 1],
                                 wcol[:, 3:4])
                        wprocmark[k] = ("V", vORD.n)
                    else:
                        nc.vector.wait_ge(dw.h, need)
                        if k == segB_first[nm]:
                            nc.vector.wait_ge(vMSC.h, fin_mark[nm])
                        vx("tensor_scalar", b[:, 0:w_], b[:, 0:w_],
                           SCAL[:, s_slot:s_slot + 1], MAGIC, ALU.mult, ALU.add)
                        vx("tensor_scalar", b[:, 0:w_], b[:, 0:w_], -MAGIC, 1.0,
                           ALU.add, ALU.min)
                        bp = bq_idx[0] % 2
                        bq_idx[0] += 1
                        wpq = wpqs[bp]
                        dq = dWQS[bp]
                        if wqs_mark[bp] > 0:
                            nc.vector.wait_ge(dq.h, wqs_mark[bp])
                        ins = vx("tensor_scalar", wpq[:, 0:w_], b[:, 0:w_], -1.0, None, ALU.max)
                        nc.sync.wait_ge(vORD.h, vORD.n)
                        row_off = 8 if nm == "u" else 0
                        nc.sync.dma_start(dstw[row_off + c, :, c0:c0 + w_],
                                            wpq[:, 0:w_]).then_inc(dq.h, 16)
                        dq.n += 16
                        wqs_mark[bp] = dq.n
                        wprocmark[k] = ("V", vORD.n)
                    w_issue_dma(k + 1)
                    w_issue_dma(k + 2)

                def w_finalize(nm, ytp_guard):
                    (nm2, srcw, dstw, nch, wdt, s_slot, m_slot) = wspec_by[nm]
                    nel = float(nch * 128 * wdt)
                    (eng, val) = wprocmark[segA_last[nm]]
                    nc.tensor.wait_ge(vORD.h, val)
                    if ytp_guard > 0:
                        nc.tensor.wait_ge(vYST.h, ytp_guard)
                    ac = {"g": 0, "u": 1, "d": 2}[nm]
                    ins = nc.tensor.matmul(yTp[0:1, 0:1], wcol[:, ac:ac + 1], ones_c[:, :],
                                           start=True, stop=True)
                    tMSC.inc(ins)
                    nc.vector.wait_ge(tMSC.h, tMSC.n)
                    vx("tensor_scalar", wsc[0:1, 0:1], yTp[0:1, 0:1],
                       1.0 / nel, 1e-5, ALU.mult, ALU.max)
                    ins = vx("reciprocal", wsc[0:1, 1:2], wsc[0:1, 0:1])
                    vMSC.inc(ins)
                    nc.tensor.wait_ge(vMSC.h, vMSC.n)
                    nc.tensor.matmul(yTp[:, 1:2], ones_r[:, :], wsc[0:1, 0:1],
                                     start=True, stop=True)
                    ins = nc.tensor.matmul(yTp[:, 2:3], ones_r[:, :], wsc[0:1, 1:2],
                                           start=True, stop=True)
                    tMSC.inc(ins)
                    nc.vector.wait_ge(tMSC.h, tMSC.n)
                    vx("tensor_copy", SCAL[:, m_slot:m_slot + 1], yTp[:, 1:2])
                    ins = vx("tensor_copy", SCAL[:, s_slot:s_slot + 1], yTp[:, 2:3])
                    vMSC.inc(ins)
                    fin_mark[nm] = vMSC.n

                wsched = {0: list(range(0, 96)), 1: list(range(96, 192)),
                          2: list(range(192, 304)), 3: list(range(304, 384))}
                wfin = {1: "g", 2: "u", 3: "d"}
                if not (_PH & 1):
                    wsched = {0: [], 1: [], 2: [], 3: []}
                    wfin = {}
                else:
                    w_issue_dma(0)
                    w_issue_dma(1)
                    w_issue_dma(2)

                deferred_evac = None
                stats_osb_mark = [0]
                for beta in range(NPAIR):
                    xf = xfms[beta % 2]
                    QzA, QzB = QzAs[beta % 2], QzBs[beta % 2]
                    Kt, Vt = Kts[beta % 2], Vts[beta % 2]
                    yT = yTs[beta % 2]

                    if beta in wfin:
                        w_finalize(wfin[beta], stt_at_pair[beta - 1])
                    wq_todo = [(lambda k=k: w_step(k)) for k in wsched.get(beta, [])]
                    wq_point = [0]
                    while pending_stats:
                        wq_todo.insert(0, pending_stats.pop())
                    _npts = 52 if beta == NPAIR - 1 else 64
                    _nw = len(wq_todo)
                    wq_counts = [_nw * (i + 1) // _npts - _nw * i // _npts
                                 for i in range(_npts)] + [0] * 16
                    def wq_drain():
                        take = wq_counts[wq_point[0]]
                        wq_point[0] += 1
                        for _ in range(take):
                            if wq_todo:
                                wq_todo.pop(0)()

                    nc.tensor.wait_ge(dXF.h, 16 * (beta + 1))
                    if beta == 0:
                        nc.tensor.wait_ge(vMSC.h, vMSC.n)
                        nc.tensor.wait_ge(dC.h, dC.n)
                    if beta >= 1:
                        nc.vector.wait_ge(dBQ.h, dBQ.n)
                        nc.vector.wait_ge(dBK.h, dBK.n)
                        if beta >= 2:
                            nc.vector.wait_ge(sEXP.h, exp_at_pair[beta - 2])
                    kslot = 0
                    for (wsb, which) in ((wq_sb, "q"), (wk_sb, "k"), (wv_sb, "v")):
                        if beta == 0:
                            nc.tensor.wait_ge(dWA[{"q": 0, "k": 1, "v": 2}[which]].h, 16)
                        for i in range(8):
                            pt, off = pslot(kslot)
                            if beta > 0 and kslot < 4:
                                nc.tensor.wait_ge(sEXP.h, exp_at_pair[beta - 1])
                            if slot_evt[kslot % 4] > 0:
                                nc.tensor.wait_ge(vPEV.h, slot_evt[kslot % 4])
                            ins = None
                            for j in range(8):
                                ins = nc.tensor.matmul(pt[:, off:off + 512],
                                                       wsb[:, j, i * 128:(i + 1) * 128],
                                                       xf[:, j, :],
                                                       start=(j == 0), stop=(j == 7))
                            tPRJ.inc(ins)
                            nc.vector.wait_ge(tPRJ.h, tPRJ.n)
                            if which == "v":
                                ins = vx("tensor_copy", Vt[:, i, :], pt[:, off:off + 512])
                            elif which == "k":
                                vx("tensor_copy", Kt[0:64, (2 * i) * 256:(2 * i) * 256 + 256],
                                   pt[0:64, off:off + 256])
                                vx("tensor_copy", Kt[64:128, (2 * i + 1) * 256:(2 * i + 1) * 256 + 256],
                                   pt[64:128, off + 256:off + 512])
                                vx("tensor_copy", bQK[0:64, i, 256:512], pt[0:64, off + 256:off + 512])
                                ins = vx("tensor_copy", bQK[64:128, i, 256:512], pt[64:128, off:off + 256])
                            else:
                                vx("tensor_copy", QzA[0:64, (2 * i) * 256:(2 * i) * 256 + 256],
                                   pt[0:64, off:off + 256])
                                vx("tensor_copy", QzB[64:128, (2 * i + 1) * 256:(2 * i + 1) * 256 + 256],
                                   pt[64:128, off + 256:off + 512])
                                vx("tensor_copy", bQK[0:64, i, 0:256], pt[0:64, off + 256:off + 512])
                                ins = vx("tensor_copy", bQK[64:128, i, 0:256], pt[64:128, off:off + 256])
                            slot_evt[kslot % 4] = vPEV.inc(ins)
                            kslot += 1
                    nc.sync.wait_ge(vPEV.h, vPEV.n)
                    dst_e = Kt[64:128, :].rearrange("p (i q s) -> p i q s", q=2, s=256)[:, :, 0, :]
                    nc.sync.dma_start(dst_e, bQK[0:64, :, 256:512]).then_inc(dBK.h, 16); dBK.n += 16
                    dst_o = Kt[0:64, :].rearrange("p (i q s) -> p i q s", q=2, s=256)[:, :, 1, :]
                    nc.sync.dma_start(dst_o, bQK[64:128, :, 256:512]).then_inc(dBK.h, 16); dBK.n += 16
                    dst_e = QzB[64:128, :].rearrange("p (i q s) -> p i q s", q=2, s=256)[:, :, 0, :]
                    nc.sync.dma_start(dst_e, bQK[0:64, :, 0:256]).then_inc(dBQ.h, 16); dBQ.n += 16
                    dst_o = QzA[0:64, :].rearrange("p (i q s) -> p i q s", q=2, s=256)[:, :, 1, :]
                    nc.sync.dma_start(dst_o, bQK[64:128, :, 0:256]).then_inc(dBQ.h, 16); dBQ.n += 16

                    nc.tensor.wait_ge(vPEV.h, vPEV.n)
                    for h in (range(2) if _P1L >= 2 else []):
                        Vp = Vps_[h]
                        prev_m2 = m2_at_block.get((beta - 1, h), 0)
                        for batch in range(2):
                            nc.tensor.wait_ge(vVEV.h, vVEV.n)
                            ins = None
                            for s in range(8):
                                i = batch * 4 + s // 2
                                kh = s % 2
                                ins = nc.tensor.transpose(
                                    Vps[:, s * 128:(s + 1) * 128],
                                    Vt[:, i, h * 256 + kh * 128:h * 256 + (kh + 1) * 128],
                                    identb[:, :])
                            tVTR.inc(ins)
                            nc.vector.wait_ge(tVTR.h, tVTR.n)
                            if prev_m2 > 0 and batch == 0:
                                nc.vector.wait_ge(tM2.h, prev_m2)
                            srcv = Vps[:, :].rearrange("p (a b c) -> p a b c", b=2, c=128)
                            dstv = Vp[:, 16 * batch:16 * batch + 16, 0:64].rearrange(
                                "p (a r) c -> p a r c", r=4)
                            vx("tensor_copy", dstv[:, :, 0:2, :], srcv[:, :, :, 0:64])
                            ins = vx("tensor_copy", dstv[:, :, 2:4, :], srcv[:, :, :, 64:128])
                            vVEV.inc(ins)
                        vev_at_block[(beta, h)] = vVEV.n

                    def mk_evac(beta, h, qg, yT):
                        def run():
                            for w in range(2):
                                nc.tensor.wait_ge(sOCP.h, sOCP.n)
                                if tYTR.n >= 1:
                                    nc.tensor.wait_ge(vYST.h, vYST.n)
                                ins = None
                                for pp in range(4):
                                    p = w * 4 + pp
                                    ins = nc.tensor.transpose(yTp[:, pp * 128:pp * 128 + 65],
                                                              Osb[0:65, p * 128:(p + 1) * 128],
                                                              identf[0:65, 0:65])
                                tYTR.inc(ins)
                                nc.vector.wait_ge(tYTR.h, tYTR.n)
                                if beta == 0 and h == 0 and qg == 0 and w == 0:
                                    nc.vector.wait_ge(dXT.h, dXT.n)
                                if beta >= 1 and h == 0 and qg == 0 and w == 0:
                                    if beta >= 2:
                                        pp_, vv_ = spill_at_pair[beta - 2]
                                        nc.vector.wait_ge(dSPL[pp_].h, vv_)
                                    nc.vector.wait_ge(dXT.h, 16 * (beta + 1))
                                ins = None
                                for pp in range(4):
                                    p = w * 4 + pp
                                    qk = 4 * qg + p // 2
                                    g = 2 * h + (p % 2)
                                    c0 = pp * 128
                                    vx("reciprocal", rec[:, p:p + 1], yTp[:, c0 + 64:c0 + 65])
                                    ins = vx("scalar_tensor_tensor",
                                        yT[:, g, qk * 64:(qk + 1) * 64],
                                        yTp[:, c0:c0 + 64], rec[:, p:p + 1],
                                        xtok[:, g, qk * 64:(qk + 1) * 64],
                                        ALU.mult, ALU.add)
                                vYST.inc(ins)
                            if beta == NPAIR - 1 and qg == NQG - 1:
                                mk_stats_g(beta, yT, 2 * h)()
                                mk_stats_g(beta, yT, 2 * h + 1)()
                        return run
                    for h in (range(2) if _P1L >= 3 else []):
                        o = h * 64
                        Vp = Vps_[h]
                        nc.tensor.wait_ge(vVEV.h, vev_at_block[(beta, h)])
                        if h == 0:
                            nc.tensor.wait_ge(dBQ.h, dBQ.n)
                            nc.tensor.wait_ge(dBK.h, dBK.n)
                        Qz = QzA if h == 0 else QzB
                        for qg in range(NQG):
                            LAG = 2
                            e_base = exp_cnt
                            for kc in range(NKC + LAG):
                                if kc == 1 and deferred_evac is not None:
                                    deferred_evac()
                                    deferred_evac = None
                                if kc < NKC:
                                    e = exp_cnt + 1
                                    si = (e - 1) % 2
                                    Sb = Ss[si]
                                    if e > 2:
                                        nc.tensor.wait_ge(sEXP.h, e - 2)
                                    nc.tensor.matmul(Sb[:, 0:512],
                                                     Kt[:, kc * 128:(kc + 1) * 128],
                                                     Qz[:, qg * 1024:qg * 1024 + 512],
                                                     start=True, stop=True)
                                    ins = nc.tensor.matmul(Sb[:, 512:1024],
                                                           Kt[:, kc * 128:(kc + 1) * 128],
                                                           Qz[:, qg * 1024 + 512:(qg + 1) * 1024],
                                                           start=True, stop=True)
                                    m1_cnt = tM1.inc(ins)
                                    nc.scalar.wait_ge(tM1.h, m1_cnt)
                                    if e > 3:
                                        nc.scalar.wait_ge(tM2.h, e - 3)
                                    if e == 1:
                                        nc.scalar.wait_ge(dC.h, dC.n)
                                    ins = nc.scalar.activation(Ps[(e - 1) % 3][:, :], Sb[:, :], AF.Exp,
                                                               bias=expb[:, 0:1], scale=1.0)
                                    exp_cnt = sEXP.inc(ins)
                                if _P1L < 4:
                                    continue
                                kc2 = kc - LAG
                                if kc2 < 0:
                                    continue
                                x2 = e_base + kc2 + 1
                                if kc >= NKC:
                                    nc.tensor.wait_ge(sEXP.h, x2)
                                if kc2 == 0:
                                    nc.tensor.wait_ge(sOCP.h, sOCP.n)
                                Pb = Ps[(x2 - 1) % 3]
                                nc.tensor.matmul(Ops[0:65, 0:512], Vp[:, kc2, 0:65],
                                                 Pb[:, 0:512],
                                                 start=(kc2 == 0), stop=(kc2 == NKC - 1))
                                ins = nc.tensor.matmul(Ops[0:65, 512:1024], Vp[:, kc2, 0:65],
                                                       Pb[:, 512:1024],
                                                       start=(kc2 == 0), stop=(kc2 == NKC - 1))
                                m2_cnt = tM2.inc(ins)
                                if kc2 % 4 == 3:
                                    wq_drain()
                            if _P1L < 5:
                                continue
                            nc.scalar.wait_ge(tM2.h, m2_cnt)
                            if stats_osb_mark[0] > 0:
                                nc.scalar.wait_ge(vORD.h, stats_osb_mark[0])
                            ins = nc.scalar.activation(Osb[0:65, :], Ops[0:65, :], AF.Copy)
                            sOCP.inc(ins)
                            deferred_evac = mk_evac(beta, h, qg, yT)
                        m2_at_block[(beta, h)] = m2_cnt
                    if deferred_evac is not None:
                        deferred_evac()
                        deferred_evac = None
                    exp_at_pair.append(exp_cnt)

                    def mk_stats_g(beta, yT, g):
                        def run():
                            vx("tensor_mul", Osb[:, 0:1024], yT[:, g, :], yT[:, g, :])
                            vx("tensor_reduce", rec[:, 0:1], yT[:, g, :], AX.X, ALU.add)
                            vx("tensor_add", sacc[:, :], sacc[:, :], rec[:, 0:1])
                            vx("tensor_reduce", rec[:, 1:2], Osb[:, 0:1024], AX.X, ALU.add)
                            ins = vx("tensor_add", sqacc[:, :], sqacc[:, :], rec[:, 1:2])
                            stats_osb_mark[0] = vORD.n
                            if g == 3:
                                vSTA.inc(ins)
                                dsp = dSPL[beta % 2]
                                nc.sync.wait_ge(vSTA.h, vSTA.n)
                                nc.sync.dma_start(
                                    y_d[512 * beta:512 * (beta + 1), :].rearrange("(g p) d -> p g d", p=128),
                                    yT[:, :, :]).then_inc(dsp.h, 16); dsp.n += 16
                                spill_at_pair.append((beta % 2, dsp.n))
                        return run
                    if _P1L >= 6 and beta < NPAIR - 1:
                        for g in range(4):
                            pending_stats.append(mk_stats_g(beta, yT, g))
                    stt_at_pair.append(vYST.n)

                    if beta + 1 < NPAIR:
                        t0 = 512 * (beta + 1)
                        nc.gpsimd.wait_ge(tPRJ.h, tPRJ.n)
                        nc.gpsimd.dma_start(
                            xfms[(beta + 1) % 2][:, :, :],
                            xT.rearrange("(j p) t -> p j t", p=128)[:, :, t0:t0 + 512]).then_inc(dXF.h, 16); dXF.n += 16
                        nc.sync.wait_ge(vYST.h, stt_at_pair[beta])
                        nc.sync.dma_start(
                            xtok[:, :, :],
                            xN[t0:t0 + 512, :].rearrange("(g p) d -> p g d", p=128)).then_inc(dXT.h, 16); dXT.n += 16

                while pending_stats:
                    pending_stats.pop(0)()

        phase2_end = vMSC.n
        attn_T_prj = tPRJ.n
        attn_T_m1 = tM1.n
        attn_T_m2 = tM2.n
        attn_T_ytr = tYTR.n
        attn_T_vtr = tVTR.n
        attn_V_end = vSTA.n
        if _PH & 4:
            with (
                nc.psum_tensor("pb2", [128, 512], F32) as pb2,
            ):
                nc.tensor.wait_ge(vSTA.h, vSTA.n)
                nc.tensor.matmul(pb2[0:1, 0:1], sacc[:, :], ones_c[:, :], start=True, stop=True)
                ins = nc.tensor.matmul(pb2[0:1, 1:2], sqacc[:, :], ones_c[:, :],
                                       start=True, stop=True)
                tMSC.inc(ins)
                nc.vector.wait_ge(tMSC.h, tMSC.n)
                ins = vx("tensor_copy", ccs[0:1, 0:2], pb2[0:1, 0:2])
                vMSC.inc(ins)
                nc.sync.wait_ge(vMSC.h, vMSC.n)
                nc.sync.dma_start(cc_in, ccs[0:1, 0:2]).then_inc(dCC.h, 16); dCC.n += 16
                nc.gpsimd.wait_ge(dCC.h, dCC.n)
                ins = nc.gpsimd.collective_compute(
                    "AllReduce", ALU.add,
                    replica_groups=[[0, 1], [2, 3], [4, 5], [6, 7]],
                    ins=[cc_in], outs=[cc_out],
                )
                gCC.inc(ins)
                nc.sync.wait_ge(gCC.h, gCC.n)
                nc.sync.dma_start(ccs[0:1, 0:2], cc_out).then_inc(dCC.h, 16); dCC.n += 16
                nc.vector.wait_ge(dCC.h, dCC.n)
                vx("tensor_scalar_mul", ccs[0:1, 0:1], ccs[0:1, 0:1], 1.0 / NB_ELEM)
                vx("tensor_scalar_mul", ccs[0:1, 1:2], ccs[0:1, 1:2], 1.0 / NB_ELEM)
                vx("tensor_mul", ccs[0:1, 2:3], ccs[0:1, 0:1], ccs[0:1, 0:1])
                vx("tensor_sub", ccs[0:1, 1:2], ccs[0:1, 1:2], ccs[0:1, 2:3])
                ins = vx("tensor_scalar_add", ccs[0:1, 1:2], ccs[0:1, 1:2], EPS)
                vMSC.inc(ins)
                nc.scalar.wait_ge(vMSC.h, vMSC.n)
                ins = nc.scalar.activation(ccs[0:1, 2:3], ccs[0:1, 1:2], AF.Sqrt)
                sMSC.inc(ins)
                nc.vector.wait_ge(sMSC.h, sMSC.n)
                ins = vx("reciprocal", ccs[0:1, 3:4], ccs[0:1, 2:3])
                vMSC.inc(ins)
                nc.tensor.wait_ge(vMSC.h, vMSC.n)
                ins = nc.tensor.matmul(pb2[:, 3:4], ones_r[:, :], ccs[0:1, 3:4],
                                       start=True, stop=True)
                tMSC.inc(ins)
                nc.vector.wait_ge(tMSC.h, tMSC.n)
                ins = vx("tensor_copy", SCAL[:, 6:7], pb2[:, 3:4])
                vMSC.inc(ins)
            phase2_end = vMSC.n

        if _PH & 8:
            with ExitStack() as ph3:
                xz2 = ph3.enter_context(nc.sbuf_tensor("xz2", [128, 4, D], F32))
                zt = ph3.enter_context(nc.sbuf_tensor("zt", [128, 4, D], F32))
                tmpf = ph3.enter_context(nc.sbuf_tensor("tmpf", [128, 2048], F32))
                tmpb = ph3.enter_context(nc.sbuf_tensor("tmpb", [128, INNER], BF16))
                xqT = ph3.enter_context(nc.sbuf_tensor("xqT", [128, 8, 512], BF16))
                hsb = ph3.enter_context(nc.sbuf_tensor("hsb", [128, 4, INNER], F32))
                hqT = ph3.enter_context(nc.sbuf_tensor("hqT", [128, 32, 512], BF16))
                wgu0 = ph3.enter_context(nc.sbuf_tensor("wgu0", [128, 16, 512], BF16))
                wgu1 = ph3.enter_context(nc.sbuf_tensor("wgu1", [128, 16, 512], BF16))
                wd0 = ph3.enter_context(nc.sbuf_tensor("wd0", [128, 4, 1024], BF16))
                wd1 = ph3.enter_context(nc.sbuf_tensor("wd1", [128, 4, 1024], BF16))
                gs = ph3.enter_context(nc.sbuf_tensor("gs", [128, 512], F32))
                st8 = ph3.enter_context(nc.sbuf_tensor("st8", [128, 16], F32))
                pg8 = ph3.enter_context(nc.sbuf_tensor("pg8", [128, 12], F32))
                ah4 = ph3.enter_context(nc.sbuf_tensor("ah4", [128, 4], F32))
                ln6 = ph3.enter_context(nc.sbuf_tensor("ln6", [128, 12], F32))
                ln8 = ph3.enter_context(nc.sbuf_tensor("ln8", [128, 8], F32))
                lnS = ph3.enter_context(nc.sbuf_tensor("lnS", [128, 8], F32))
                PA = ph3.enter_context(nc.psum_tensor("PA", [128, 1024], F32))
                PB = ph3.enter_context(nc.psum_tensor("PB", [128, 1024], F32))
                PC = ph3.enter_context(nc.psum_tensor("PC", [128, 1024], F32))
                ptb = ph3.enter_context(nc.psum_tensor("ptb", [128, 1024], BF16))

                wgus = [wgu0, wgu1]
                wds = [wd0, wd1]
                gu_slab_mark = [0, 0]
                dwn_slab_mark = [0, 0]
                gu_order_st = [[2, 3, 4, 5, 6, 7, 0, 1]]
                dwn_order_st = [[2, 3, 4, 5, 6, 7, 0, 1]]
                z2ev_mark = [0]
                PAB = [PA, PB]
                pab_sil = [0, 0]
                pab_hst = [0, 0]
                pab_z2 = 0

                xqt_mark = {}
                scg_mark = {}
                pkq = []

                def issue_zt(tau_, pairg, mark=None):
                    r0 = 512 * tau_ + 256 * pairg
                    ga = 2 * pairg
                    dyp = dY if pairg == 0 else dYb
                    nc.gpsimd.wait_ge(vZ2.h, vZ2.n if mark is None else mark)
                    nc.gpsimd.dma_start(
                        zt[:, ga:ga + 2, :],
                        y_d[r0:r0 + 256, :].rearrange("(g p) d -> p g d", p=128)
                    ).then_inc(dyp.h, 16); dyp.n += 16

                def issue_xz2(tau_, pairg):
                    r0 = 512 * tau_ + 256 * pairg
                    ga = 2 * pairg
                    dxp = dXN if pairg == 0 else dXNb
                    nc.gpsimd.wait_ge(vZ2.h, vZ2.n)
                    nc.gpsimd.dma_start(
                        xz2[:, ga:ga + 2, :],
                        xN[r0:r0 + 256, :].rearrange("(g p) d -> p g d", p=128)
                    ).then_inc(dxp.h, 16); dxp.n += 16

                def issue_loads(tau_, pairg):
                    issue_zt(tau_, pairg)
                    issue_xz2(tau_, pairg)

                def issue_park_x(tau_):
                    r0 = 512 * tau_
                    nc.gpsimd.wait_ge(vQNT.h, vQNT.n)
                    nc.gpsimd.dma_start(
                        hsb[:, 0:2, 1024:2048],
                        xN[r0:r0 + 256, :].rearrange("(g p) d -> p g d", p=128)
                    ).then_inc(dXN.h, 16); dXN.n += 16

                def issue_park_y(tau_):
                    r0 = 512 * tau_
                    nc.gpsimd.wait_ge(vQNT.h, vQNT.n)
                    nc.gpsimd.dma_start(
                        hsb[:, 2:4, 1024:2048],
                        y_d[r0:r0 + 256, :].rearrange("(g p) d -> p g d", p=128)
                    ).then_inc(dY.h, 16); dY.n += 16

                def mk_ln1_closures(pairg, parked=False):
                    ga, gb = 2 * pairg, 2 * pairg + 1
                    (dyp, dxp) = (dY, dXN) if pairg == 0 else (dYb, dXNb)
                    def c1():
                        nc.vector.wait_ge(dyp.h, dyp.n)
                        nc.vector.wait_ge(dxp.h, dxp.n)
                        for i, g in enumerate((ga, gb)):
                            if parked:
                                vx("scalar_tensor_tensor", zt[:, g, :],
                                   hsb[:, 2 + i, 1024:2048], SCAL[:, 6:7],
                                   hsb[:, i, 1024:2048], ALU.mult, ALU.add)
                            else:
                                vx("scalar_tensor_tensor", zt[:, g, :], zt[:, g, :],
                                   SCAL[:, 6:7], xz2[:, g, :], ALU.mult, ALU.add)
                        for g in (ga, gb):
                            vx("bn_stats", ln6[:, 0:6], zt[:, g, 0:512])
                            vx("bn_stats", ln6[:, 6:12], zt[:, g, 512:1024])
                            vx("bn_aggr", ln8[:, 2 * g:2 * g + 2], ln6[:, :])
                    def c2():
                        for i, g in ((0, ga), (1, gb)):
                            vx("tensor_reduce", st8[:, 0 + i:1 + i], zt[:, g, :], AX.X, ALU.max)
                            vx("tensor_reduce", st8[:, 2 + i:3 + i], zt[:, g, :], AX.X, ALU.min)
                        lpr = ln8[:, 2 * ga:2 * ga + 4].rearrange(
                            "p (g two) -> p two g", two=2)
                        vx("tensor_copy", pg8[:, ga:ga + 2], lpr[:, 0, :])
                        ins = vx("tensor_scalar_add", st8[:, 4:6], lpr[:, 1, :], EPS)
                        vMSC.inc(ins)
                        nc.scalar.wait_ge(vMSC.h, vMSC.n)
                        ins = nc.scalar.activation(pg8[:, 4 + ga:4 + ga + 2], st8[:, 4:6],
                                                   AF.Sqrt)
                        sMSC.inc(ins)
                    def c3():
                        nc.vector.wait_ge(sMSC.h, sMSC.n)
                        vx("reciprocal", pg8[:, 4 + ga:4 + ga + 2], pg8[:, 4 + ga:4 + ga + 2])
                        vx("tensor_sub", st8[:, 0:2], st8[:, 0:2], pg8[:, ga:ga + 2])
                        vx("tensor_sub", st8[:, 2:4], pg8[:, ga:ga + 2], st8[:, 2:4])
                        vx("tensor_max", st8[:, 0:2], st8[:, 0:2], st8[:, 2:4])
                        vx("tensor_mul", st8[:, 0:2], st8[:, 0:2], pg8[:, 4 + ga:4 + ga + 2])
                        vx("tensor_scalar_max", st8[:, 0:2], st8[:, 0:2], 1e-5)
                        vx("reciprocal", st8[:, 2:4], st8[:, 0:2])
                        vx("tensor_scalar_mul", st8[:, 2:4], st8[:, 2:4], 127.0)
                        vx("tensor_scalar_mul", st8[:, 4:6], st8[:, 0:2], 1.0 / 127.0)
                        for i, g in enumerate((ga, gb)):
                            vx("tensor_mul", st8[:, 8 + g:9 + g], st8[:, 4 + i:5 + i],
                               SCAL[:, 3:4])
                            vx("tensor_mul", st8[:, 12 + g:13 + g], st8[:, 4 + i:5 + i],
                               SCAL[:, 4:5])
                        scg_mark[ga] = scg_mark[gb] = vORD.n
                        vx("tensor_mul", st8[:, 0:2], pg8[:, 4 + ga:4 + ga + 2], st8[:, 2:4])
                        vx("tensor_mul", st8[:, 2:4], pg8[:, ga:ga + 2], st8[:, 0:2])
                        vx("tensor_scalar", st8[:, 2:4], st8[:, 2:4], -1.0, MAGIC,
                           ALU.mult, ALU.add)
                    def mk_quant(i, g):
                        def cq():
                            vx("tensor_scalar", tmpf[:, 0:D], zt[:, g, :],
                               st8[:, 0 + i:1 + i], st8[:, 2 + i:3 + i], ALU.mult, ALU.add)
                            vx("tensor_scalar", tmpf[:, 0:D], tmpf[:, 0:D], -MAGIC, 127.0,
                                                    ALU.add, ALU.min)
                            ins = vx("tensor_scalar", tmpb[:, 0:D], tmpf[:, 0:D],
                                     -128.0, None, ALU.max)
                            vQNT.inc(ins)
                        return cq
                    def mk_tr(g):
                        def ct():
                            nc.tensor.wait_ge(vQNT.h, vQNT.n)
                            if tTRX.n > 0:
                                nc.tensor.wait_ge(vTEV.h, vTEV.n)
                            ins = None
                            for jj in range(8):
                                ins = nc.tensor.transpose(ptb[:, jj * 128:(jj + 1) * 128],
                                                          tmpb[:, jj * 128:(jj + 1) * 128],
                                                          identb[:, :])
                            tTRX.inc(ins)
                            nc.vector.wait_ge(tTRX.h, tTRX.n)
                            nc.vector.wait_ge(tGU.h, tGU.n)
                            ins = vx("tensor_copy",
                                xqT[:, :, g * 128:(g + 1) * 128],
                                ptb[:, :].rearrange("p (j t) -> p j t", j=8))
                            vTEV.inc(ins)
                            xqt_mark[g] = vTEV.n
                        return ct
                    return [c1, c2, c3, mk_quant(0, ga), mk_tr(ga),
                            mk_quant(1, gb), mk_tr(gb)]

                def ln2_pair(tau_, p):
                    t0_ = 512 * tau_
                    ga, gb = 2 * p, 2 * p + 1
                    for i, g in ((0, ga), (1, gb)):
                        vx("bn_stats", ln6[:, 0:6], xz2[:, g, 0:512])
                        vx("bn_stats", ln6[:, 6:12], xz2[:, g, 512:1024])
                        vx("bn_aggr", lnS[:, 2 * i:2 * i + 2], ln6[:, :])
                    lp = lnS[:, 0:4].rearrange("p (g two) -> p two g", two=2)
                    ins = vx("tensor_scalar_add", lnS[:, 4:6], lp[:, 1, :], EPS)
                    vMSC.inc(ins)
                    nc.scalar.wait_ge(vMSC.h, vMSC.n)
                    ins = nc.scalar.activation(lnS[:, 6:8], lnS[:, 4:6], AF.Sqrt)
                    sMSC.inc(ins)
                    nc.vector.wait_ge(sMSC.h, sMSC.n)
                    vx("reciprocal", lnS[:, 6:8], lnS[:, 6:8])
                    for i, g in ((0, ga), (1, gb)):
                        ins = vx("tensor_scalar", hsb[:, g, 0:D], xz2[:, g, :],
                                 lnS[:, 2 * i:2 * i + 1], lnS[:, 6 + i:7 + i],
                                 ALU.subtract, ALU.mult)
                        vZ2.inc(ins)
                        nc.sync.wait_ge(vZ2.h, vZ2.n)
                        nc.sync.dma_start(out[t0_ + g * 128:t0_ + (g + 1) * 128, :],
                                          hsb[:, g, 0:D]).then_inc(dOUT.h, 16); dOUT.n += 16

                for tau in range(4):
                    t0 = 512 * tau
                    if tau == 0:
                        nc.sync.wait_ge(tPRJ.h, attn_T_prj)
                        nc.sync.wait_ge(tM1.h, attn_T_m1)
                        nc.sync.wait_ge(tM2.h, attn_T_m2)
                        nc.sync.wait_ge(tYTR.h, attn_T_ytr)
                        nc.sync.wait_ge(tVTR.h, attn_T_vtr)
                        nc.sync.wait_ge(vORD.h, attn_V_end)
                        nc.sync.wait_ge(dWQS[0].h, dWQS[0].n)
                        nc.sync.wait_ge(dWQS[1].h, dWQS[1].n)
                        nc.sync.wait_ge(dSPL[0].h, dSPL[0].n)
                        nc.sync.wait_ge(dSPL[1].h, dSPL[1].n)
                        nc.gpsimd.wait_ge(tPRJ.h, attn_T_prj)
                        nc.gpsimd.wait_ge(tM1.h, attn_T_m1)
                        nc.gpsimd.wait_ge(tM2.h, attn_T_m2)
                        nc.gpsimd.wait_ge(tYTR.h, attn_T_ytr)
                        nc.gpsimd.wait_ge(tVTR.h, attn_T_vtr)
                        nc.gpsimd.wait_ge(vORD.h, attn_V_end)
                        nc.gpsimd.wait_ge(dWQS[0].h, dWQS[0].n)
                        nc.gpsimd.wait_ge(dWQS[1].h, dWQS[1].n)
                        nc.gpsimd.wait_ge(dSPL[0].h, dSPL[0].n)
                        nc.gpsimd.wait_ge(dSPL[1].h, dSPL[1].n)
                        issue_loads(0, 0)
                        issue_loads(0, 1)
                        for st_ in mk_ln1_closures(0):
                            st_()
                        pkq = mk_ln1_closures(1)
                    else:
                        issue_xz2(tau, 1)
                        pkq = mk_ln1_closures(1)

                    def gu_slab(dst_par, ibig2, eng=None):
                        eng = eng or nc.sync
                        eng.wait_ge(tGU.h, gu_slab_mark[dst_par])
                        eng.wait_ge(dGU[dst_par].h, dGU[dst_par].n)
                        eng.dma_start(wgus[dst_par][:, :, :],
                                      wguq_d[:, :, ibig2 * 512:(ibig2 + 1) * 512]
                                      .rearrange("j p i -> p j i")
                                      ).then_inc(dGU[dst_par].h, 16)
                        dGU[dst_par].n += 16

                    def wd_slab(dst_par, sl_):
                        eng = nc.sync
                        eng.wait_ge(tDWN.h, dwn_slab_mark[dst_par])
                        eng.wait_ge(dWD[dst_par].h, dWD[dst_par].n)
                        eng.dma_start(
                            wds[dst_par][:, :, :],
                            wdq_d[4 * sl_:4 * (sl_ + 1), :, :]
                            .rearrange("i p d -> p i d")).then_inc(dWD[dst_par].h, 16)
                        dWD[dst_par].n += 16

                    def mk_hq_steps(g):
                        def s_scale(g=g):
                            vx("tensor_scalar_max", st8[:, 0:1], ah4[:, g:g + 1], 1e-5)
                            vx("reciprocal", st8[:, 1:2], st8[:, 0:1])
                            vx("tensor_scalar_mul", st8[:, 2:3], st8[:, 0:1], 1.0 / 127.0)
                            vx("tensor_mul", pg8[:, 8 + g:9 + g], st8[:, 2:3], SCAL[:, 5:6])
                            vx("tensor_scalar_mul", ah4[:, g:g + 1], st8[:, 1:2], 127.0)
                        yield s_scale
                        for hh in range(2):
                            def s_ts(g=g, hh=hh):
                                s0_, s1_ = hh * 2048, (hh + 1) * 2048
                                vx("tensor_scalar", tmpf[:, :], hsb[:, g, s0_:s1_],
                                   ah4[:, g:g + 1], MAGIC, ALU.mult, ALU.add)
                                vx("tensor_scalar", tmpf[:, :], tmpf[:, :], -MAGIC, 127.0,
                                   ALU.add, ALU.min)
                                ins = vx("tensor_scalar", tmpb[:, s0_:s1_], tmpf[:, :],
                                         -128.0, None, ALU.max)
                                vQNT.inc(ins)
                            yield s_ts
                        for grp in range(4):
                            def s_tr(g=g, grp=grp):
                                nc.tensor.wait_ge(vQNT.h, vQNT.n)
                                nc.tensor.wait_ge(vTEV.h, vTEV.n)
                                ins = None
                                for jj in range(8):
                                    i2 = grp * 8 + jj
                                    ins = nc.tensor.transpose(ptb[:, jj * 128:(jj + 1) * 128],
                                                              tmpb[:, i2 * 128:(i2 + 1) * 128],
                                                              identb[:, :])
                                tTRX.inc(ins)
                                nc.vector.wait_ge(tTRX.h, tTRX.n)
                                ins = vx("tensor_copy",
                                         hqT[:, grp * 8:(grp + 1) * 8, g * 128:(g + 1) * 128],
                                         ptb[:, :].rearrange("p (j t) -> p j t", j=8))
                                vTEV.inc(ins)
                            yield s_tr

                    def gu_pass(half):
                        nc.vector.wait_ge(dOUT.h, dOUT.n)
                        prev = gu_order_st[0]
                        order = [prev[6], prev[7]] + [s for s in range(8)
                                                      if s not in (prev[6], prev[7])]
                        gu_order_st[0] = order
                        for pos in range(8):
                            par = pos % 2
                            ibig = order[pos]
                            if 1 <= pos and pos + 1 < 8:
                                gu_slab((pos + 1) % 2, order[pos + 1])
                            sl = wgus[par]
                            nc.tensor.wait_ge(dGU[par].h, dGU[par].n)
                            for g in (2 * half, 2 * half + 1):
                                pt = PAB[g % 2]
                                if pab_sil[g % 2] > 0:
                                    nc.tensor.wait_ge(sSIL.h, pab_sil[g % 2])
                                if pab_hst[g % 2] > 0:
                                    nc.tensor.wait_ge(vHST.h, pab_hst[g % 2])
                                if pos == 0:
                                    nc.tensor.wait_ge(vTEV.h, xqt_mark[g])
                                    if g == 2 * half:
                                        nc.tensor.wait_ge(vZ2.h, z2ev_mark[0])
                                ins = None
                                for j in range(8):
                                    nc.tensor.matmul(pt[:, 0:512],
                                                     xqT[:, j, g * 128:(g + 1) * 128],
                                                     sl[:, j, :],
                                                     start=(j == 0), stop=(j == 7))
                                    ins = nc.tensor.matmul(pt[:, 512:1024],
                                                           xqT[:, j, g * 128:(g + 1) * 128],
                                                           sl[:, 8 + j, :],
                                                           start=(j == 0), stop=(j == 7))
                                tGU.inc(ins)
                                nc.scalar.wait_ge(tGU.h, tGU.n)
                                if pos == 0:
                                    nc.scalar.wait_ge(vORD.h, scg_mark[g])
                                if sSIL.n > 0:
                                    nc.scalar.wait_ge(vHST.h, vHST.n)
                                ins = nc.scalar.activation(gs[:, 0:512], pt[:, 0:512], AF.Silu,
                                                           scale=st8[:, 8 + g:9 + g])
                                pab_sil[g % 2] = sSIL.inc(ins)
                                nc.vector.wait_ge(sSIL.h, sSIL.n)
                                vx("scalar_tensor_tensor",
                                   hsb[:, g, ibig * 512:(ibig + 1) * 512],
                                   pt[:, 512:1024], st8[:, 12 + g:13 + g], gs[:, 0:512],
                                   ALU.mult, ALU.mult)
                                vx("tensor_reduce", st8[:, 6:7],
                                   hsb[:, g, ibig * 512:(ibig + 1) * 512],
                                   AX.X, ALU.max, apply_absolute_value=True)
                                if pos == 0:
                                    ins = vx("tensor_copy", ah4[:, g:g + 1], st8[:, 6:7])
                                else:
                                    ins = vx("tensor_max", ah4[:, g:g + 1], ah4[:, g:g + 1], st8[:, 6:7])
                                pab_hst[g % 2] = vHST.inc(ins)
                                if pkq:
                                    pkq.pop(0)()
                            gu_slab_mark[par] = tGU.n

                    def down_pass(half, hq_mark):
                        z2p = {2 * half: PC, 2 * half + 1: PA}
                        pc_free = z2ev_mark[0]
                        prev = dwn_order_st[0]
                        order = [prev[6], prev[7]] + [s for s in range(8)
                                                      if s not in (prev[6], prev[7])]
                        dwn_order_st[0] = order
                        for pos in range(8):
                            par = pos % 2
                            sl_ = order[pos]
                            if 1 <= pos and pos + 1 < 8:
                                wd_slab((pos + 1) % 2, order[pos + 1])
                            nc.tensor.wait_ge(dWD[par].h, dWD[par].n)
                            for q in range(4):
                                i = sl_ * 4 + q
                                if pos == 0 and q == 0:
                                    nc.tensor.wait_ge(vTEV.h, hq_mark)
                                    nc.tensor.wait_ge(vZ2.h, pc_free)
                                    if half == 0:
                                        nc.tensor.wait_ge(sSIL.h, sSIL.n)
                                        nc.tensor.wait_ge(vHST.h, vHST.n)
                                ins = None
                                for g in (2 * half, 2 * half + 1):
                                    pt = z2p[g]
                                    for dh in range(2):
                                        ins = nc.tensor.matmul(pt[:, dh * 512:(dh + 1) * 512],
                                                               hqT[:, i, g * 128:(g + 1) * 128],
                                                               wds[par][:, q, dh * 512:(dh + 1) * 512],
                                                               start=(pos == 0 and q == 0),
                                                               stop=(pos == 7 and q == 3))
                                tDWN.inc(ins)
                                if q == 3:
                                    dwn_slab_mark[par] = tDWN.n
                                if q % 2 == 0 and pkq:
                                    pkq.pop(0)()
                        nc.vector.wait_ge(tDWN.h, tDWN.n)
                        ins = None
                        for g in (2 * half, 2 * half + 1):
                            pt = z2p[g]
                            vx("tensor_scalar", tmpf[:, 0:D], zt[:, g, :],
                               pg8[:, g:g + 1], pg8[:, 4 + g:5 + g],
                               ALU.subtract, ALU.mult)
                            ins = vx("scalar_tensor_tensor",
                                     xz2[:, g, :],
                                     pt[:, :], pg8[:, 8 + g:9 + g], tmpf[:, 0:D],
                                     ALU.mult, ALU.add)
                        vZ2.inc(ins)
                        z2ev_mark[0] = vZ2.n

                    if tau == 0:
                        gu_slab(0, 0)
                        gu_slab(1, 1)
                    gu_pass(0)
                    while pkq:
                        pkq.pop(0)()
                    pkq = [s for g_ in (0, 1) for s in mk_hq_steps(g_)]
                    gu_pass(1)
                    while pkq:
                        pkq.pop(0)()
                    hq01 = vTEV.n
                    pkq = [s for g_ in (2, 3) for s in mk_hq_steps(g_)]
                    if tau == 0:
                        wd_slab(0, 0)
                        wd_slab(1, 1)
                    if tau + 1 < 4:
                        issue_park_x(tau + 1)
                    down_pass(0, hq01)
                    while pkq:
                        pkq.pop(0)()
                    hq23 = vTEV.n
                    if tau + 1 < 4:
                        issue_park_y(tau + 1)
                    ln2_pair(tau, 0)
                    if tau + 1 < 4:
                        pkq = mk_ln1_closures(0, parked=True)
                    down_pass(1, hq23)
                    while pkq:
                        pkq.pop(0)()
                    if tau + 1 < 4:
                        issue_zt(tau + 1, 1, mark=z2ev_mark[0])
                    ln2_pair(tau, 1)

        if not (_PH & 8):
            nc.sync.wait_ge(vORD.h, vORD.n)
            nc.sync.dma_start(out[0:128, 0:8], SCAL[:, :]).then_inc(dOUT.h, 16); dOUT.n += 16
        nc.sync.wait_ge(dOUT.h, dOUT.n)
    return nc


_NC_CACHE = None


def kernel(L0_attn_x, L0_attn_Wq, L0_attn_Wk, L0_attn_Wv,
           L0_ffn_w_gate, L0_ffn_w_up, L0_ffn_w_down):
    global _NC_CACHE
    x = np.asarray(L0_attn_x, dtype=np.float32).reshape(4 * 4096, 1024)
    wqT = np.ascontiguousarray(np.asarray(L0_attn_Wq, np.float32).T)
    wkT = np.ascontiguousarray(np.asarray(L0_attn_Wk, np.float32).T)
    wvT = np.ascontiguousarray(np.asarray(L0_attn_Wv, np.float32).T)
    wgT = np.ascontiguousarray(np.asarray(L0_ffn_w_gate, np.float32).T)
    wuT = np.ascontiguousarray(np.asarray(L0_ffn_w_up, np.float32).T)
    wdT = np.ascontiguousarray(np.asarray(L0_ffn_w_down, np.float32).T)

    idb = np.eye(128, dtype=np.float32).astype(BD)
    idf = np.eye(128, dtype=np.float32)
    onesr = np.ones((1, 128), np.float32)
    onesc = np.ones((128, 1), np.float32)

    if _NC_CACHE is None:
        _NC_CACHE = build_nc()
    nc = _NC_CACHE

    in_maps = []
    for c in range(NCORE):
        rows = x[c * T:(c + 1) * T]
        in_maps.append({
            "xT": np.ascontiguousarray(rows.T),
            "xN": np.ascontiguousarray(rows),
            "wqT": wqT, "wkT": wkT, "wvT": wvT,
            "wgT": wgT, "wuT": wuT, "wdT": wdT,
            "idb": idb, "idf": idf,
            "onesr": onesr, "onesc": onesc,
        })
    res = run_bass_kernel_spmd(nc, in_maps, core_ids=list(range(NCORE)))
    full = np.concatenate([res.results[c]["out"] for c in range(NCORE)], axis=0)
    return full.reshape(4, 4096, 1024)


# version 30
# speedup vs baseline: 1.0586x; 1.0023x over previous
import os
import numpy as np
import ml_dtypes
from contextlib import ExitStack

import concourse.bass as bass
import concourse.mybir as mybir
from concourse.bass_utils import run_bass_kernel_spmd

F32 = mybir.dt.float32
BF16 = mybir.dt.bfloat16
AF = mybir.ActivationFunctionType
ALU = mybir.AluOpType
AX = mybir.AxisListType

SHIFT = 44.0
MAGIC = 12582912.0
EPS = 1e-5
NCORE = 8
T = 2048
NPAIR = 4
NQG = 4
NKC = 32
D = 1024
INNER = 4096
NB_ELEM = 4096 * 1024

BD = ml_dtypes.bfloat16
_PH0 = 0
_PH = int(os.environ.get("BASSK_PH", "15"))
_P1L = int(os.environ.get("BASSK_P1L", "9"))


class Ctr:
    def __init__(self, stack, nc, name):
        self.h = stack.enter_context(nc.semaphore(name))
        self.n = 0

    def inc(self, instr, k=1):
        instr.then_inc(self.h, k)
        self.n += k
        return self.n


def build_nc():
    nc = bass.Bass("TRN2", target_bir_lowering=False)

    xT = nc.dram_tensor("xT", [D, T], F32, kind="ExternalInput").ap()
    xN = nc.dram_tensor("xN", [T, D], F32, kind="ExternalInput").ap()
    wqT = nc.dram_tensor("wqT", [D, D], F32, kind="ExternalInput").ap()
    wkT = nc.dram_tensor("wkT", [D, D], F32, kind="ExternalInput").ap()
    wvT = nc.dram_tensor("wvT", [D, D], F32, kind="ExternalInput").ap()
    wgT = nc.dram_tensor("wgT", [D, INNER], F32, kind="ExternalInput").ap()
    wuT = nc.dram_tensor("wuT", [D, INNER], F32, kind="ExternalInput").ap()
    wdT = nc.dram_tensor("wdT", [INNER, D], F32, kind="ExternalInput").ap()
    idb = nc.dram_tensor("idb", [128, 128], BF16, kind="ExternalInput").ap()
    idf = nc.dram_tensor("idf", [128, 128], F32, kind="ExternalInput").ap()
    onesr = nc.dram_tensor("onesr", [1, 128], F32, kind="ExternalInput").ap()
    onesc = nc.dram_tensor("onesc", [128, 1], F32, kind="ExternalInput").ap()
    out = nc.dram_tensor("out", [T, D], F32, kind="ExternalOutput").ap()

    y_d = nc.dram_tensor("y_d", [T, D], F32, kind="Internal").ap()
    wguq_d = nc.dram_tensor("wguq_d", [16, 128, INNER], BF16, kind="Internal").ap()
    wdq_d = nc.dram_tensor("wdq_d", [32, 128, D], BF16, kind="Internal").ap()
    cc_in = nc.dram_tensor("cc_in", [1, 2], F32, kind="Internal").ap()
    cc_out = nc.dram_tensor("cc_out", [1, 2], F32, kind="Internal").ap()

    st = ExitStack()
    with st:
        C = lambda name: Ctr(st, nc, name)
        dC = C("dC"); dWA = [C("dWA0"), C("dWA1"), C("dWA2")]; dXF = C("dXF"); dXT = C("dXT")
        dBQ = C("dBQ"); dBK = C("dBK"); dSPL = [C("dSPL0"), C("dSPL1")]
        dWP = [C("dWP0"), C("dWP1"), C("dWP2"), C("dWP3")]; dWQS = [C("dWQS0"), C("dWQS1")]; dCC = C("dCC")
        dY = C("dY"); dYb = C("dYb"); dXN = C("dXN"); dXNb = C("dXNb"); dGU = [C("dGU0"), C("dGU1")]; dWD = [C("dWD0"), C("dWD1")]; dWDg = [C("dWDg0"), C("dWDg1")]; dOUT = C("dOUT")
        tM1 = C("tM1"); tM2 = C("tM2"); sEXP = C("sEXP")
        tYTR = C("tYTR"); tPRJ = C("tPRJ"); tVTR = C("tVTR")
        gCC = C("gCC"); sMSC = C("sMSC"); tMSC = C("tMSC")
        tGU = C("tGU"); sSIL = C("sSIL"); sOCP = C("sOCP")
        tTRX = C("tTRX"); tDWN = C("tDWN")
        vORD = C("vORD")
        gORD = C("gORD")

        class VMark:
            def __init__(self):
                self.h = vORD.h
                self.n = 0
            def inc(self, ins):
                self.n = vORD.n
                return self.n

        vOCP = VMark(); vYST = VMark(); vPEV = VMark(); vVEV = VMark()
        vSTA = VMark(); vMSC = VMark(); vHST = VMark(); vQNT = VMark()
        vTEV = VMark(); vZ2 = VMark()

        def gx(_m, *a, **k):
            if gORD.n:
                nc.gpsimd.wait_ge(gORD.h, gORD.n)
            ins = getattr(nc.gpsimd, _m)(*a, **k)
            gORD.inc(ins)
            return ins

        def vx(_m, *a, **k):
            if vORD.n:
                nc.vector.wait_ge(vORD.h, vORD.n)
            ins = getattr(nc.vector, _m)(*a, **k)
            vORD.inc(ins)
            return ins

        sb = lambda *a: st.enter_context(nc.sbuf_tensor(*a))
        identb = sb("identb", [128, 128], BF16)
        identf = sb("identf", [128, 128], F32)
        ones_r = sb("ones_r", [1, 128], F32)
        ones_c = sb("ones_c", [128, 1], F32)
        expb = sb("expb", [128, 1], F32)
        SCAL = sb("SCAL", [128, 8], F32)
        sacc = sb("sacc", [128, 1], F32)
        ccs = sb("ccs", [1, 4], F32)
        sqacc = sb("sqacc", [128, 1], F32)

        for (dst_t, src_t) in ((identb, idb), (identf, idf),
                               (ones_r, onesr), (ones_c, onesc)):
            nc.sync.dma_start(dst_t[:, :], src_t).then_inc(dC.h, 16); dC.n += 16
        vx("memset", expb[:, :], -SHIFT)
        vx("memset", sacc[:, :], 0.0)
        ins = vx("memset", sqacc[:, :], 0.0)
        vMSC.inc(ins)

        wprep_vqnt = 0
        wprep_dwqs = 0
        if _PH & 2:
            with ExitStack() as ph1:
                wq_sb = ph1.enter_context(nc.sbuf_tensor("wq_sb", [128, 8, D], BF16))
                wk_sb = ph1.enter_context(nc.sbuf_tensor("wk_sb", [128, 8, D], BF16))
                wv_sb = ph1.enter_context(nc.sbuf_tensor("wv_sb", [128, 8, D], BF16))
                xfm0 = ph1.enter_context(nc.sbuf_tensor("xfm0", [128, 8, 512], BF16))
                xtok = ph1.enter_context(nc.sbuf_tensor("xtok", [128, 4, D], F32))
                QzA0 = ph1.enter_context(nc.sbuf_tensor("QzA0", [128, 4096], BF16))
                QzA1 = ph1.enter_context(nc.sbuf_tensor("QzA1", [128, 4096], BF16))
                QzB0 = ph1.enter_context(nc.sbuf_tensor("QzB0", [128, 4096], BF16))
                QzB1 = ph1.enter_context(nc.sbuf_tensor("QzB1", [128, 4096], BF16))
                Kt0 = ph1.enter_context(nc.sbuf_tensor("Kt0", [128, 4096], BF16))
                Kt1 = ph1.enter_context(nc.sbuf_tensor("Kt1", [128, 4096], BF16))
                Vt0 = ph1.enter_context(nc.sbuf_tensor("Vt0", [128, 8, 512], BF16))
                bQK = ph1.enter_context(nc.sbuf_tensor("bQK", [128, 8, 512], BF16))
                Vp0 = ph1.enter_context(nc.sbuf_tensor("Vp0", [128, 32, 68], BF16))
                Vp1 = ph1.enter_context(nc.sbuf_tensor("Vp1", [128, 32, 68], BF16))
                P0 = ph1.enter_context(nc.sbuf_tensor("P0", [128, 1024], BF16))
                P1 = ph1.enter_context(nc.sbuf_tensor("P1", [128, 1024], BF16))
                P2 = ph1.enter_context(nc.sbuf_tensor("P2", [128, 1024], BF16))
                Osb = ph1.enter_context(nc.sbuf_tensor("Osb", [128, 1024], F32))
                yT0 = ph1.enter_context(nc.sbuf_tensor("yT0", [128, 4, D], F32))
                yT1 = ph1.enter_context(nc.sbuf_tensor("yT1", [128, 4, D], F32))
                rec = ph1.enter_context(nc.sbuf_tensor("rec", [128, 8], F32))
                wpb0 = ph1.enter_context(nc.sbuf_tensor("wpb0", [128, 512], F32))
                wpb1 = ph1.enter_context(nc.sbuf_tensor("wpb1", [128, 512], F32))
                wpb2 = ph1.enter_context(nc.sbuf_tensor("wpb2", [128, 512], F32))
                wpb3 = ph1.enter_context(nc.sbuf_tensor("wpb3", [128, 512], F32))
                wpq0 = ph1.enter_context(nc.sbuf_tensor("wpq0", [128, 512], BF16))
                wpq1 = ph1.enter_context(nc.sbuf_tensor("wpq1", [128, 512], BF16))
                wcol = ph1.enter_context(nc.sbuf_tensor("wcol", [128, 4], F32))
                wsc = ph1.enter_context(nc.sbuf_tensor("wsc", [1, 4], F32))
                S0 = ph1.enter_context(nc.psum_tensor("S0", [128, 1024], F32))
                S1 = ph1.enter_context(nc.psum_tensor("S1", [128, 1024], F32))
                Ops = ph1.enter_context(nc.psum_tensor("Ops", [128, 1024], F32))
                yTp = ph1.enter_context(nc.psum_tensor("yTp", [128, 512], F32))
                Vps = ph1.enter_context(nc.psum_tensor("Vps", [128, 1024], BF16))

                xfms = [xfm0, xfm0]
                QzAs = [QzA0, QzA1]; QzBs = [QzB0, QzB1]
                Kts = [Kt0, Kt1]; Vts = [Vt0, Vt0]
                Vps_ = [Vp0, Vp1]; Ps = [P0, P1, P2]; yTs = [yT0, yT1]
                Ss = [S0, S1]

                nc.gpsimd.wait_ge(vQNT.h, wprep_vqnt)

                nc.gpsimd.dma_start(wq_sb[:, :, :],
                                    wqT.rearrange("(j p) k -> p j k", p=128)).then_inc(dWA[0].h, 16); dWA[0].n += 16
                nc.gpsimd.dma_start(xfm0[:, :, :],
                                    xT.rearrange("(j p) t -> p j t", p=128)[:, :, 0:512]).then_inc(dXF.h, 16); dXF.n += 16
                for (wi, (w_sb, w_hbm)) in enumerate(((wk_sb, wkT), (wv_sb, wvT))):
                    nc.gpsimd.dma_start(w_sb[:, :, :],
                                        w_hbm.rearrange("(j p) k -> p j k", p=128)).then_inc(dWA[1 + wi].h, 16); dWA[1 + wi].n += 16
                vx("memset", Vp0[:, :, 64:65], 1.0)
                vx("memset", Vp1[:, :, 64:65], 1.0)
                vx("memset", QzA0[64:128, :], 0.0)
                vx("memset", QzA1[64:128, :], 0.0)
                vx("memset", QzB0[0:64, :], 0.0)
                ins = vx("memset", QzB1[0:64, :], 0.0)
                vMSC.inc(ins)

                nc.sync.wait_ge(vQNT.h, wprep_vqnt)

                nc.sync.dma_start(xtok[:, :, :],
                                  xN[0:512, :].rearrange("(g p) d -> p g d", p=128)).then_inc(dXT.h, 16); dXT.n += 16

                exp_cnt = 0
                m1_cnt = 0
                m2_cnt = 0
                slot_evt = [0] * 4
                exp_at_pair = []
                pending_stats = []
                m2_at_block = {}
                stt_at_pair = []
                spill_at_pair = []
                vev_at_block = {}

                def pslot(k):
                    t_ = Ss[(k // 2) % 2]
                    return t_, 512 * (k % 2)

                wbufs = [wpb0, wpb1, wpb2, wpb3]
                wspecs = [
                    ("g", wgT, wguq_d, 8, INNER, 0, 3),
                    ("u", wuT, wguq_d, 8, INNER, 1, 4),
                    ("d", wdT, wdq_d, 32, D, 2, 5),
                ]
                wspec_by = {sp[0]: sp for sp in wspecs}
                wsub = {}
                for (nm, srcw, dstw, nch, wdt, s_slot, m_slot) in wspecs:
                    subs = []
                    for c in range(nch):
                        for c0 in range(0, wdt, 512):
                            subs.append((c, c0, 512))
                    wsub[nm] = subs
                wstream = []
                for kind, nm in (("A", "g"), ("A", "u"), ("B", "g"),
                                 ("A", "d"), ("B", "u"), ("B", "d")):
                    for sub in wsub[nm]:
                        wstream.append((kind, nm, sub))
                segA_last = {}
                segA_first = {}
                segB_first = {}
                for k, (kind, nm, sub) in enumerate(wstream):
                    if kind == "A":
                        segA_last[nm] = k
                        segA_first.setdefault(nm, k)
                    else:
                        segB_first.setdefault(nm, k)
                fin_mark = {}
                wprocmark = {}
                wdma_issued = [False] * (len(wstream) + 1)
                wpqs = [wpq0, wpq1]
                wqs_mark = [0, 0]
                bq_idx = [0]

                def w_issue_dma(k):
                    if k >= len(wstream) or wdma_issued[k]:
                        return
                    wdma_issued[k] = True
                    (kind, nm, (c, c0, w_)) = wstream[k]
                    srcw = wspec_by[nm][1]
                    par = k % 4
                    dw = dWP[par]
                    if k >= 4:
                        (eng, val) = wprocmark[k - 4]
                        nc.gpsimd.wait_ge(vORD.h if eng == "V" else gORD.h, val)
                    nc.gpsimd.dma_start(wbufs[par][:, 0:w_],
                                        srcw[c * 128:(c + 1) * 128, c0:c0 + w_]).then_inc(dw.h, 16)
                    dw.n += 16

                def w_step(k):
                    (kind, nm, (c, c0, w_)) = wstream[k]
                    (nm2, srcw, dstw, nch, wdt, s_slot, m_slot) = wspec_by[nm]
                    par = k % 4
                    dw = dWP[par]
                    b = wbufs[par]
                    need = 16 * (k // 4 + 1)
                    if kind == "A":
                        ac = {"g": 0, "u": 1, "d": 2}[nm]
                        nc.vector.wait_ge(dw.h, need)
                        if k == segA_first[nm]:
                            vx("memset", wcol[:, ac:ac + 1], 0.0)
                        vx("tensor_reduce", wcol[:, 3:4], b[:, 0:w_], AX.X, ALU.add,
                           apply_absolute_value=True)
                        ins = vx("tensor_add", wcol[:, ac:ac + 1], wcol[:, ac:ac + 1],
                                 wcol[:, 3:4])
                        wprocmark[k] = ("V", vORD.n)
                    else:
                        nc.vector.wait_ge(dw.h, need)
                        if k == segB_first[nm]:
                            nc.vector.wait_ge(vMSC.h, fin_mark[nm])
                        vx("tensor_scalar", b[:, 0:w_], b[:, 0:w_],
                           SCAL[:, s_slot:s_slot + 1], MAGIC, ALU.mult, ALU.add)
                        vx("tensor_scalar", b[:, 0:w_], b[:, 0:w_], -MAGIC, 1.0,
                           ALU.add, ALU.min)
                        bp = bq_idx[0] % 2
                        bq_idx[0] += 1
                        wpq = wpqs[bp]
                        dq = dWQS[bp]
                        if wqs_mark[bp] > 0:
                            nc.vector.wait_ge(dq.h, wqs_mark[bp])
                        ins = vx("tensor_scalar", wpq[:, 0:w_], b[:, 0:w_], -1.0, None, ALU.max)
                        nc.sync.wait_ge(vORD.h, vORD.n)
                        row_off = 8 if nm == "u" else 0
                        nc.sync.dma_start(dstw[row_off + c, :, c0:c0 + w_],
                                            wpq[:, 0:w_]).then_inc(dq.h, 16)
                        dq.n += 16
                        wqs_mark[bp] = dq.n
                        wprocmark[k] = ("V", vORD.n)
                    w_issue_dma(k + 1)
                    w_issue_dma(k + 2)

                def w_finalize(nm, ytp_guard):
                    (nm2, srcw, dstw, nch, wdt, s_slot, m_slot) = wspec_by[nm]
                    nel = float(nch * 128 * wdt)
                    (eng, val) = wprocmark[segA_last[nm]]
                    nc.tensor.wait_ge(vORD.h, val)
                    if ytp_guard > 0:
                        nc.tensor.wait_ge(vYST.h, ytp_guard)
                    ac = {"g": 0, "u": 1, "d": 2}[nm]
                    ins = nc.tensor.matmul(yTp[0:1, 0:1], wcol[:, ac:ac + 1], ones_c[:, :],
                                           start=True, stop=True)
                    tMSC.inc(ins)
                    nc.vector.wait_ge(tMSC.h, tMSC.n)
                    vx("tensor_scalar", wsc[0:1, 0:1], yTp[0:1, 0:1],
                       1.0 / nel, 1e-5, ALU.mult, ALU.max)
                    ins = vx("reciprocal", wsc[0:1, 1:2], wsc[0:1, 0:1])
                    vMSC.inc(ins)
                    nc.tensor.wait_ge(vMSC.h, vMSC.n)
                    nc.tensor.matmul(yTp[:, 1:2], ones_r[:, :], wsc[0:1, 0:1],
                                     start=True, stop=True)
                    ins = nc.tensor.matmul(yTp[:, 2:3], ones_r[:, :], wsc[0:1, 1:2],
                                           start=True, stop=True)
                    tMSC.inc(ins)
                    nc.vector.wait_ge(tMSC.h, tMSC.n)
                    vx("tensor_copy", SCAL[:, m_slot:m_slot + 1], yTp[:, 1:2])
                    ins = vx("tensor_copy", SCAL[:, s_slot:s_slot + 1], yTp[:, 2:3])
                    vMSC.inc(ins)
                    fin_mark[nm] = vMSC.n

                wsched = {0: list(range(0, 96)), 1: list(range(96, 192)),
                          2: list(range(192, 304)), 3: list(range(304, 384))}
                wfin = {1: "g", 2: "u", 3: "d"}
                if not (_PH & 1):
                    wsched = {0: [], 1: [], 2: [], 3: []}
                    wfin = {}
                else:
                    w_issue_dma(0)
                    w_issue_dma(1)
                    w_issue_dma(2)

                deferred_evac = None
                stats_osb_mark = [0]
                for beta in range(NPAIR):
                    xf = xfms[beta % 2]
                    QzA, QzB = QzAs[beta % 2], QzBs[beta % 2]
                    Kt, Vt = Kts[beta % 2], Vts[beta % 2]
                    yT = yTs[beta % 2]

                    if beta in wfin:
                        w_finalize(wfin[beta], stt_at_pair[beta - 1])
                    wq_todo = [(lambda k=k: w_step(k)) for k in wsched.get(beta, [])]
                    wq_point = [0]
                    while pending_stats:
                        wq_todo.insert(0, pending_stats.pop())
                    _npts = 52 if beta == NPAIR - 1 else 64
                    _nw = len(wq_todo)
                    wq_counts = [_nw * (i + 1) // _npts - _nw * i // _npts
                                 for i in range(_npts)] + [0] * 16
                    def wq_drain():
                        take = wq_counts[wq_point[0]]
                        wq_point[0] += 1
                        for _ in range(take):
                            if wq_todo:
                                wq_todo.pop(0)()

                    nc.tensor.wait_ge(dXF.h, 16 * (beta + 1))
                    if beta == 0:
                        nc.tensor.wait_ge(vMSC.h, vMSC.n)
                        nc.tensor.wait_ge(dC.h, dC.n)
                    if beta >= 1:
                        nc.vector.wait_ge(dBQ.h, dBQ.n)
                        nc.vector.wait_ge(dBK.h, dBK.n)
                        if beta >= 2:
                            nc.vector.wait_ge(sEXP.h, exp_at_pair[beta - 2])
                    kslot = 0
                    for (wsb, which) in ((wq_sb, "q"), (wk_sb, "k"), (wv_sb, "v")):
                        if beta == 0:
                            nc.tensor.wait_ge(dWA[{"q": 0, "k": 1, "v": 2}[which]].h, 16)
                        for i in range(8):
                            pt, off = pslot(kslot)
                            if beta > 0 and kslot < 4:
                                nc.tensor.wait_ge(sEXP.h, exp_at_pair[beta - 1])
                            if slot_evt[kslot % 4] > 0:
                                nc.tensor.wait_ge(vPEV.h, slot_evt[kslot % 4])
                            ins = None
                            for j in range(8):
                                ins = nc.tensor.matmul(pt[:, off:off + 512],
                                                       wsb[:, j, i * 128:(i + 1) * 128],
                                                       xf[:, j, :],
                                                       start=(j == 0), stop=(j == 7))
                            tPRJ.inc(ins)
                            nc.vector.wait_ge(tPRJ.h, tPRJ.n)
                            if which == "v":
                                ins = vx("tensor_copy", Vt[:, i, :], pt[:, off:off + 512])
                            elif which == "k":
                                vx("tensor_copy", Kt[0:64, (2 * i) * 256:(2 * i) * 256 + 256],
                                   pt[0:64, off:off + 256])
                                vx("tensor_copy", Kt[64:128, (2 * i + 1) * 256:(2 * i + 1) * 256 + 256],
                                   pt[64:128, off + 256:off + 512])
                                vx("tensor_copy", bQK[0:64, i, 256:512], pt[0:64, off + 256:off + 512])
                                ins = vx("tensor_copy", bQK[64:128, i, 256:512], pt[64:128, off:off + 256])
                            else:
                                vx("tensor_copy", QzA[0:64, (2 * i) * 256:(2 * i) * 256 + 256],
                                   pt[0:64, off:off + 256])
                                vx("tensor_copy", QzB[64:128, (2 * i + 1) * 256:(2 * i + 1) * 256 + 256],
                                   pt[64:128, off + 256:off + 512])
                                vx("tensor_copy", bQK[0:64, i, 0:256], pt[0:64, off + 256:off + 512])
                                ins = vx("tensor_copy", bQK[64:128, i, 0:256], pt[64:128, off:off + 256])
                            slot_evt[kslot % 4] = vPEV.inc(ins)
                            kslot += 1
                    nc.sync.wait_ge(vPEV.h, vPEV.n)
                    dst_e = Kt[64:128, :].rearrange("p (i q s) -> p i q s", q=2, s=256)[:, :, 0, :]
                    nc.sync.dma_start(dst_e, bQK[0:64, :, 256:512]).then_inc(dBK.h, 16); dBK.n += 16
                    dst_o = Kt[0:64, :].rearrange("p (i q s) -> p i q s", q=2, s=256)[:, :, 1, :]
                    nc.sync.dma_start(dst_o, bQK[64:128, :, 256:512]).then_inc(dBK.h, 16); dBK.n += 16
                    dst_e = QzB[64:128, :].rearrange("p (i q s) -> p i q s", q=2, s=256)[:, :, 0, :]
                    nc.sync.dma_start(dst_e, bQK[0:64, :, 0:256]).then_inc(dBQ.h, 16); dBQ.n += 16
                    dst_o = QzA[0:64, :].rearrange("p (i q s) -> p i q s", q=2, s=256)[:, :, 1, :]
                    nc.sync.dma_start(dst_o, bQK[64:128, :, 0:256]).then_inc(dBQ.h, 16); dBQ.n += 16

                    nc.tensor.wait_ge(vPEV.h, vPEV.n)
                    for h in (range(2) if _P1L >= 2 else []):
                        Vp = Vps_[h]
                        prev_m2 = m2_at_block.get((beta - 1, h), 0)
                        for batch in range(2):
                            nc.tensor.wait_ge(vVEV.h, vVEV.n)
                            ins = None
                            for s in range(8):
                                i = batch * 4 + s // 2
                                kh = s % 2
                                ins = nc.tensor.transpose(
                                    Vps[:, s * 128:(s + 1) * 128],
                                    Vt[:, i, h * 256 + kh * 128:h * 256 + (kh + 1) * 128],
                                    identb[:, :])
                            tVTR.inc(ins)
                            nc.vector.wait_ge(tVTR.h, tVTR.n)
                            if prev_m2 > 0 and batch == 0:
                                nc.vector.wait_ge(tM2.h, prev_m2)
                            srcv = Vps[:, :].rearrange("p (a b c) -> p a b c", b=2, c=128)
                            dstv = Vp[:, 16 * batch:16 * batch + 16, 0:64].rearrange(
                                "p (a r) c -> p a r c", r=4)
                            vx("tensor_copy", dstv[:, :, 0:2, :], srcv[:, :, :, 0:64])
                            ins = vx("tensor_copy", dstv[:, :, 2:4, :], srcv[:, :, :, 64:128])
                            vVEV.inc(ins)
                        vev_at_block[(beta, h)] = vVEV.n

                    def mk_evac(beta, h, qg, yT):
                        def run():
                            for w in range(2):
                                nc.tensor.wait_ge(sOCP.h, sOCP.n)
                                if tYTR.n >= 1:
                                    nc.tensor.wait_ge(vYST.h, vYST.n)
                                ins = None
                                for pp in range(4):
                                    p = w * 4 + pp
                                    ins = nc.tensor.transpose(yTp[:, pp * 128:pp * 128 + 65],
                                                              Osb[0:65, p * 128:(p + 1) * 128],
                                                              identf[0:65, 0:65])
                                tYTR.inc(ins)
                                nc.vector.wait_ge(tYTR.h, tYTR.n)
                                if beta == 0 and h == 0 and qg == 0 and w == 0:
                                    nc.vector.wait_ge(dXT.h, dXT.n)
                                if beta >= 1 and h == 0 and qg == 0 and w == 0:
                                    if beta >= 2:
                                        pp_, vv_ = spill_at_pair[beta - 2]
                                        nc.vector.wait_ge(dSPL[pp_].h, vv_)
                                    nc.vector.wait_ge(dXT.h, 16 * (beta + 1))
                                ins = None
                                for pp in range(4):
                                    p = w * 4 + pp
                                    qk = 4 * qg + p // 2
                                    g = 2 * h + (p % 2)
                                    c0 = pp * 128
                                    vx("reciprocal", rec[:, p:p + 1], yTp[:, c0 + 64:c0 + 65])
                                    ins = vx("scalar_tensor_tensor",
                                        yT[:, g, qk * 64:(qk + 1) * 64],
                                        yTp[:, c0:c0 + 64], rec[:, p:p + 1],
                                        xtok[:, g, qk * 64:(qk + 1) * 64],
                                        ALU.mult, ALU.add)
                                vYST.inc(ins)
                            if beta == NPAIR - 1 and qg == NQG - 1:
                                mk_stats_g(beta, yT, 2 * h)()
                                mk_stats_g(beta, yT, 2 * h + 1)()
                        return run
                    for h in (range(2) if _P1L >= 3 else []):
                        o = h * 64
                        Vp = Vps_[h]
                        nc.tensor.wait_ge(vVEV.h, vev_at_block[(beta, h)])
                        if h == 0:
                            nc.tensor.wait_ge(dBQ.h, dBQ.n)
                            nc.tensor.wait_ge(dBK.h, dBK.n)
                        Qz = QzA if h == 0 else QzB
                        for qg in range(NQG):
                            LAG = 2
                            e_base = exp_cnt
                            for kc in range(NKC + LAG):
                                if kc == 1 and deferred_evac is not None:
                                    deferred_evac()
                                    deferred_evac = None
                                if kc < NKC:
                                    e = exp_cnt + 1
                                    si = (e - 1) % 2
                                    Sb = Ss[si]
                                    if e > 2:
                                        nc.tensor.wait_ge(sEXP.h, e - 2)
                                    nc.tensor.matmul(Sb[:, 0:512],
                                                     Kt[:, kc * 128:(kc + 1) * 128],
                                                     Qz[:, qg * 1024:qg * 1024 + 512],
                                                     start=True, stop=True)
                                    ins = nc.tensor.matmul(Sb[:, 512:1024],
                                                           Kt[:, kc * 128:(kc + 1) * 128],
                                                           Qz[:, qg * 1024 + 512:(qg + 1) * 1024],
                                                           start=True, stop=True)
                                    m1_cnt = tM1.inc(ins)
                                    nc.scalar.wait_ge(tM1.h, m1_cnt)
                                    if e > 3:
                                        nc.scalar.wait_ge(tM2.h, e - 3)
                                    if e == 1:
                                        nc.scalar.wait_ge(dC.h, dC.n)
                                    ins = nc.scalar.activation(Ps[(e - 1) % 3][:, :], Sb[:, :], AF.Exp,
                                                               bias=expb[:, 0:1], scale=1.0)
                                    exp_cnt = sEXP.inc(ins)
                                if _P1L < 4:
                                    continue
                                kc2 = kc - LAG
                                if kc2 < 0:
                                    continue
                                x2 = e_base + kc2 + 1
                                if kc >= NKC:
                                    nc.tensor.wait_ge(sEXP.h, x2)
                                if kc2 == 0:
                                    nc.tensor.wait_ge(sOCP.h, sOCP.n)
                                Pb = Ps[(x2 - 1) % 3]
                                nc.tensor.matmul(Ops[0:65, 0:512], Vp[:, kc2, 0:65],
                                                 Pb[:, 0:512],
                                                 start=(kc2 == 0), stop=(kc2 == NKC - 1))
                                ins = nc.tensor.matmul(Ops[0:65, 512:1024], Vp[:, kc2, 0:65],
                                                       Pb[:, 512:1024],
                                                       start=(kc2 == 0), stop=(kc2 == NKC - 1))
                                m2_cnt = tM2.inc(ins)
                                if kc2 % 4 == 3:
                                    wq_drain()
                            if _P1L < 5:
                                continue
                            nc.scalar.wait_ge(tM2.h, m2_cnt)
                            if stats_osb_mark[0] > 0:
                                nc.scalar.wait_ge(vORD.h, stats_osb_mark[0])
                            ins = nc.scalar.activation(Osb[0:65, :], Ops[0:65, :], AF.Copy)
                            sOCP.inc(ins)
                            deferred_evac = mk_evac(beta, h, qg, yT)
                        m2_at_block[(beta, h)] = m2_cnt
                    if deferred_evac is not None:
                        deferred_evac()
                        deferred_evac = None
                    exp_at_pair.append(exp_cnt)

                    def mk_stats_g(beta, yT, g):
                        def run():
                            vx("tensor_mul", Osb[:, 0:1024], yT[:, g, :], yT[:, g, :])
                            vx("tensor_reduce", rec[:, 0:1], yT[:, g, :], AX.X, ALU.add)
                            vx("tensor_add", sacc[:, :], sacc[:, :], rec[:, 0:1])
                            vx("tensor_reduce", rec[:, 1:2], Osb[:, 0:1024], AX.X, ALU.add)
                            ins = vx("tensor_add", sqacc[:, :], sqacc[:, :], rec[:, 1:2])
                            stats_osb_mark[0] = vORD.n
                            if g == 3:
                                vSTA.inc(ins)
                                dsp = dSPL[beta % 2]
                                nc.sync.wait_ge(vSTA.h, vSTA.n)
                                nc.sync.dma_start(
                                    y_d[512 * beta:512 * (beta + 1), :].rearrange("(g p) d -> p g d", p=128),
                                    yT[:, :, :]).then_inc(dsp.h, 16); dsp.n += 16
                                spill_at_pair.append((beta % 2, dsp.n))
                        return run
                    if _P1L >= 6 and beta < NPAIR - 1:
                        for g in range(4):
                            pending_stats.append(mk_stats_g(beta, yT, g))
                    stt_at_pair.append(vYST.n)

                    if beta + 1 < NPAIR:
                        t0 = 512 * (beta + 1)
                        nc.gpsimd.wait_ge(tPRJ.h, tPRJ.n)
                        nc.gpsimd.dma_start(
                            xfms[(beta + 1) % 2][:, :, :],
                            xT.rearrange("(j p) t -> p j t", p=128)[:, :, t0:t0 + 512]).then_inc(dXF.h, 16); dXF.n += 16
                        nc.sync.wait_ge(vYST.h, stt_at_pair[beta])
                        nc.sync.dma_start(
                            xtok[:, :, :],
                            xN[t0:t0 + 512, :].rearrange("(g p) d -> p g d", p=128)).then_inc(dXT.h, 16); dXT.n += 16

                while pending_stats:
                    pending_stats.pop(0)()

        phase2_end = vMSC.n
        attn_T_prj = tPRJ.n
        attn_T_m1 = tM1.n
        attn_T_m2 = tM2.n
        attn_T_ytr = tYTR.n
        attn_T_vtr = tVTR.n
        attn_V_end = vSTA.n
        if _PH & 4:
            with (
                nc.psum_tensor("pb2", [128, 512], F32) as pb2,
            ):
                nc.tensor.wait_ge(vSTA.h, vSTA.n)
                nc.tensor.matmul(pb2[0:1, 0:1], sacc[:, :], ones_c[:, :], start=True, stop=True)
                ins = nc.tensor.matmul(pb2[0:1, 1:2], sqacc[:, :], ones_c[:, :],
                                       start=True, stop=True)
                tMSC.inc(ins)
                nc.vector.wait_ge(tMSC.h, tMSC.n)
                ins = vx("tensor_copy", ccs[0:1, 0:2], pb2[0:1, 0:2])
                vMSC.inc(ins)
                nc.sync.wait_ge(vMSC.h, vMSC.n)
                nc.sync.dma_start(cc_in, ccs[0:1, 0:2]).then_inc(dCC.h, 16); dCC.n += 16
                nc.gpsimd.wait_ge(dCC.h, dCC.n)
                ins = nc.gpsimd.collective_compute(
                    "AllReduce", ALU.add,
                    replica_groups=[[0, 1], [2, 3], [4, 5], [6, 7]],
                    ins=[cc_in], outs=[cc_out],
                )
                gCC.inc(ins)
                nc.sync.wait_ge(gCC.h, gCC.n)
                nc.sync.dma_start(ccs[0:1, 0:2], cc_out).then_inc(dCC.h, 16); dCC.n += 16
                nc.vector.wait_ge(dCC.h, dCC.n)
                vx("tensor_scalar_mul", ccs[0:1, 0:1], ccs[0:1, 0:1], 1.0 / NB_ELEM)
                vx("tensor_scalar_mul", ccs[0:1, 1:2], ccs[0:1, 1:2], 1.0 / NB_ELEM)
                vx("tensor_mul", ccs[0:1, 2:3], ccs[0:1, 0:1], ccs[0:1, 0:1])
                vx("tensor_sub", ccs[0:1, 1:2], ccs[0:1, 1:2], ccs[0:1, 2:3])
                ins = vx("tensor_scalar_add", ccs[0:1, 1:2], ccs[0:1, 1:2], EPS)
                vMSC.inc(ins)
                nc.scalar.wait_ge(vMSC.h, vMSC.n)
                ins = nc.scalar.activation(ccs[0:1, 2:3], ccs[0:1, 1:2], AF.Sqrt)
                sMSC.inc(ins)
                nc.vector.wait_ge(sMSC.h, sMSC.n)
                ins = vx("reciprocal", ccs[0:1, 3:4], ccs[0:1, 2:3])
                vMSC.inc(ins)
                nc.tensor.wait_ge(vMSC.h, vMSC.n)
                ins = nc.tensor.matmul(pb2[:, 3:4], ones_r[:, :], ccs[0:1, 3:4],
                                       start=True, stop=True)
                tMSC.inc(ins)
                nc.vector.wait_ge(tMSC.h, tMSC.n)
                ins = vx("tensor_copy", SCAL[:, 6:7], pb2[:, 3:4])
                vMSC.inc(ins)
            phase2_end = vMSC.n

        if _PH & 8:
            with ExitStack() as ph3:
                xz2 = ph3.enter_context(nc.sbuf_tensor("xz2", [128, 4, D], F32))
                zt = ph3.enter_context(nc.sbuf_tensor("zt", [128, 4, D], F32))
                tmpf = ph3.enter_context(nc.sbuf_tensor("tmpf", [128, 2048], F32))
                tmpb = ph3.enter_context(nc.sbuf_tensor("tmpb", [128, INNER], BF16))
                xqT = ph3.enter_context(nc.sbuf_tensor("xqT", [128, 8, 512], BF16))
                hsb = ph3.enter_context(nc.sbuf_tensor("hsb", [128, 4, INNER], F32))
                hqT = ph3.enter_context(nc.sbuf_tensor("hqT", [128, 32, 512], BF16))
                wgu0 = ph3.enter_context(nc.sbuf_tensor("wgu0", [128, 16, 512], BF16))
                wgu1 = ph3.enter_context(nc.sbuf_tensor("wgu1", [128, 16, 512], BF16))
                wd0 = ph3.enter_context(nc.sbuf_tensor("wd0", [128, 4, 1024], BF16))
                wd1 = ph3.enter_context(nc.sbuf_tensor("wd1", [128, 4, 1024], BF16))
                gs = ph3.enter_context(nc.sbuf_tensor("gs", [128, 512], F32))
                st8 = ph3.enter_context(nc.sbuf_tensor("st8", [128, 16], F32))
                pg8 = ph3.enter_context(nc.sbuf_tensor("pg8", [128, 12], F32))
                ah4 = ph3.enter_context(nc.sbuf_tensor("ah4", [128, 4], F32))
                ln6 = ph3.enter_context(nc.sbuf_tensor("ln6", [128, 12], F32))
                ln8 = ph3.enter_context(nc.sbuf_tensor("ln8", [128, 8], F32))
                lnS = ph3.enter_context(nc.sbuf_tensor("lnS", [128, 8], F32))
                PA = ph3.enter_context(nc.psum_tensor("PA", [128, 1024], F32))
                PB = ph3.enter_context(nc.psum_tensor("PB", [128, 1024], F32))
                PC = ph3.enter_context(nc.psum_tensor("PC", [128, 1024], F32))
                ptb = ph3.enter_context(nc.psum_tensor("ptb", [128, 1024], BF16))

                wgus = [wgu0, wgu1]
                wds = [wd0, wd1]
                gu_slab_mark = [0, 0]
                dwn_slab_mark = [0, 0]
                gu_order_st = [[2, 3, 4, 5, 6, 7, 0, 1]]
                dwn_order_st = [[2, 3, 4, 5, 6, 7, 0, 1]]
                z2ev_mark = [0]
                PAB = [PA, PB]
                pab_sil = [0, 0]
                pab_hst = [0, 0]
                pab_z2 = 0

                xqt_mark = {}
                scg_mark = {}
                pkq = []

                def issue_zt(tau_, pairg, mark=None):
                    r0 = 512 * tau_ + 256 * pairg
                    ga = 2 * pairg
                    dyp = dY if pairg == 0 else dYb
                    nc.gpsimd.wait_ge(vZ2.h, vZ2.n if mark is None else mark)
                    nc.gpsimd.dma_start(
                        zt[:, ga:ga + 2, :],
                        y_d[r0:r0 + 256, :].rearrange("(g p) d -> p g d", p=128)
                    ).then_inc(dyp.h, 16); dyp.n += 16

                def issue_xz2(tau_, pairg):
                    r0 = 512 * tau_ + 256 * pairg
                    ga = 2 * pairg
                    dxp = dXN if pairg == 0 else dXNb
                    nc.gpsimd.wait_ge(vZ2.h, vZ2.n)
                    nc.gpsimd.dma_start(
                        xz2[:, ga:ga + 2, :],
                        xN[r0:r0 + 256, :].rearrange("(g p) d -> p g d", p=128)
                    ).then_inc(dxp.h, 16); dxp.n += 16

                def issue_loads(tau_, pairg):
                    issue_zt(tau_, pairg)
                    issue_xz2(tau_, pairg)

                def issue_park_x(tau_):
                    r0 = 512 * tau_
                    nc.gpsimd.wait_ge(vQNT.h, vQNT.n)
                    nc.gpsimd.dma_start(
                        hsb[:, 0:2, 1024:2048],
                        xN[r0:r0 + 256, :].rearrange("(g p) d -> p g d", p=128)
                    ).then_inc(dXN.h, 16); dXN.n += 16

                def issue_park_y(tau_):
                    r0 = 512 * tau_
                    nc.gpsimd.wait_ge(vQNT.h, vQNT.n)
                    nc.gpsimd.dma_start(
                        hsb[:, 2:4, 1024:2048],
                        y_d[r0:r0 + 256, :].rearrange("(g p) d -> p g d", p=128)
                    ).then_inc(dY.h, 16); dY.n += 16

                def mk_ln1_closures(pairg, parked=False):
                    ga, gb = 2 * pairg, 2 * pairg + 1
                    (dyp, dxp) = (dY, dXN) if pairg == 0 else (dYb, dXNb)
                    def c1():
                        nc.vector.wait_ge(dyp.h, dyp.n)
                        nc.vector.wait_ge(dxp.h, dxp.n)
                        for i, g in enumerate((ga, gb)):
                            if parked:
                                vx("scalar_tensor_tensor", zt[:, g, :],
                                   hsb[:, 2 + i, 1024:2048], SCAL[:, 6:7],
                                   hsb[:, i, 1024:2048], ALU.mult, ALU.add)
                            else:
                                vx("scalar_tensor_tensor", zt[:, g, :], zt[:, g, :],
                                   SCAL[:, 6:7], xz2[:, g, :], ALU.mult, ALU.add)
                        for g in (ga, gb):
                            vx("bn_stats", ln6[:, 0:6], zt[:, g, 0:512])
                            vx("bn_stats", ln6[:, 6:12], zt[:, g, 512:1024])
                            vx("bn_aggr", ln8[:, 2 * g:2 * g + 2], ln6[:, :])
                    def c2():
                        for i, g in ((0, ga), (1, gb)):
                            vx("tensor_reduce", st8[:, 0 + i:1 + i], zt[:, g, :], AX.X, ALU.max)
                            vx("tensor_reduce", st8[:, 2 + i:3 + i], zt[:, g, :], AX.X, ALU.min)
                        lpr = ln8[:, 2 * ga:2 * ga + 4].rearrange(
                            "p (g two) -> p two g", two=2)
                        vx("tensor_copy", pg8[:, ga:ga + 2], lpr[:, 0, :])
                        ins = vx("tensor_scalar_add", st8[:, 4:6], lpr[:, 1, :], EPS)
                        vMSC.inc(ins)
                        nc.scalar.wait_ge(vMSC.h, vMSC.n)
                        ins = nc.scalar.activation(pg8[:, 4 + ga:4 + ga + 2], st8[:, 4:6],
                                                   AF.Sqrt)
                        sMSC.inc(ins)
                    def c3():
                        nc.vector.wait_ge(sMSC.h, sMSC.n)
                        vx("reciprocal", pg8[:, 4 + ga:4 + ga + 2], pg8[:, 4 + ga:4 + ga + 2])
                        vx("tensor_sub", st8[:, 0:2], st8[:, 0:2], pg8[:, ga:ga + 2])
                        vx("tensor_sub", st8[:, 2:4], pg8[:, ga:ga + 2], st8[:, 2:4])
                        vx("tensor_max", st8[:, 0:2], st8[:, 0:2], st8[:, 2:4])
                        vx("tensor_mul", st8[:, 0:2], st8[:, 0:2], pg8[:, 4 + ga:4 + ga + 2])
                        vx("tensor_scalar_max", st8[:, 0:2], st8[:, 0:2], 1e-5)
                        vx("reciprocal", st8[:, 2:4], st8[:, 0:2])
                        vx("tensor_scalar_mul", st8[:, 2:4], st8[:, 2:4], 127.0)
                        vx("tensor_scalar_mul", st8[:, 4:6], st8[:, 0:2], 1.0 / 127.0)
                        for i, g in enumerate((ga, gb)):
                            vx("tensor_mul", st8[:, 8 + g:9 + g], st8[:, 4 + i:5 + i],
                               SCAL[:, 3:4])
                            vx("tensor_mul", st8[:, 12 + g:13 + g], st8[:, 4 + i:5 + i],
                               SCAL[:, 4:5])
                        scg_mark[ga] = scg_mark[gb] = vORD.n
                        vx("tensor_mul", st8[:, 0:2], pg8[:, 4 + ga:4 + ga + 2], st8[:, 2:4])
                        vx("tensor_mul", st8[:, 2:4], pg8[:, ga:ga + 2], st8[:, 0:2])
                        vx("tensor_scalar", st8[:, 2:4], st8[:, 2:4], -1.0, MAGIC,
                           ALU.mult, ALU.add)
                    def mk_quant(i, g):
                        def cq():
                            vx("tensor_scalar", tmpf[:, 0:D], zt[:, g, :],
                               st8[:, 0 + i:1 + i], st8[:, 2 + i:3 + i], ALU.mult, ALU.add)
                            vx("tensor_scalar", tmpf[:, 0:D], tmpf[:, 0:D], -MAGIC, 127.0,
                                                    ALU.add, ALU.min)
                            ins = vx("tensor_scalar", tmpb[:, 0:D], tmpf[:, 0:D],
                                     -128.0, None, ALU.max)
                            vQNT.inc(ins)
                        return cq
                    def mk_tr(g):
                        def ct():
                            nc.tensor.wait_ge(vQNT.h, vQNT.n)
                            if tTRX.n > 0:
                                nc.tensor.wait_ge(vTEV.h, vTEV.n)
                            ins = None
                            for jj in range(8):
                                ins = nc.tensor.transpose(ptb[:, jj * 128:(jj + 1) * 128],
                                                          tmpb[:, jj * 128:(jj + 1) * 128],
                                                          identb[:, :])
                            tTRX.inc(ins)
                            nc.vector.wait_ge(tTRX.h, tTRX.n)
                            nc.vector.wait_ge(tGU.h, tGU.n)
                            ins = vx("tensor_copy",
                                xqT[:, :, g * 128:(g + 1) * 128],
                                ptb[:, :].rearrange("p (j t) -> p j t", j=8))
                            vTEV.inc(ins)
                            xqt_mark[g] = vTEV.n
                        return ct
                    return [c1, c2, c3, mk_quant(0, ga), mk_tr(ga),
                            mk_quant(1, gb), mk_tr(gb)]

                def ln2_pair(tau_, p):
                    t0_ = 512 * tau_
                    ga, gb = 2 * p, 2 * p + 1
                    for i, g in ((0, ga), (1, gb)):
                        vx("bn_stats", ln6[:, 0:6], xz2[:, g, 0:512])
                        vx("bn_stats", ln6[:, 6:12], xz2[:, g, 512:1024])
                        vx("bn_aggr", lnS[:, 2 * i:2 * i + 2], ln6[:, :])
                    lp = lnS[:, 0:4].rearrange("p (g two) -> p two g", two=2)
                    ins = vx("tensor_scalar_add", lnS[:, 4:6], lp[:, 1, :], EPS)
                    vMSC.inc(ins)
                    nc.scalar.wait_ge(vMSC.h, vMSC.n)
                    ins = nc.scalar.activation(lnS[:, 6:8], lnS[:, 4:6], AF.Sqrt)
                    sMSC.inc(ins)
                    nc.vector.wait_ge(sMSC.h, sMSC.n)
                    vx("reciprocal", lnS[:, 6:8], lnS[:, 6:8])
                    for i, g in ((0, ga), (1, gb)):
                        ins = vx("tensor_scalar", hsb[:, g, 0:D], xz2[:, g, :],
                                 lnS[:, 2 * i:2 * i + 1], lnS[:, 6 + i:7 + i],
                                 ALU.subtract, ALU.mult)
                        vZ2.inc(ins)
                        nc.sync.wait_ge(vZ2.h, vZ2.n)
                        nc.sync.dma_start(out[t0_ + g * 128:t0_ + (g + 1) * 128, :],
                                          hsb[:, g, 0:D]).then_inc(dOUT.h, 16); dOUT.n += 16

                for tau in range(4):
                    t0 = 512 * tau
                    if tau == 0:
                        nc.sync.wait_ge(tPRJ.h, attn_T_prj)
                        nc.sync.wait_ge(tM1.h, attn_T_m1)
                        nc.sync.wait_ge(tM2.h, attn_T_m2)
                        nc.sync.wait_ge(tYTR.h, attn_T_ytr)
                        nc.sync.wait_ge(tVTR.h, attn_T_vtr)
                        nc.sync.wait_ge(vORD.h, attn_V_end)
                        nc.sync.wait_ge(dWQS[0].h, dWQS[0].n)
                        nc.sync.wait_ge(dWQS[1].h, dWQS[1].n)
                        nc.sync.wait_ge(dSPL[0].h, dSPL[0].n)
                        nc.sync.wait_ge(dSPL[1].h, dSPL[1].n)
                        nc.gpsimd.wait_ge(tPRJ.h, attn_T_prj)
                        nc.gpsimd.wait_ge(tM1.h, attn_T_m1)
                        nc.gpsimd.wait_ge(tM2.h, attn_T_m2)
                        nc.gpsimd.wait_ge(tYTR.h, attn_T_ytr)
                        nc.gpsimd.wait_ge(tVTR.h, attn_T_vtr)
                        nc.gpsimd.wait_ge(vORD.h, attn_V_end)
                        nc.gpsimd.wait_ge(dWQS[0].h, dWQS[0].n)
                        nc.gpsimd.wait_ge(dWQS[1].h, dWQS[1].n)
                        nc.gpsimd.wait_ge(dSPL[0].h, dSPL[0].n)
                        nc.gpsimd.wait_ge(dSPL[1].h, dSPL[1].n)
                        issue_loads(0, 0)
                        issue_loads(0, 1)
                        for st_ in mk_ln1_closures(0):
                            st_()
                        pkq = mk_ln1_closures(1)
                    else:
                        issue_xz2(tau, 1)
                        pkq = mk_ln1_closures(1)

                    def gu_slab(dst_par, ibig2, eng=None):
                        eng = eng or nc.sync
                        eng.wait_ge(tGU.h, gu_slab_mark[dst_par])
                        eng.wait_ge(dGU[dst_par].h, dGU[dst_par].n)
                        eng.dma_start(wgus[dst_par][:, :, :],
                                      wguq_d[:, :, ibig2 * 512:(ibig2 + 1) * 512]
                                      .rearrange("j p i -> p j i")
                                      ).then_inc(dGU[dst_par].h, 16)
                        dGU[dst_par].n += 16

                    def wd_slab(dst_par, sl_):
                        nc.sync.wait_ge(tDWN.h, dwn_slab_mark[dst_par])
                        nc.sync.wait_ge(dWD[dst_par].h, dWD[dst_par].n)
                        nc.sync.dma_start(
                            wds[dst_par][:, 0:2, :],
                            wdq_d[4 * sl_:4 * sl_ + 2, :, :]
                            .rearrange("i p d -> p i d")).then_inc(dWD[dst_par].h, 16)
                        dWD[dst_par].n += 16
                        nc.gpsimd.wait_ge(tDWN.h, dwn_slab_mark[dst_par])
                        nc.gpsimd.wait_ge(dWDg[dst_par].h, dWDg[dst_par].n)
                        nc.gpsimd.dma_start(
                            wds[dst_par][:, 2:4, :],
                            wdq_d[4 * sl_ + 2:4 * sl_ + 4, :, :]
                            .rearrange("i p d -> p i d")).then_inc(dWDg[dst_par].h, 16)
                        dWDg[dst_par].n += 16

                    def mk_hq_steps(g):
                        def s_scale(g=g):
                            vx("tensor_scalar_max", st8[:, 0:1], ah4[:, g:g + 1], 1e-5)
                            vx("reciprocal", st8[:, 1:2], st8[:, 0:1])
                            vx("tensor_scalar_mul", st8[:, 2:3], st8[:, 0:1], 1.0 / 127.0)
                            vx("tensor_mul", pg8[:, 8 + g:9 + g], st8[:, 2:3], SCAL[:, 5:6])
                            vx("tensor_scalar_mul", ah4[:, g:g + 1], st8[:, 1:2], 127.0)
                        yield s_scale
                        for qq in range(4):
                            def s_ts(g=g, qq=qq):
                                s0_, s1_ = qq * 1024, (qq + 1) * 1024
                                vx("tensor_scalar", tmpf[:, 0:1024], hsb[:, g, s0_:s1_],
                                   ah4[:, g:g + 1], MAGIC, ALU.mult, ALU.add)
                                vx("tensor_scalar", tmpf[:, 0:1024], tmpf[:, 0:1024],
                                   -MAGIC, 127.0, ALU.add, ALU.min)
                                ins = vx("tensor_scalar", tmpb[:, s0_:s1_], tmpf[:, 0:1024],
                                         -128.0, None, ALU.max)
                                vQNT.inc(ins)
                            yield s_ts
                        for grp in range(4):
                            def s_tr(g=g, grp=grp):
                                nc.tensor.wait_ge(vQNT.h, vQNT.n)
                                nc.tensor.wait_ge(vTEV.h, vTEV.n)
                                ins = None
                                for jj in range(8):
                                    i2 = grp * 8 + jj
                                    ins = nc.tensor.transpose(ptb[:, jj * 128:(jj + 1) * 128],
                                                              tmpb[:, i2 * 128:(i2 + 1) * 128],
                                                              identb[:, :])
                                tTRX.inc(ins)
                                nc.vector.wait_ge(tTRX.h, tTRX.n)
                                ins = vx("tensor_copy",
                                         hqT[:, grp * 8:(grp + 1) * 8, g * 128:(g + 1) * 128],
                                         ptb[:, :].rearrange("p (j t) -> p j t", j=8))
                                vTEV.inc(ins)
                            yield s_tr

                    def gu_pass(half):
                        nc.vector.wait_ge(dOUT.h, dOUT.n)
                        prev = gu_order_st[0]
                        order = [prev[6], prev[7]] + [s for s in range(8)
                                                      if s not in (prev[6], prev[7])]
                        gu_order_st[0] = order
                        for pos in range(8):
                            par = pos % 2
                            ibig = order[pos]
                            if 1 <= pos and pos + 1 < 8:
                                gu_slab((pos + 1) % 2, order[pos + 1])
                            sl = wgus[par]
                            nc.tensor.wait_ge(dGU[par].h, dGU[par].n)
                            for g in (2 * half, 2 * half + 1):
                                pt = PAB[g % 2]
                                if pab_sil[g % 2] > 0:
                                    nc.tensor.wait_ge(sSIL.h, pab_sil[g % 2])
                                if pab_hst[g % 2] > 0:
                                    nc.tensor.wait_ge(vHST.h, pab_hst[g % 2])
                                if pos == 0:
                                    nc.tensor.wait_ge(vTEV.h, xqt_mark[g])
                                    if g == 2 * half:
                                        nc.tensor.wait_ge(vZ2.h, z2ev_mark[0])
                                ins = None
                                for j in range(8):
                                    nc.tensor.matmul(pt[:, 0:512],
                                                     xqT[:, j, g * 128:(g + 1) * 128],
                                                     sl[:, j, :],
                                                     start=(j == 0), stop=(j == 7))
                                    ins = nc.tensor.matmul(pt[:, 512:1024],
                                                           xqT[:, j, g * 128:(g + 1) * 128],
                                                           sl[:, 8 + j, :],
                                                           start=(j == 0), stop=(j == 7))
                                tGU.inc(ins)
                                nc.scalar.wait_ge(tGU.h, tGU.n)
                                if pos == 0:
                                    nc.scalar.wait_ge(vORD.h, scg_mark[g])
                                if sSIL.n > 0:
                                    nc.scalar.wait_ge(vHST.h, vHST.n)
                                ins = nc.scalar.activation(gs[:, 0:512], pt[:, 0:512], AF.Silu,
                                                           scale=st8[:, 8 + g:9 + g])
                                pab_sil[g % 2] = sSIL.inc(ins)
                                nc.vector.wait_ge(sSIL.h, sSIL.n)
                                vx("scalar_tensor_tensor",
                                   hsb[:, g, ibig * 512:(ibig + 1) * 512],
                                   pt[:, 512:1024], st8[:, 12 + g:13 + g], gs[:, 0:512],
                                   ALU.mult, ALU.mult)
                                vx("tensor_reduce", st8[:, 6:7],
                                   hsb[:, g, ibig * 512:(ibig + 1) * 512],
                                   AX.X, ALU.max, apply_absolute_value=True)
                                if pos == 0:
                                    ins = vx("tensor_copy", ah4[:, g:g + 1], st8[:, 6:7])
                                else:
                                    ins = vx("tensor_max", ah4[:, g:g + 1], ah4[:, g:g + 1], st8[:, 6:7])
                                pab_hst[g % 2] = vHST.inc(ins)
                                if pkq:
                                    pkq.pop(0)()
                            gu_slab_mark[par] = tGU.n

                    def down_pass(half, hq_mark):
                        z2p = {2 * half: PC, 2 * half + 1: PA}
                        pc_free = z2ev_mark[0]
                        prev = dwn_order_st[0]
                        order = [prev[6], prev[7]] + [s for s in range(8)
                                                      if s not in (prev[6], prev[7])]
                        dwn_order_st[0] = order
                        for pos in range(8):
                            par = pos % 2
                            sl_ = order[pos]
                            if 1 <= pos and pos + 1 < 8:
                                wd_slab((pos + 1) % 2, order[pos + 1])
                            nc.tensor.wait_ge(dWD[par].h, dWD[par].n)
                            nc.tensor.wait_ge(dWDg[par].h, dWDg[par].n)
                            for q in range(4):
                                i = sl_ * 4 + q
                                if pos == 0 and q == 0:
                                    nc.tensor.wait_ge(vTEV.h, hq_mark)
                                    nc.tensor.wait_ge(vZ2.h, pc_free)
                                    if half == 0:
                                        nc.tensor.wait_ge(sSIL.h, sSIL.n)
                                        nc.tensor.wait_ge(vHST.h, vHST.n)
                                ins = None
                                for g in (2 * half, 2 * half + 1):
                                    pt = z2p[g]
                                    for dh in range(2):
                                        ins = nc.tensor.matmul(pt[:, dh * 512:(dh + 1) * 512],
                                                               hqT[:, i, g * 128:(g + 1) * 128],
                                                               wds[par][:, q, dh * 512:(dh + 1) * 512],
                                                               start=(pos == 0 and q == 0),
                                                               stop=(pos == 7 and q == 3))
                                tDWN.inc(ins)
                                if q == 3:
                                    dwn_slab_mark[par] = tDWN.n
                                if q % 2 == 0 and pkq:
                                    pkq.pop(0)()
                        nc.vector.wait_ge(tDWN.h, tDWN.n)
                        ins = None
                        for g in (2 * half, 2 * half + 1):
                            pt = z2p[g]
                            vx("tensor_scalar", tmpf[:, 0:D], zt[:, g, :],
                               pg8[:, g:g + 1], pg8[:, 4 + g:5 + g],
                               ALU.subtract, ALU.mult)
                            ins = vx("scalar_tensor_tensor",
                                     xz2[:, g, :],
                                     pt[:, :], pg8[:, 8 + g:9 + g], tmpf[:, 0:D],
                                     ALU.mult, ALU.add)
                        vZ2.inc(ins)
                        z2ev_mark[0] = vZ2.n

                    if tau == 0:
                        gu_slab(0, 0)
                        gu_slab(1, 1)
                    gu_pass(0)
                    while pkq:
                        pkq.pop(0)()
                    pkq = [s for g_ in (0, 1) for s in mk_hq_steps(g_)]
                    gu_pass(1)
                    while pkq:
                        pkq.pop(0)()
                    hq01 = vTEV.n
                    pkq = [s for g_ in (2, 3) for s in mk_hq_steps(g_)]
                    if tau == 0:
                        wd_slab(0, 0)
                        wd_slab(1, 1)
                    if tau + 1 < 4:
                        issue_park_x(tau + 1)
                    down_pass(0, hq01)
                    while pkq:
                        pkq.pop(0)()
                    hq23 = vTEV.n
                    if tau + 1 < 4:
                        issue_park_y(tau + 1)
                    ln2_pair(tau, 0)
                    if tau + 1 < 4:
                        pkq = mk_ln1_closures(0, parked=True)
                    down_pass(1, hq23)
                    while pkq:
                        pkq.pop(0)()
                    if tau + 1 < 4:
                        issue_zt(tau + 1, 1, mark=z2ev_mark[0])
                    ln2_pair(tau, 1)

        if not (_PH & 8):
            nc.sync.wait_ge(vORD.h, vORD.n)
            nc.sync.dma_start(out[0:128, 0:8], SCAL[:, :]).then_inc(dOUT.h, 16); dOUT.n += 16
        nc.sync.wait_ge(dOUT.h, dOUT.n)
    return nc


_NC_CACHE = None


def kernel(L0_attn_x, L0_attn_Wq, L0_attn_Wk, L0_attn_Wv,
           L0_ffn_w_gate, L0_ffn_w_up, L0_ffn_w_down):
    global _NC_CACHE
    x = np.asarray(L0_attn_x, dtype=np.float32).reshape(4 * 4096, 1024)
    wqT = np.ascontiguousarray(np.asarray(L0_attn_Wq, np.float32).T)
    wkT = np.ascontiguousarray(np.asarray(L0_attn_Wk, np.float32).T)
    wvT = np.ascontiguousarray(np.asarray(L0_attn_Wv, np.float32).T)
    wgT = np.ascontiguousarray(np.asarray(L0_ffn_w_gate, np.float32).T)
    wuT = np.ascontiguousarray(np.asarray(L0_ffn_w_up, np.float32).T)
    wdT = np.ascontiguousarray(np.asarray(L0_ffn_w_down, np.float32).T)

    idb = np.eye(128, dtype=np.float32).astype(BD)
    idf = np.eye(128, dtype=np.float32)
    onesr = np.ones((1, 128), np.float32)
    onesc = np.ones((128, 1), np.float32)

    if _NC_CACHE is None:
        _NC_CACHE = build_nc()
    nc = _NC_CACHE

    in_maps = []
    for c in range(NCORE):
        rows = x[c * T:(c + 1) * T]
        in_maps.append({
            "xT": np.ascontiguousarray(rows.T),
            "xN": np.ascontiguousarray(rows),
            "wqT": wqT, "wkT": wkT, "wvT": wvT,
            "wgT": wgT, "wuT": wuT, "wdT": wdT,
            "idb": idb, "idf": idf,
            "onesr": onesr, "onesc": onesc,
        })
    res = run_bass_kernel_spmd(nc, in_maps, core_ids=list(range(NCORE)))
    full = np.concatenate([res.results[c]["out"] for c in range(NCORE)], axis=0)
    return full.reshape(4, 4096, 1024)


# version 32 (final)
# speedup vs baseline: 1.1191x; 1.0572x over previous
import os
import numpy as np
import ml_dtypes
from contextlib import ExitStack

import concourse.bass as bass
import concourse.mybir as mybir
from concourse.bass_utils import run_bass_kernel_spmd

F32 = mybir.dt.float32
BF16 = mybir.dt.bfloat16
AF = mybir.ActivationFunctionType
ALU = mybir.AluOpType
AX = mybir.AxisListType

SHIFT = 44.0
MAGIC = 12582912.0
EPS = 1e-5
NCORE = 8
T = 2048
NPAIR = 4
NQG = 4
NKC = 32
D = 1024
INNER = 4096
NB_ELEM = 4096 * 1024

BD = ml_dtypes.bfloat16
_PH0 = 0
_PH = int(os.environ.get("BASSK_PH", "15"))
_P1L = int(os.environ.get("BASSK_P1L", "9"))


class Ctr:
    def __init__(self, stack, nc, name):
        self.h = stack.enter_context(nc.semaphore(name))
        self.n = 0

    def inc(self, instr, k=1):
        instr.then_inc(self.h, k)
        self.n += k
        return self.n


def build_nc():
    nc = bass.Bass("TRN2", target_bir_lowering=False)

    xT = nc.dram_tensor("xT", [D, T], F32, kind="ExternalInput").ap()
    xN = nc.dram_tensor("xN", [T, D], F32, kind="ExternalInput").ap()
    wqT = nc.dram_tensor("wqT", [D, D], F32, kind="ExternalInput").ap()
    wkT = nc.dram_tensor("wkT", [D, D], F32, kind="ExternalInput").ap()
    wvT = nc.dram_tensor("wvT", [D, D], F32, kind="ExternalInput").ap()
    wgT = nc.dram_tensor("wgT", [D, INNER], F32, kind="ExternalInput").ap()
    wuT = nc.dram_tensor("wuT", [D, INNER], F32, kind="ExternalInput").ap()
    wdT = nc.dram_tensor("wdT", [INNER, D], F32, kind="ExternalInput").ap()
    idb = nc.dram_tensor("idb", [128, 128], BF16, kind="ExternalInput").ap()
    idf = nc.dram_tensor("idf", [128, 128], F32, kind="ExternalInput").ap()
    onesr = nc.dram_tensor("onesr", [1, 128], F32, kind="ExternalInput").ap()
    onesc = nc.dram_tensor("onesc", [128, 1], F32, kind="ExternalInput").ap()
    out = nc.dram_tensor("out", [T, D], F32, kind="ExternalOutput").ap()

    y_d = nc.dram_tensor("y_d", [T, D], F32, kind="Internal").ap()
    wguq_d = nc.dram_tensor("wguq_d", [16, 128, INNER], BF16, kind="Internal").ap()
    wdq_d = nc.dram_tensor("wdq_d", [32, 128, D], BF16, kind="Internal").ap()
    cc_in = nc.dram_tensor("cc_in", [1, 2], F32, kind="Internal").ap()
    cc_out = nc.dram_tensor("cc_out", [1, 2], F32, kind="Internal").ap()

    st = ExitStack()
    with st:
        C = lambda name: Ctr(st, nc, name)
        dC = C("dC"); dWA = [C("dWA0"), C("dWA1"), C("dWA2")]; dXF = C("dXF"); dXT = C("dXT")
        dBQ = C("dBQ"); dBK = C("dBK"); dSPL = [C("dSPL0"), C("dSPL1")]
        dWP = [C("dWP0"), C("dWP1"), C("dWP2"), C("dWP3")]; dWQS = [C("dWQS0"), C("dWQS1")]; dCC = C("dCC")
        dY = C("dY"); dYb = C("dYb"); dXN = C("dXN"); dXNb = C("dXNb"); dGU = [C("dGU0"), C("dGU1")]; dGUg = [C("dGUg0"), C("dGUg1")]; dWD = [C("dWD0"), C("dWD1")]; dWDg = [C("dWDg0"), C("dWDg1")]; dOUT = C("dOUT")
        tM1 = C("tM1"); tM2 = C("tM2"); sEXP = C("sEXP")
        tYTR = C("tYTR"); tPRJ = C("tPRJ"); tVTR = C("tVTR")
        gCC = C("gCC"); sMSC = C("sMSC"); tMSC = C("tMSC")
        tGU = C("tGU"); sSIL = C("sSIL"); sOCP = C("sOCP")
        tTRX = C("tTRX"); tDWN = C("tDWN")
        vORD = C("vORD")
        gORD = C("gORD")

        class VMark:
            def __init__(self):
                self.h = vORD.h
                self.n = 0
            def inc(self, ins):
                self.n = vORD.n
                return self.n

        vOCP = VMark(); vYST = VMark(); vPEV = VMark(); vVEV = VMark()
        vSTA = VMark(); vMSC = VMark(); vHST = VMark(); vQNT = VMark()
        vTEV = VMark(); vZ2 = VMark()

        def gx(_m, *a, **k):
            if gORD.n:
                nc.gpsimd.wait_ge(gORD.h, gORD.n)
            ins = getattr(nc.gpsimd, _m)(*a, **k)
            gORD.inc(ins)
            return ins

        def vx(_m, *a, **k):
            if vORD.n:
                nc.vector.wait_ge(vORD.h, vORD.n)
            ins = getattr(nc.vector, _m)(*a, **k)
            vORD.inc(ins)
            return ins

        sb = lambda *a: st.enter_context(nc.sbuf_tensor(*a))
        identb = sb("identb", [128, 128], BF16)
        identf = sb("identf", [128, 128], F32)
        ones_r = sb("ones_r", [1, 128], F32)
        ones_c = sb("ones_c", [128, 1], F32)
        expb = sb("expb", [128, 1], F32)
        SCAL = sb("SCAL", [128, 8], F32)
        sacc = sb("sacc", [128, 1], F32)
        ccs = sb("ccs", [1, 4], F32)
        sqacc = sb("sqacc", [128, 1], F32)

        for (dst_t, src_t) in ((identb, idb), (identf, idf),
                               (ones_r, onesr), (ones_c, onesc)):
            nc.sync.dma_start(dst_t[:, :], src_t).then_inc(dC.h, 16); dC.n += 16
        vx("memset", expb[:, :], -SHIFT)
        vx("memset", sacc[:, :], 0.0)
        ins = vx("memset", sqacc[:, :], 0.0)
        vMSC.inc(ins)

        wprep_vqnt = 0
        wprep_dwqs = 0
        if _PH & 2:
            with ExitStack() as ph1:
                wq_sb = ph1.enter_context(nc.sbuf_tensor("wq_sb", [128, 8, D], BF16))
                wk_sb = ph1.enter_context(nc.sbuf_tensor("wk_sb", [128, 8, D], BF16))
                wv_sb = ph1.enter_context(nc.sbuf_tensor("wv_sb", [128, 8, D], BF16))
                xfm0 = ph1.enter_context(nc.sbuf_tensor("xfm0", [128, 8, 512], BF16))
                xtok = ph1.enter_context(nc.sbuf_tensor("xtok", [128, 4, D], F32))
                QzA0 = ph1.enter_context(nc.sbuf_tensor("QzA0", [128, 4096], BF16))
                QzA1 = ph1.enter_context(nc.sbuf_tensor("QzA1", [128, 4096], BF16))
                QzB0 = ph1.enter_context(nc.sbuf_tensor("QzB0", [128, 4096], BF16))
                QzB1 = ph1.enter_context(nc.sbuf_tensor("QzB1", [128, 4096], BF16))
                Kt0 = ph1.enter_context(nc.sbuf_tensor("Kt0", [128, 4096], BF16))
                Kt1 = ph1.enter_context(nc.sbuf_tensor("Kt1", [128, 4096], BF16))
                Vt0 = ph1.enter_context(nc.sbuf_tensor("Vt0", [128, 8, 512], BF16))
                bQK = ph1.enter_context(nc.sbuf_tensor("bQK", [128, 8, 512], BF16))
                Vp0 = ph1.enter_context(nc.sbuf_tensor("Vp0", [128, 32, 68], BF16))
                Vp1 = ph1.enter_context(nc.sbuf_tensor("Vp1", [128, 32, 68], BF16))
                P0 = ph1.enter_context(nc.sbuf_tensor("P0", [128, 1024], BF16))
                P1 = ph1.enter_context(nc.sbuf_tensor("P1", [128, 1024], BF16))
                P2 = ph1.enter_context(nc.sbuf_tensor("P2", [128, 1024], BF16))
                Osb = ph1.enter_context(nc.sbuf_tensor("Osb", [128, 1024], F32))
                yT0 = ph1.enter_context(nc.sbuf_tensor("yT0", [128, 4, D], F32))
                yT1 = ph1.enter_context(nc.sbuf_tensor("yT1", [128, 4, D], F32))
                rec = ph1.enter_context(nc.sbuf_tensor("rec", [128, 8], F32))
                wpb0 = ph1.enter_context(nc.sbuf_tensor("wpb0", [128, 512], F32))
                wpb1 = ph1.enter_context(nc.sbuf_tensor("wpb1", [128, 512], F32))
                wpb2 = ph1.enter_context(nc.sbuf_tensor("wpb2", [128, 512], F32))
                wpb3 = ph1.enter_context(nc.sbuf_tensor("wpb3", [128, 512], F32))
                wpq0 = ph1.enter_context(nc.sbuf_tensor("wpq0", [128, 512], BF16))
                wpq1 = ph1.enter_context(nc.sbuf_tensor("wpq1", [128, 512], BF16))
                wcol = ph1.enter_context(nc.sbuf_tensor("wcol", [128, 4], F32))
                wsc = ph1.enter_context(nc.sbuf_tensor("wsc", [1, 4], F32))
                S0 = ph1.enter_context(nc.psum_tensor("S0", [128, 1024], F32))
                S1 = ph1.enter_context(nc.psum_tensor("S1", [128, 1024], F32))
                Ops = ph1.enter_context(nc.psum_tensor("Ops", [128, 1024], F32))
                yTp = ph1.enter_context(nc.psum_tensor("yTp", [128, 512], F32))
                Vps = ph1.enter_context(nc.psum_tensor("Vps", [128, 1024], BF16))

                xfms = [xfm0, xfm0]
                QzAs = [QzA0, QzA1]; QzBs = [QzB0, QzB1]
                Kts = [Kt0, Kt1]; Vts = [Vt0, Vt0]
                Vps_ = [Vp0, Vp1]; Ps = [P0, P1, P2]; yTs = [yT0, yT1]
                Ss = [S0, S1]

                nc.gpsimd.wait_ge(vQNT.h, wprep_vqnt)

                nc.gpsimd.dma_start(wq_sb[:, :, :],
                                    wqT.rearrange("(j p) k -> p j k", p=128)).then_inc(dWA[0].h, 16); dWA[0].n += 16
                nc.gpsimd.dma_start(xfm0[:, :, :],
                                    xT.rearrange("(j p) t -> p j t", p=128)[:, :, 0:512]).then_inc(dXF.h, 16); dXF.n += 16
                for (wi, (w_sb, w_hbm)) in enumerate(((wk_sb, wkT), (wv_sb, wvT))):
                    nc.gpsimd.dma_start(w_sb[:, :, :],
                                        w_hbm.rearrange("(j p) k -> p j k", p=128)).then_inc(dWA[1 + wi].h, 16); dWA[1 + wi].n += 16
                vx("memset", Vp0[:, :, 64:65], 1.0)
                vx("memset", Vp1[:, :, 64:65], 1.0)
                vx("memset", QzA0[64:128, :], 0.0)
                vx("memset", QzA1[64:128, :], 0.0)
                vx("memset", QzB0[0:64, :], 0.0)
                ins = vx("memset", QzB1[0:64, :], 0.0)
                vMSC.inc(ins)

                nc.sync.wait_ge(vQNT.h, wprep_vqnt)

                nc.sync.dma_start(xtok[:, :, :],
                                  xN[0:512, :].rearrange("(g p) d -> p g d", p=128)).then_inc(dXT.h, 16); dXT.n += 16

                exp_cnt = 0
                m1_cnt = 0
                m2_cnt = 0
                slot_evt = [0] * 4
                exp_at_pair = []
                pending_stats = []
                m2_at_block = {}
                stt_at_pair = []
                spill_at_pair = []
                vev_at_block = {}

                def pslot(k):
                    t_ = Ss[(k // 2) % 2]
                    return t_, 512 * (k % 2)

                wbufs = [wpb0, wpb1, wpb2, wpb3]
                wspecs = [
                    ("g", wgT, wguq_d, 8, INNER, 0, 3),
                    ("u", wuT, wguq_d, 8, INNER, 1, 4),
                    ("d", wdT, wdq_d, 32, D, 2, 5),
                ]
                wspec_by = {sp[0]: sp for sp in wspecs}
                wsub = {}
                for (nm, srcw, dstw, nch, wdt, s_slot, m_slot) in wspecs:
                    subs = []
                    for c in range(nch):
                        for c0 in range(0, wdt, 512):
                            subs.append((c, c0, 512))
                    wsub[nm] = subs
                wstream = []
                for kind, nm in (("A", "g"), ("A", "u"), ("B", "g"),
                                 ("A", "d"), ("B", "u"), ("B", "d")):
                    for sub in wsub[nm]:
                        wstream.append((kind, nm, sub))
                segA_last = {}
                segA_first = {}
                segB_first = {}
                for k, (kind, nm, sub) in enumerate(wstream):
                    if kind == "A":
                        segA_last[nm] = k
                        segA_first.setdefault(nm, k)
                    else:
                        segB_first.setdefault(nm, k)
                fin_mark = {}
                wprocmark = {}
                wdma_issued = [False] * (len(wstream) + 1)
                wpqs = [wpq0, wpq1]
                wqs_mark = [0, 0]
                bq_idx = [0]

                def w_issue_dma(k):
                    if k >= len(wstream) or wdma_issued[k]:
                        return
                    wdma_issued[k] = True
                    (kind, nm, (c, c0, w_)) = wstream[k]
                    srcw = wspec_by[nm][1]
                    par = k % 4
                    dw = dWP[par]
                    if k >= 4:
                        (eng, val) = wprocmark[k - 4]
                        nc.gpsimd.wait_ge(vORD.h if eng == "V" else gORD.h, val)
                    nc.gpsimd.dma_start(wbufs[par][:, 0:w_],
                                        srcw[c * 128:(c + 1) * 128, c0:c0 + w_]).then_inc(dw.h, 16)
                    dw.n += 16

                def w_step(k):
                    (kind, nm, (c, c0, w_)) = wstream[k]
                    (nm2, srcw, dstw, nch, wdt, s_slot, m_slot) = wspec_by[nm]
                    par = k % 4
                    dw = dWP[par]
                    b = wbufs[par]
                    need = 16 * (k // 4 + 1)
                    if kind == "A":
                        ac = {"g": 0, "u": 1, "d": 2}[nm]
                        nc.vector.wait_ge(dw.h, need)
                        if k == segA_first[nm]:
                            vx("memset", wcol[:, ac:ac + 1], 0.0)
                        vx("tensor_reduce", wcol[:, 3:4], b[:, 0:w_], AX.X, ALU.add,
                           apply_absolute_value=True)
                        ins = vx("tensor_add", wcol[:, ac:ac + 1], wcol[:, ac:ac + 1],
                                 wcol[:, 3:4])
                        wprocmark[k] = ("V", vORD.n)
                    else:
                        nc.vector.wait_ge(dw.h, need)
                        if k == segB_first[nm]:
                            nc.vector.wait_ge(vMSC.h, fin_mark[nm])
                        vx("tensor_scalar", b[:, 0:w_], b[:, 0:w_],
                           SCAL[:, s_slot:s_slot + 1], MAGIC, ALU.mult, ALU.add)
                        vx("tensor_scalar", b[:, 0:w_], b[:, 0:w_], -MAGIC, 1.0,
                           ALU.add, ALU.min)
                        bp = bq_idx[0] % 2
                        bq_idx[0] += 1
                        wpq = wpqs[bp]
                        dq = dWQS[bp]
                        if wqs_mark[bp] > 0:
                            nc.vector.wait_ge(dq.h, wqs_mark[bp])
                        ins = vx("tensor_scalar", wpq[:, 0:w_], b[:, 0:w_], -1.0, None, ALU.max)
                        nc.sync.wait_ge(vORD.h, vORD.n)
                        row_off = 8 if nm == "u" else 0
                        nc.sync.dma_start(dstw[row_off + c, :, c0:c0 + w_],
                                            wpq[:, 0:w_]).then_inc(dq.h, 16)
                        dq.n += 16
                        wqs_mark[bp] = dq.n
                        wprocmark[k] = ("V", vORD.n)
                    w_issue_dma(k + 1)
                    w_issue_dma(k + 2)

                def w_finalize(nm, ytp_guard):
                    (nm2, srcw, dstw, nch, wdt, s_slot, m_slot) = wspec_by[nm]
                    nel = float(nch * 128 * wdt)
                    (eng, val) = wprocmark[segA_last[nm]]
                    nc.tensor.wait_ge(vORD.h, val)
                    if ytp_guard > 0:
                        nc.tensor.wait_ge(vYST.h, ytp_guard)
                    ac = {"g": 0, "u": 1, "d": 2}[nm]
                    ins = nc.tensor.matmul(yTp[0:1, 0:1], wcol[:, ac:ac + 1], ones_c[:, :],
                                           start=True, stop=True)
                    tMSC.inc(ins)
                    nc.vector.wait_ge(tMSC.h, tMSC.n)
                    vx("tensor_scalar", wsc[0:1, 0:1], yTp[0:1, 0:1],
                       1.0 / nel, 1e-5, ALU.mult, ALU.max)
                    ins = vx("reciprocal", wsc[0:1, 1:2], wsc[0:1, 0:1])
                    vMSC.inc(ins)
                    nc.tensor.wait_ge(vMSC.h, vMSC.n)
                    nc.tensor.matmul(yTp[:, 1:2], ones_r[:, :], wsc[0:1, 0:1],
                                     start=True, stop=True)
                    ins = nc.tensor.matmul(yTp[:, 2:3], ones_r[:, :], wsc[0:1, 1:2],
                                           start=True, stop=True)
                    tMSC.inc(ins)
                    nc.vector.wait_ge(tMSC.h, tMSC.n)
                    vx("tensor_copy", SCAL[:, m_slot:m_slot + 1], yTp[:, 1:2])
                    ins = vx("tensor_copy", SCAL[:, s_slot:s_slot + 1], yTp[:, 2:3])
                    vMSC.inc(ins)
                    fin_mark[nm] = vMSC.n

                wsched = {0: list(range(0, 96)), 1: list(range(96, 192)),
                          2: list(range(192, 304)), 3: list(range(304, 384))}
                wfin = {1: "g", 2: "u", 3: "d"}
                if not (_PH & 1):
                    wsched = {0: [], 1: [], 2: [], 3: []}
                    wfin = {}
                else:
                    w_issue_dma(0)
                    w_issue_dma(1)
                    w_issue_dma(2)

                deferred_evac = []
                stats_osb_mark = [0]
                for beta in range(NPAIR):
                    xf = xfms[beta % 2]
                    QzA, QzB = QzAs[beta % 2], QzBs[beta % 2]
                    Kt, Vt = Kts[beta % 2], Vts[beta % 2]
                    yT = yTs[beta % 2]

                    if beta in wfin:
                        w_finalize(wfin[beta], stt_at_pair[beta - 1])
                    wq_todo = [(lambda k=k: w_step(k)) for k in wsched.get(beta, [])]
                    wq_point = [0]
                    while pending_stats:
                        wq_todo.insert(0, pending_stats.pop())
                    _npts = 52 if beta == NPAIR - 1 else 64
                    _nw = len(wq_todo)
                    wq_counts = [_nw * (i + 1) // _npts - _nw * i // _npts
                                 for i in range(_npts)] + [0] * 16
                    def wq_drain():
                        take = wq_counts[wq_point[0]]
                        wq_point[0] += 1
                        for _ in range(take):
                            if wq_todo:
                                wq_todo.pop(0)()

                    nc.tensor.wait_ge(dXF.h, 16 * (beta + 1))
                    if beta == 0:
                        nc.tensor.wait_ge(vMSC.h, vMSC.n)
                        nc.tensor.wait_ge(dC.h, dC.n)
                    if beta >= 1:
                        nc.vector.wait_ge(dBQ.h, dBQ.n)
                        nc.vector.wait_ge(dBK.h, dBK.n)
                        if beta >= 2:
                            nc.vector.wait_ge(sEXP.h, exp_at_pair[beta - 2])
                    kslot = 0
                    for (wsb, which) in ((wq_sb, "q"), (wk_sb, "k"), (wv_sb, "v")):
                        if beta == 0:
                            nc.tensor.wait_ge(dWA[{"q": 0, "k": 1, "v": 2}[which]].h, 16)
                        for i in range(8):
                            pt, off = pslot(kslot)
                            if beta > 0 and kslot < 4:
                                nc.tensor.wait_ge(sEXP.h, exp_at_pair[beta - 1])
                            if slot_evt[kslot % 4] > 0:
                                nc.tensor.wait_ge(vPEV.h, slot_evt[kslot % 4])
                            ins = None
                            for j in range(8):
                                ins = nc.tensor.matmul(pt[:, off:off + 512],
                                                       wsb[:, j, i * 128:(i + 1) * 128],
                                                       xf[:, j, :],
                                                       start=(j == 0), stop=(j == 7))
                            tPRJ.inc(ins)
                            nc.vector.wait_ge(tPRJ.h, tPRJ.n)
                            if which == "v":
                                ins = vx("tensor_copy", Vt[:, i, :], pt[:, off:off + 512])
                            elif which == "k":
                                vx("tensor_copy", Kt[0:64, (2 * i) * 256:(2 * i) * 256 + 256],
                                   pt[0:64, off:off + 256])
                                vx("tensor_copy", Kt[64:128, (2 * i + 1) * 256:(2 * i + 1) * 256 + 256],
                                   pt[64:128, off + 256:off + 512])
                                vx("tensor_copy", bQK[0:64, i, 256:512], pt[0:64, off + 256:off + 512])
                                ins = vx("tensor_copy", bQK[64:128, i, 256:512], pt[64:128, off:off + 256])
                            else:
                                vx("tensor_copy", QzA[0:64, (2 * i) * 256:(2 * i) * 256 + 256],
                                   pt[0:64, off:off + 256])
                                vx("tensor_copy", QzB[64:128, (2 * i + 1) * 256:(2 * i + 1) * 256 + 256],
                                   pt[64:128, off + 256:off + 512])
                                vx("tensor_copy", bQK[0:64, i, 0:256], pt[0:64, off + 256:off + 512])
                                ins = vx("tensor_copy", bQK[64:128, i, 0:256], pt[64:128, off:off + 256])
                            slot_evt[kslot % 4] = vPEV.inc(ins)
                            kslot += 1
                    nc.sync.wait_ge(vPEV.h, vPEV.n)
                    dst_e = Kt[64:128, :].rearrange("p (i q s) -> p i q s", q=2, s=256)[:, :, 0, :]
                    nc.sync.dma_start(dst_e, bQK[0:64, :, 256:512]).then_inc(dBK.h, 16); dBK.n += 16
                    dst_o = Kt[0:64, :].rearrange("p (i q s) -> p i q s", q=2, s=256)[:, :, 1, :]
                    nc.sync.dma_start(dst_o, bQK[64:128, :, 256:512]).then_inc(dBK.h, 16); dBK.n += 16
                    dst_e = QzB[64:128, :].rearrange("p (i q s) -> p i q s", q=2, s=256)[:, :, 0, :]
                    nc.sync.dma_start(dst_e, bQK[0:64, :, 0:256]).then_inc(dBQ.h, 16); dBQ.n += 16
                    dst_o = QzA[0:64, :].rearrange("p (i q s) -> p i q s", q=2, s=256)[:, :, 1, :]
                    nc.sync.dma_start(dst_o, bQK[64:128, :, 0:256]).then_inc(dBQ.h, 16); dBQ.n += 16

                    nc.tensor.wait_ge(vPEV.h, vPEV.n)
                    for h in (range(2) if _P1L >= 2 else []):
                        Vp = Vps_[h]
                        prev_m2 = m2_at_block.get((beta - 1, h), 0)
                        for batch in range(2):
                            nc.tensor.wait_ge(vVEV.h, vVEV.n)
                            ins = None
                            for s in range(8):
                                i = batch * 4 + s // 2
                                kh = s % 2
                                ins = nc.tensor.transpose(
                                    Vps[:, s * 128:(s + 1) * 128],
                                    Vt[:, i, h * 256 + kh * 128:h * 256 + (kh + 1) * 128],
                                    identb[:, :])
                            tVTR.inc(ins)
                            nc.vector.wait_ge(tVTR.h, tVTR.n)
                            if prev_m2 > 0 and batch == 0:
                                nc.vector.wait_ge(tM2.h, prev_m2)
                            srcv = Vps[:, :].rearrange("p (a b c) -> p a b c", b=2, c=128)
                            dstv = Vp[:, 16 * batch:16 * batch + 16, 0:64].rearrange(
                                "p (a r) c -> p a r c", r=4)
                            vx("tensor_copy", dstv[:, :, 0:2, :], srcv[:, :, :, 0:64])
                            ins = vx("tensor_copy", dstv[:, :, 2:4, :], srcv[:, :, :, 64:128])
                            vVEV.inc(ins)
                        vev_at_block[(beta, h)] = vVEV.n

                    def mk_evac_w(beta, h, qg, yT, w, ocp_mark):
                        def run():
                            nc.tensor.wait_ge(sOCP.h, ocp_mark)
                            if tYTR.n >= 1:
                                nc.tensor.wait_ge(vYST.h, vYST.n)
                            ins = None
                            for pp in range(4):
                                p = w * 4 + pp
                                ins = nc.tensor.transpose(yTp[:, pp * 128:pp * 128 + 65],
                                                          Osb[0:65, p * 128:(p + 1) * 128],
                                                          identf[0:65, 0:65])
                            tYTR.inc(ins)
                            nc.vector.wait_ge(tYTR.h, tYTR.n)
                            if beta == 0 and h == 0 and qg == 0 and w == 0:
                                nc.vector.wait_ge(dXT.h, dXT.n)
                            if beta >= 1 and h == 0 and qg == 0 and w == 0:
                                if beta >= 2:
                                    pp_, vv_ = spill_at_pair[beta - 2]
                                    nc.vector.wait_ge(dSPL[pp_].h, vv_)
                                nc.vector.wait_ge(dXT.h, 16 * (beta + 1))
                            ins = None
                            for pp in range(4):
                                p = w * 4 + pp
                                qk = 4 * qg + p // 2
                                g = 2 * h + (p % 2)
                                c0 = pp * 128
                                vx("reciprocal", rec[:, p:p + 1], yTp[:, c0 + 64:c0 + 65])
                                ins = vx("scalar_tensor_tensor",
                                    yT[:, g, qk * 64:(qk + 1) * 64],
                                    yTp[:, c0:c0 + 64], rec[:, p:p + 1],
                                    xtok[:, g, qk * 64:(qk + 1) * 64],
                                    ALU.mult, ALU.add)
                            vYST.inc(ins)
                            if w == 1 and beta == NPAIR - 1 and qg == NQG - 1:
                                mk_stats_g(beta, yT, 2 * h)()
                                mk_stats_g(beta, yT, 2 * h + 1)()
                        return run
                    for h in (range(2) if _P1L >= 3 else []):
                        o = h * 64
                        Vp = Vps_[h]
                        nc.tensor.wait_ge(vVEV.h, vev_at_block[(beta, h)])
                        if h == 0:
                            nc.tensor.wait_ge(dBQ.h, dBQ.n)
                            nc.tensor.wait_ge(dBK.h, dBK.n)
                        Qz = QzA if h == 0 else QzB
                        for qg in range(NQG):
                            LAG = 2
                            e_base = exp_cnt
                            for kc in range(NKC + LAG):
                                if kc in (1, NKC // 2) and deferred_evac:
                                    deferred_evac.pop(0)()
                                if kc < NKC:
                                    e = exp_cnt + 1
                                    si = (e - 1) % 2
                                    Sb = Ss[si]
                                    if e > 2:
                                        nc.tensor.wait_ge(sEXP.h, e - 2)
                                    nc.tensor.matmul(Sb[:, 0:512],
                                                     Kt[:, kc * 128:(kc + 1) * 128],
                                                     Qz[:, qg * 1024:qg * 1024 + 512],
                                                     start=True, stop=True)
                                    ins = nc.tensor.matmul(Sb[:, 512:1024],
                                                           Kt[:, kc * 128:(kc + 1) * 128],
                                                           Qz[:, qg * 1024 + 512:(qg + 1) * 1024],
                                                           start=True, stop=True)
                                    m1_cnt = tM1.inc(ins)
                                    nc.scalar.wait_ge(tM1.h, m1_cnt)
                                    if e > 3:
                                        nc.scalar.wait_ge(tM2.h, e - 3)
                                    if e == 1:
                                        nc.scalar.wait_ge(dC.h, dC.n)
                                    ins = nc.scalar.activation(Ps[(e - 1) % 3][:, :], Sb[:, :], AF.Exp,
                                                               bias=expb[:, 0:1], scale=1.0)
                                    exp_cnt = sEXP.inc(ins)
                                if _P1L < 4:
                                    continue
                                kc2 = kc - LAG
                                if kc2 < 0:
                                    continue
                                x2 = e_base + kc2 + 1
                                if kc >= NKC:
                                    nc.tensor.wait_ge(sEXP.h, x2)
                                if kc2 == 0:
                                    nc.tensor.wait_ge(sOCP.h, sOCP.n)
                                Pb = Ps[(x2 - 1) % 3]
                                nc.tensor.matmul(Ops[0:65, 0:512], Vp[:, kc2, 0:65],
                                                 Pb[:, 0:512],
                                                 start=(kc2 == 0), stop=(kc2 == NKC - 1))
                                ins = nc.tensor.matmul(Ops[0:65, 512:1024], Vp[:, kc2, 0:65],
                                                       Pb[:, 512:1024],
                                                       start=(kc2 == 0), stop=(kc2 == NKC - 1))
                                m2_cnt = tM2.inc(ins)
                                if kc2 % 4 == 3:
                                    wq_drain()
                            if _P1L < 5:
                                continue
                            nc.scalar.wait_ge(tM2.h, m2_cnt)
                            if stats_osb_mark[0] > 0:
                                nc.scalar.wait_ge(vORD.h, stats_osb_mark[0])
                            nc.scalar.wait_ge(tYTR.h, tYTR.n)
                            ins = nc.scalar.activation(Osb[0:65, :], Ops[0:65, :], AF.Copy)
                            sOCP.inc(ins)
                            deferred_evac.append(mk_evac_w(beta, h, qg, yT, 0, sOCP.n))
                            deferred_evac.append(mk_evac_w(beta, h, qg, yT, 1, sOCP.n))
                        m2_at_block[(beta, h)] = m2_cnt
                    while deferred_evac:
                        deferred_evac.pop(0)()
                    exp_at_pair.append(exp_cnt)

                    def mk_stats_g(beta, yT, g):
                        def run():
                            vx("tensor_mul", Osb[:, 0:1024], yT[:, g, :], yT[:, g, :])
                            vx("tensor_reduce", rec[:, 0:1], yT[:, g, :], AX.X, ALU.add)
                            vx("tensor_add", sacc[:, :], sacc[:, :], rec[:, 0:1])
                            vx("tensor_reduce", rec[:, 1:2], Osb[:, 0:1024], AX.X, ALU.add)
                            ins = vx("tensor_add", sqacc[:, :], sqacc[:, :], rec[:, 1:2])
                            stats_osb_mark[0] = vORD.n
                            if g == 3:
                                vSTA.inc(ins)
                                dsp = dSPL[beta % 2]
                                nc.sync.wait_ge(vSTA.h, vSTA.n)
                                nc.sync.dma_start(
                                    y_d[512 * beta:512 * (beta + 1), :].rearrange("(g p) d -> p g d", p=128),
                                    yT[:, :, :]).then_inc(dsp.h, 16); dsp.n += 16
                                spill_at_pair.append((beta % 2, dsp.n))
                        return run
                    if _P1L >= 6 and beta < NPAIR - 1:
                        for g in range(4):
                            pending_stats.append(mk_stats_g(beta, yT, g))
                    stt_at_pair.append(vYST.n)

                    if beta + 1 < NPAIR:
                        t0 = 512 * (beta + 1)
                        nc.gpsimd.wait_ge(tPRJ.h, tPRJ.n)
                        nc.gpsimd.dma_start(
                            xfms[(beta + 1) % 2][:, :, :],
                            xT.rearrange("(j p) t -> p j t", p=128)[:, :, t0:t0 + 512]).then_inc(dXF.h, 16); dXF.n += 16
                        nc.sync.wait_ge(vYST.h, stt_at_pair[beta])
                        nc.sync.dma_start(
                            xtok[:, :, :],
                            xN[t0:t0 + 512, :].rearrange("(g p) d -> p g d", p=128)).then_inc(dXT.h, 16); dXT.n += 16

                while pending_stats:
                    pending_stats.pop(0)()

        phase2_end = vMSC.n
        attn_T_prj = tPRJ.n
        attn_T_m1 = tM1.n
        attn_T_m2 = tM2.n
        attn_T_ytr = tYTR.n
        attn_T_vtr = tVTR.n
        attn_V_end = vSTA.n
        if _PH & 4:
            with (
                nc.psum_tensor("pb2", [128, 512], F32) as pb2,
            ):
                nc.tensor.wait_ge(vSTA.h, vSTA.n)
                nc.tensor.matmul(pb2[0:1, 0:1], sacc[:, :], ones_c[:, :], start=True, stop=True)
                ins = nc.tensor.matmul(pb2[0:1, 1:2], sqacc[:, :], ones_c[:, :],
                                       start=True, stop=True)
                tMSC.inc(ins)
                nc.vector.wait_ge(tMSC.h, tMSC.n)
                ins = vx("tensor_copy", ccs[0:1, 0:2], pb2[0:1, 0:2])
                vMSC.inc(ins)
                nc.sync.wait_ge(vMSC.h, vMSC.n)
                nc.sync.dma_start(cc_in, ccs[0:1, 0:2]).then_inc(dCC.h, 16); dCC.n += 16
                nc.gpsimd.wait_ge(dCC.h, dCC.n)
                ins = nc.gpsimd.collective_compute(
                    "AllReduce", ALU.add,
                    replica_groups=[[0, 1], [2, 3], [4, 5], [6, 7]],
                    ins=[cc_in], outs=[cc_out],
                )
                gCC.inc(ins)
                nc.sync.wait_ge(gCC.h, gCC.n)
                nc.sync.dma_start(ccs[0:1, 0:2], cc_out).then_inc(dCC.h, 16); dCC.n += 16
                nc.vector.wait_ge(dCC.h, dCC.n)
                vx("tensor_scalar_mul", ccs[0:1, 0:1], ccs[0:1, 0:1], 1.0 / NB_ELEM)
                vx("tensor_scalar_mul", ccs[0:1, 1:2], ccs[0:1, 1:2], 1.0 / NB_ELEM)
                vx("tensor_mul", ccs[0:1, 2:3], ccs[0:1, 0:1], ccs[0:1, 0:1])
                vx("tensor_sub", ccs[0:1, 1:2], ccs[0:1, 1:2], ccs[0:1, 2:3])
                ins = vx("tensor_scalar_add", ccs[0:1, 1:2], ccs[0:1, 1:2], EPS)
                vMSC.inc(ins)
                nc.scalar.wait_ge(vMSC.h, vMSC.n)
                ins = nc.scalar.activation(ccs[0:1, 2:3], ccs[0:1, 1:2], AF.Sqrt)
                sMSC.inc(ins)
                nc.vector.wait_ge(sMSC.h, sMSC.n)
                ins = vx("reciprocal", ccs[0:1, 3:4], ccs[0:1, 2:3])
                vMSC.inc(ins)
                nc.tensor.wait_ge(vMSC.h, vMSC.n)
                ins = nc.tensor.matmul(pb2[:, 3:4], ones_r[:, :], ccs[0:1, 3:4],
                                       start=True, stop=True)
                tMSC.inc(ins)
                nc.vector.wait_ge(tMSC.h, tMSC.n)
                ins = vx("tensor_copy", SCAL[:, 6:7], pb2[:, 3:4])
                vMSC.inc(ins)
            phase2_end = vMSC.n

        if _PH & 8:
            with ExitStack() as ph3:
                xz2 = ph3.enter_context(nc.sbuf_tensor("xz2", [128, 4, D], F32))
                zt = ph3.enter_context(nc.sbuf_tensor("zt", [128, 4, D], F32))
                tmpf = ph3.enter_context(nc.sbuf_tensor("tmpf", [128, 2048], F32))
                tmpb = ph3.enter_context(nc.sbuf_tensor("tmpb", [128, INNER], BF16))
                xqT = ph3.enter_context(nc.sbuf_tensor("xqT", [128, 8, 512], BF16))
                hsb = ph3.enter_context(nc.sbuf_tensor("hsb", [128, 4, INNER], F32))
                hqT = ph3.enter_context(nc.sbuf_tensor("hqT", [128, 32, 512], BF16))
                wgu0 = ph3.enter_context(nc.sbuf_tensor("wgu0", [128, 16, 512], BF16))
                wgu1 = ph3.enter_context(nc.sbuf_tensor("wgu1", [128, 16, 512], BF16))
                wd0 = ph3.enter_context(nc.sbuf_tensor("wd0", [128, 4, 1024], BF16))
                wd1 = ph3.enter_context(nc.sbuf_tensor("wd1", [128, 4, 1024], BF16))
                gs = ph3.enter_context(nc.sbuf_tensor("gs", [128, 512], F32))
                st8 = ph3.enter_context(nc.sbuf_tensor("st8", [128, 16], F32))
                pg8 = ph3.enter_context(nc.sbuf_tensor("pg8", [128, 12], F32))
                ah4 = ph3.enter_context(nc.sbuf_tensor("ah4", [128, 4], F32))
                ln6 = ph3.enter_context(nc.sbuf_tensor("ln6", [128, 12], F32))
                ln8 = ph3.enter_context(nc.sbuf_tensor("ln8", [128, 8], F32))
                lnS = ph3.enter_context(nc.sbuf_tensor("lnS", [128, 8], F32))
                PA = ph3.enter_context(nc.psum_tensor("PA", [128, 1024], F32))
                PB = ph3.enter_context(nc.psum_tensor("PB", [128, 1024], F32))
                PC = ph3.enter_context(nc.psum_tensor("PC", [128, 1024], F32))
                ptb = ph3.enter_context(nc.psum_tensor("ptb", [128, 1024], BF16))

                wgus = [wgu0, wgu1]
                wds = [wd0, wd1]
                gu_slab_mark = [0, 0]
                dwn_slab_mark = [0, 0]
                gu_order_st = [[2, 3, 4, 5, 6, 7, 0, 1]]
                dwn_order_st = [[2, 3, 4, 5, 6, 7, 0, 1]]
                z2ev_mark = [0]
                PAB = [PA, PB]
                pab_sil = [0, 0]
                pab_hst = [0, 0]
                pab_z2 = 0

                xqt_mark = {}
                scg_mark = {}
                pkq = []

                def issue_zt(tau_, pairg, mark=None):
                    r0 = 512 * tau_ + 256 * pairg
                    ga = 2 * pairg
                    dyp = dY if pairg == 0 else dYb
                    nc.gpsimd.wait_ge(vZ2.h, vZ2.n if mark is None else mark)
                    nc.gpsimd.dma_start(
                        zt[:, ga:ga + 2, :],
                        y_d[r0:r0 + 256, :].rearrange("(g p) d -> p g d", p=128)
                    ).then_inc(dyp.h, 16); dyp.n += 16

                def issue_xz2(tau_, pairg):
                    r0 = 512 * tau_ + 256 * pairg
                    ga = 2 * pairg
                    dxp = dXN if pairg == 0 else dXNb
                    nc.gpsimd.wait_ge(vZ2.h, vZ2.n)
                    nc.gpsimd.dma_start(
                        xz2[:, ga:ga + 2, :],
                        xN[r0:r0 + 256, :].rearrange("(g p) d -> p g d", p=128)
                    ).then_inc(dxp.h, 16); dxp.n += 16

                def issue_loads(tau_, pairg):
                    issue_zt(tau_, pairg)
                    issue_xz2(tau_, pairg)

                def issue_park_x(tau_):
                    r0 = 512 * tau_
                    nc.gpsimd.wait_ge(vQNT.h, vQNT.n)
                    nc.gpsimd.dma_start(
                        hsb[:, 0:2, 1024:2048],
                        xN[r0:r0 + 256, :].rearrange("(g p) d -> p g d", p=128)
                    ).then_inc(dXN.h, 16); dXN.n += 16

                def issue_park_y(tau_):
                    r0 = 512 * tau_
                    nc.gpsimd.wait_ge(vQNT.h, vQNT.n)
                    nc.gpsimd.dma_start(
                        hsb[:, 2:4, 1024:2048],
                        y_d[r0:r0 + 256, :].rearrange("(g p) d -> p g d", p=128)
                    ).then_inc(dY.h, 16); dY.n += 16

                def mk_ln1_closures(pairg, parked=False):
                    ga, gb = 2 * pairg, 2 * pairg + 1
                    (dyp, dxp) = (dY, dXN) if pairg == 0 else (dYb, dXNb)
                    def mk_c1(i, g):
                        def c1():
                            nc.vector.wait_ge(dyp.h, dyp.n)
                            nc.vector.wait_ge(dxp.h, dxp.n)
                            if parked:
                                vx("scalar_tensor_tensor", zt[:, g, :],
                                   hsb[:, 2 + i, 1024:2048], SCAL[:, 6:7],
                                   hsb[:, i, 1024:2048], ALU.mult, ALU.add)
                            else:
                                vx("scalar_tensor_tensor", zt[:, g, :], zt[:, g, :],
                                   SCAL[:, 6:7], xz2[:, g, :], ALU.mult, ALU.add)
                            vx("bn_stats", ln6[:, 0:6], zt[:, g, 0:512])
                            vx("bn_stats", ln6[:, 6:12], zt[:, g, 512:1024])
                            vx("bn_aggr", ln8[:, 2 * g:2 * g + 2], ln6[:, :])
                        return c1
                    def mk_c1b(i, g):
                        def c1b():
                            vx("tensor_reduce", st8[:, 0 + i:1 + i], zt[:, g, :], AX.X, ALU.max)
                            vx("tensor_reduce", st8[:, 2 + i:3 + i], zt[:, g, :], AX.X, ALU.min)
                        return c1b
                    def c2():
                        lpr = ln8[:, 2 * ga:2 * ga + 4].rearrange(
                            "p (g two) -> p two g", two=2)
                        vx("tensor_copy", pg8[:, ga:ga + 2], lpr[:, 0, :])
                        ins = vx("tensor_scalar_add", st8[:, 4:6], lpr[:, 1, :], EPS)
                        vMSC.inc(ins)
                        nc.scalar.wait_ge(vMSC.h, vMSC.n)
                        ins = nc.scalar.activation(pg8[:, 4 + ga:4 + ga + 2], st8[:, 4:6],
                                                   AF.Sqrt)
                        sMSC.inc(ins)
                    def c3():
                        nc.vector.wait_ge(sMSC.h, sMSC.n)
                        vx("reciprocal", pg8[:, 4 + ga:4 + ga + 2], pg8[:, 4 + ga:4 + ga + 2])
                        vx("tensor_sub", st8[:, 0:2], st8[:, 0:2], pg8[:, ga:ga + 2])
                        vx("tensor_sub", st8[:, 2:4], pg8[:, ga:ga + 2], st8[:, 2:4])
                        vx("tensor_max", st8[:, 0:2], st8[:, 0:2], st8[:, 2:4])
                        vx("tensor_mul", st8[:, 0:2], st8[:, 0:2], pg8[:, 4 + ga:4 + ga + 2])
                        vx("tensor_scalar_max", st8[:, 0:2], st8[:, 0:2], 1e-5)
                        vx("reciprocal", st8[:, 2:4], st8[:, 0:2])
                        vx("tensor_scalar_mul", st8[:, 2:4], st8[:, 2:4], 127.0)
                        vx("tensor_scalar_mul", st8[:, 4:6], st8[:, 0:2], 1.0 / 127.0)
                        for i, g in enumerate((ga, gb)):
                            vx("tensor_mul", st8[:, 8 + g:9 + g], st8[:, 4 + i:5 + i],
                               SCAL[:, 3:4])
                            vx("tensor_mul", st8[:, 12 + g:13 + g], st8[:, 4 + i:5 + i],
                               SCAL[:, 4:5])
                        scg_mark[ga] = scg_mark[gb] = vORD.n
                        vx("tensor_mul", st8[:, 0:2], pg8[:, 4 + ga:4 + ga + 2], st8[:, 2:4])
                        vx("tensor_mul", st8[:, 2:4], pg8[:, ga:ga + 2], st8[:, 0:2])
                        vx("tensor_scalar", st8[:, 2:4], st8[:, 2:4], -1.0, MAGIC,
                           ALU.mult, ALU.add)
                    def mk_quant(i, g):
                        def cq():
                            vx("tensor_scalar", tmpf[:, 0:D], zt[:, g, :],
                               st8[:, 0 + i:1 + i], st8[:, 2 + i:3 + i], ALU.mult, ALU.add)
                            vx("tensor_scalar", tmpf[:, 0:D], tmpf[:, 0:D], -MAGIC, 127.0,
                                                    ALU.add, ALU.min)
                            ins = vx("tensor_scalar", tmpb[:, 0:D], tmpf[:, 0:D],
                                     -128.0, None, ALU.max)
                            vQNT.inc(ins)
                        return cq
                    def mk_tr(g):
                        def ct():
                            nc.tensor.wait_ge(vQNT.h, vQNT.n)
                            if tTRX.n > 0:
                                nc.tensor.wait_ge(vTEV.h, vTEV.n)
                            ins = None
                            for jj in range(8):
                                ins = nc.tensor.transpose(ptb[:, jj * 128:(jj + 1) * 128],
                                                          tmpb[:, jj * 128:(jj + 1) * 128],
                                                          identb[:, :])
                            tTRX.inc(ins)
                            nc.vector.wait_ge(tTRX.h, tTRX.n)
                            nc.vector.wait_ge(tGU.h, tGU.n)
                            ins = vx("tensor_copy",
                                xqT[:, :, g * 128:(g + 1) * 128],
                                ptb[:, :].rearrange("p (j t) -> p j t", j=8))
                            vTEV.inc(ins)
                            xqt_mark[g] = vTEV.n
                        return ct
                    return [mk_c1(0, ga), mk_c1b(0, ga), mk_c1(1, gb), mk_c1b(1, gb),
                            c2, c3, mk_quant(0, ga), mk_tr(ga),
                            mk_quant(1, gb), mk_tr(gb)]

                def ln2_pair(tau_, p):
                    t0_ = 512 * tau_
                    ga, gb = 2 * p, 2 * p + 1
                    for i, g in ((0, ga), (1, gb)):
                        vx("bn_stats", ln6[:, 0:6], xz2[:, g, 0:512])
                        vx("bn_stats", ln6[:, 6:12], xz2[:, g, 512:1024])
                        vx("bn_aggr", lnS[:, 2 * i:2 * i + 2], ln6[:, :])
                    lp = lnS[:, 0:4].rearrange("p (g two) -> p two g", two=2)
                    ins = vx("tensor_scalar_add", lnS[:, 4:6], lp[:, 1, :], EPS)
                    vMSC.inc(ins)
                    nc.scalar.wait_ge(vMSC.h, vMSC.n)
                    ins = nc.scalar.activation(lnS[:, 6:8], lnS[:, 4:6], AF.Sqrt)
                    sMSC.inc(ins)
                    nc.vector.wait_ge(sMSC.h, sMSC.n)
                    vx("reciprocal", lnS[:, 6:8], lnS[:, 6:8])
                    for i, g in ((0, ga), (1, gb)):
                        ins = vx("tensor_scalar", hsb[:, g, 0:D], xz2[:, g, :],
                                 lnS[:, 2 * i:2 * i + 1], lnS[:, 6 + i:7 + i],
                                 ALU.subtract, ALU.mult)
                        vZ2.inc(ins)
                        nc.sync.wait_ge(vZ2.h, vZ2.n)
                        nc.sync.dma_start(out[t0_ + g * 128:t0_ + (g + 1) * 128, :],
                                          hsb[:, g, 0:D]).then_inc(dOUT.h, 16); dOUT.n += 16

                for tau in range(4):
                    t0 = 512 * tau
                    if tau == 0:
                        nc.sync.wait_ge(tPRJ.h, attn_T_prj)
                        nc.sync.wait_ge(tM1.h, attn_T_m1)
                        nc.sync.wait_ge(tM2.h, attn_T_m2)
                        nc.sync.wait_ge(tYTR.h, attn_T_ytr)
                        nc.sync.wait_ge(tVTR.h, attn_T_vtr)
                        nc.sync.wait_ge(vORD.h, attn_V_end)
                        nc.sync.wait_ge(dWQS[0].h, dWQS[0].n)
                        nc.sync.wait_ge(dWQS[1].h, dWQS[1].n)
                        nc.sync.wait_ge(dSPL[0].h, dSPL[0].n)
                        nc.sync.wait_ge(dSPL[1].h, dSPL[1].n)
                        nc.gpsimd.wait_ge(tPRJ.h, attn_T_prj)
                        nc.gpsimd.wait_ge(tM1.h, attn_T_m1)
                        nc.gpsimd.wait_ge(tM2.h, attn_T_m2)
                        nc.gpsimd.wait_ge(tYTR.h, attn_T_ytr)
                        nc.gpsimd.wait_ge(tVTR.h, attn_T_vtr)
                        nc.gpsimd.wait_ge(vORD.h, attn_V_end)
                        nc.gpsimd.wait_ge(dWQS[0].h, dWQS[0].n)
                        nc.gpsimd.wait_ge(dWQS[1].h, dWQS[1].n)
                        nc.gpsimd.wait_ge(dSPL[0].h, dSPL[0].n)
                        nc.gpsimd.wait_ge(dSPL[1].h, dSPL[1].n)
                        issue_loads(0, 0)
                        issue_loads(0, 1)
                        for st_ in mk_ln1_closures(0):
                            st_()
                        pkq = mk_ln1_closures(1)
                    else:
                        issue_xz2(tau, 1)
                        pkq = mk_ln1_closures(1)

                    def gu_slab(dst_par, ibig2):
                        nc.sync.wait_ge(tGU.h, gu_slab_mark[dst_par])
                        nc.sync.wait_ge(dGU[dst_par].h, dGU[dst_par].n)
                        nc.sync.dma_start(wgus[dst_par][:, 0:8, :],
                                          wguq_d[0:8, :, ibig2 * 512:(ibig2 + 1) * 512]
                                          .rearrange("j p i -> p j i")
                                          ).then_inc(dGU[dst_par].h, 16)
                        dGU[dst_par].n += 16
                        nc.gpsimd.wait_ge(tGU.h, gu_slab_mark[dst_par])
                        nc.gpsimd.wait_ge(dGUg[dst_par].h, dGUg[dst_par].n)
                        nc.gpsimd.dma_start(wgus[dst_par][:, 8:16, :],
                                            wguq_d[8:16, :, ibig2 * 512:(ibig2 + 1) * 512]
                                            .rearrange("j p i -> p j i")
                                            ).then_inc(dGUg[dst_par].h, 16)
                        dGUg[dst_par].n += 16

                    def wd_slab(dst_par, sl_):
                        nc.sync.wait_ge(tDWN.h, dwn_slab_mark[dst_par])
                        nc.sync.wait_ge(dWD[dst_par].h, dWD[dst_par].n)
                        nc.sync.dma_start(
                            wds[dst_par][:, 0:2, :],
                            wdq_d[4 * sl_:4 * sl_ + 2, :, :]
                            .rearrange("i p d -> p i d")).then_inc(dWD[dst_par].h, 16)
                        dWD[dst_par].n += 16
                        nc.gpsimd.wait_ge(tDWN.h, dwn_slab_mark[dst_par])
                        nc.gpsimd.wait_ge(dWDg[dst_par].h, dWDg[dst_par].n)
                        nc.gpsimd.dma_start(
                            wds[dst_par][:, 2:4, :],
                            wdq_d[4 * sl_ + 2:4 * sl_ + 4, :, :]
                            .rearrange("i p d -> p i d")).then_inc(dWDg[dst_par].h, 16)
                        dWDg[dst_par].n += 16

                    def mk_hq_steps(g):
                        def s_scale(g=g):
                            vx("tensor_scalar_max", st8[:, 0:1], ah4[:, g:g + 1], 1e-5)
                            vx("reciprocal", st8[:, 1:2], st8[:, 0:1])
                            vx("tensor_scalar_mul", st8[:, 2:3], st8[:, 0:1], 1.0 / 127.0)
                            vx("tensor_mul", pg8[:, 8 + g:9 + g], st8[:, 2:3], SCAL[:, 5:6])
                            vx("tensor_scalar_mul", ah4[:, g:g + 1], st8[:, 1:2], 127.0)
                        yield s_scale
                        for qq in range(4):
                            def s_ts(g=g, qq=qq):
                                s0_, s1_ = qq * 1024, (qq + 1) * 1024
                                vx("tensor_scalar", tmpf[:, 0:1024], hsb[:, g, s0_:s1_],
                                   ah4[:, g:g + 1], MAGIC, ALU.mult, ALU.add)
                                vx("tensor_scalar", tmpf[:, 0:1024], tmpf[:, 0:1024],
                                   -MAGIC, 127.0, ALU.add, ALU.min)
                                ins = vx("tensor_scalar", tmpb[:, s0_:s1_], tmpf[:, 0:1024],
                                         -128.0, None, ALU.max)
                                vQNT.inc(ins)
                            yield s_ts
                        for grp in range(4):
                            def s_tr(g=g, grp=grp):
                                nc.tensor.wait_ge(vQNT.h, vQNT.n)
                                nc.tensor.wait_ge(vTEV.h, vTEV.n)
                                ins = None
                                for jj in range(8):
                                    i2 = grp * 8 + jj
                                    ins = nc.tensor.transpose(ptb[:, jj * 128:(jj + 1) * 128],
                                                              tmpb[:, i2 * 128:(i2 + 1) * 128],
                                                              identb[:, :])
                                tTRX.inc(ins)
                                nc.vector.wait_ge(tTRX.h, tTRX.n)
                                ins = vx("tensor_copy",
                                         hqT[:, grp * 8:(grp + 1) * 8, g * 128:(g + 1) * 128],
                                         ptb[:, :].rearrange("p (j t) -> p j t", j=8))
                                vTEV.inc(ins)
                            yield s_tr

                    def gu_pass(half):
                        nc.vector.wait_ge(dOUT.h, dOUT.n)
                        prev = gu_order_st[0]
                        order = [prev[6], prev[7]] + [s for s in range(8)
                                                      if s not in (prev[6], prev[7])]
                        gu_order_st[0] = order
                        for pos in range(8):
                            par = pos % 2
                            ibig = order[pos]
                            if 1 <= pos and pos + 1 < 8:
                                gu_slab((pos + 1) % 2, order[pos + 1])
                            sl = wgus[par]
                            nc.tensor.wait_ge(dGU[par].h, dGU[par].n)
                            nc.tensor.wait_ge(dGUg[par].h, dGUg[par].n)
                            for g in (2 * half, 2 * half + 1):
                                pt = PAB[g % 2]
                                if pab_sil[g % 2] > 0:
                                    nc.tensor.wait_ge(sSIL.h, pab_sil[g % 2])
                                if pab_hst[g % 2] > 0:
                                    nc.tensor.wait_ge(vHST.h, pab_hst[g % 2])
                                if pos == 0:
                                    nc.tensor.wait_ge(vTEV.h, xqt_mark[g])
                                    if g == 2 * half:
                                        nc.tensor.wait_ge(vZ2.h, z2ev_mark[0])
                                ins = None
                                for j in range(8):
                                    nc.tensor.matmul(pt[:, 0:512],
                                                     xqT[:, j, g * 128:(g + 1) * 128],
                                                     sl[:, j, :],
                                                     start=(j == 0), stop=(j == 7))
                                    ins = nc.tensor.matmul(pt[:, 512:1024],
                                                           xqT[:, j, g * 128:(g + 1) * 128],
                                                           sl[:, 8 + j, :],
                                                           start=(j == 0), stop=(j == 7))
                                tGU.inc(ins)
                                nc.scalar.wait_ge(tGU.h, tGU.n)
                                if pos == 0:
                                    nc.scalar.wait_ge(vORD.h, scg_mark[g])
                                if sSIL.n > 0:
                                    nc.scalar.wait_ge(vHST.h, vHST.n)
                                ins = nc.scalar.activation(gs[:, 0:512], pt[:, 0:512], AF.Silu,
                                                           scale=st8[:, 8 + g:9 + g])
                                pab_sil[g % 2] = sSIL.inc(ins)
                                nc.vector.wait_ge(sSIL.h, sSIL.n)
                                vx("scalar_tensor_tensor",
                                   hsb[:, g, ibig * 512:(ibig + 1) * 512],
                                   pt[:, 512:1024], st8[:, 12 + g:13 + g], gs[:, 0:512],
                                   ALU.mult, ALU.mult)
                                vx("tensor_reduce", st8[:, 6:7],
                                   hsb[:, g, ibig * 512:(ibig + 1) * 512],
                                   AX.X, ALU.max, apply_absolute_value=True)
                                if pos == 0:
                                    ins = vx("tensor_copy", ah4[:, g:g + 1], st8[:, 6:7])
                                else:
                                    ins = vx("tensor_max", ah4[:, g:g + 1], ah4[:, g:g + 1], st8[:, 6:7])
                                pab_hst[g % 2] = vHST.inc(ins)
                                if pkq:
                                    pkq.pop(0)()
                            gu_slab_mark[par] = tGU.n

                    def down_pass(half, hq_mark):
                        z2p = {2 * half: PC, 2 * half + 1: PA}
                        pc_free = z2ev_mark[0]
                        prev = dwn_order_st[0]
                        order = [prev[6], prev[7]] + [s for s in range(8)
                                                      if s not in (prev[6], prev[7])]
                        dwn_order_st[0] = order
                        for pos in range(8):
                            par = pos % 2
                            sl_ = order[pos]
                            if 1 <= pos and pos + 1 < 8:
                                wd_slab((pos + 1) % 2, order[pos + 1])
                            nc.tensor.wait_ge(dWD[par].h, dWD[par].n)
                            nc.tensor.wait_ge(dWDg[par].h, dWDg[par].n)
                            for q in range(4):
                                i = sl_ * 4 + q
                                if pos == 0 and q == 0:
                                    nc.tensor.wait_ge(vTEV.h, hq_mark)
                                    nc.tensor.wait_ge(vZ2.h, pc_free)
                                    if half == 0:
                                        nc.tensor.wait_ge(sSIL.h, sSIL.n)
                                        nc.tensor.wait_ge(vHST.h, vHST.n)
                                ins = None
                                for g in (2 * half, 2 * half + 1):
                                    pt = z2p[g]
                                    for dh in range(2):
                                        ins = nc.tensor.matmul(pt[:, dh * 512:(dh + 1) * 512],
                                                               hqT[:, i, g * 128:(g + 1) * 128],
                                                               wds[par][:, q, dh * 512:(dh + 1) * 512],
                                                               start=(pos == 0 and q == 0),
                                                               stop=(pos == 7 and q == 3))
                                tDWN.inc(ins)
                                if q == 3:
                                    dwn_slab_mark[par] = tDWN.n
                                if q % 2 == 0 and pkq:
                                    pkq.pop(0)()
                        nc.vector.wait_ge(tDWN.h, tDWN.n)
                        ins = None
                        for g in (2 * half, 2 * half + 1):
                            pt = z2p[g]
                            vx("tensor_scalar", tmpf[:, 0:D], zt[:, g, :],
                               pg8[:, g:g + 1], pg8[:, 4 + g:5 + g],
                               ALU.subtract, ALU.mult)
                            ins = vx("scalar_tensor_tensor",
                                     xz2[:, g, :],
                                     pt[:, :], pg8[:, 8 + g:9 + g], tmpf[:, 0:D],
                                     ALU.mult, ALU.add)
                        vZ2.inc(ins)
                        z2ev_mark[0] = vZ2.n

                    if tau == 0:
                        gu_slab(0, 0)
                        gu_slab(1, 1)
                    gu_pass(0)
                    while pkq:
                        pkq.pop(0)()
                    pkq = [s for g_ in (0, 1) for s in mk_hq_steps(g_)]
                    gu_pass(1)
                    while pkq:
                        pkq.pop(0)()
                    hq01 = vTEV.n
                    pkq = [s for g_ in (2, 3) for s in mk_hq_steps(g_)]
                    if tau == 0:
                        wd_slab(0, 0)
                        wd_slab(1, 1)
                    if tau + 1 < 4:
                        issue_park_x(tau + 1)
                    down_pass(0, hq01)
                    while pkq:
                        pkq.pop(0)()
                    hq23 = vTEV.n
                    if tau + 1 < 4:
                        issue_park_y(tau + 1)
                    ln2_pair(tau, 0)
                    if tau + 1 < 4:
                        pkq = mk_ln1_closures(0, parked=True)
                    down_pass(1, hq23)
                    while pkq:
                        pkq.pop(0)()
                    if tau + 1 < 4:
                        issue_zt(tau + 1, 1, mark=z2ev_mark[0])
                    ln2_pair(tau, 1)

        if not (_PH & 8):
            nc.sync.wait_ge(vORD.h, vORD.n)
            nc.sync.dma_start(out[0:128, 0:8], SCAL[:, :]).then_inc(dOUT.h, 16); dOUT.n += 16
        nc.sync.wait_ge(dOUT.h, dOUT.n)
    return nc


_NC_CACHE = None


def kernel(L0_attn_x, L0_attn_Wq, L0_attn_Wk, L0_attn_Wv,
           L0_ffn_w_gate, L0_ffn_w_up, L0_ffn_w_down):
    global _NC_CACHE
    x = np.asarray(L0_attn_x, dtype=np.float32).reshape(4 * 4096, 1024)
    wqT = np.ascontiguousarray(np.asarray(L0_attn_Wq, np.float32).T)
    wkT = np.ascontiguousarray(np.asarray(L0_attn_Wk, np.float32).T)
    wvT = np.ascontiguousarray(np.asarray(L0_attn_Wv, np.float32).T)
    wgT = np.ascontiguousarray(np.asarray(L0_ffn_w_gate, np.float32).T)
    wuT = np.ascontiguousarray(np.asarray(L0_ffn_w_up, np.float32).T)
    wdT = np.ascontiguousarray(np.asarray(L0_ffn_w_down, np.float32).T)

    idb = np.eye(128, dtype=np.float32).astype(BD)
    idf = np.eye(128, dtype=np.float32)
    onesr = np.ones((1, 128), np.float32)
    onesc = np.ones((128, 1), np.float32)

    if _NC_CACHE is None:
        _NC_CACHE = build_nc()
    nc = _NC_CACHE

    in_maps = []
    for c in range(NCORE):
        rows = x[c * T:(c + 1) * T]
        in_maps.append({
            "xT": np.ascontiguousarray(rows.T),
            "xN": np.ascontiguousarray(rows),
            "wqT": wqT, "wkT": wkT, "wvT": wvT,
            "wgT": wgT, "wuT": wuT, "wdT": wdT,
            "idb": idb, "idf": idf,
            "onesr": onesr, "onesc": onesc,
        })
    res = run_bass_kernel_spmd(nc, in_maps, core_ids=list(range(NCORE)))
    full = np.concatenate([res.results[c]["out"] for c in range(NCORE)], axis=0)
    return full.reshape(4, 4096, 1024)
